# version 01
# baseline (speedup 1.0000x reference)
import numpy as np
import ml_dtypes
import concourse.bass as bass
import concourse.mybir as mybir
from concourse.bass_utils import run_bass_kernel_spmd

F32 = mybir.dt.float32
BF16 = mybir.dt.bfloat16
U8 = mybir.dt.uint8
AF = mybir.ActivationFunctionType
ALU = mybir.AluOpType

NCORES = 8
B_PER_CORE = 2
T = 2048
D = 1024
NCH = D // 128
TOK = B_PER_CORE * T
BLK = 512
EPS = 1e-5

ENGINES = ("T", "V", "S", "SQ", "GQ")
SERIALIZE_TAGS = ("am_", "sp_", "cal_")
_ENG_BLOCK = {"T": "tensor", "V": "vector", "S": "scalar", "SQ": "sync", "GQ": "gpsimd"}
_DMA_ENGS = ("SQ", "GQ")


def _prod(xs):
    p = 1
    for x in xs:
        p *= x
    return p


class Buf:
    __slots__ = ("name", "writer", "readers", "ptrw")

    def __init__(self, name=""):
        self.name = name
        self.writer = None
        self.readers = []
        self.ptrw = False


class Plan:
    R_LANES = 12

    def __init__(self):
        self.ops = {e: [] for e in ENGINES}
        self.nop_emit = {}

    def op(self, eng, emit, reads=(), writes=(), tag="", sreads=(), ptrw=False):
        waits = {}

        def need(dep, allow_self=False):
            if dep is None:
                return
            e2, i2 = dep
            if e2 == eng and not allow_self:
                return
            key = (e2, i2 % self.R_LANES) if e2 in _DMA_ENGS else e2
            if waits.get(key, -1) < i2:
                waits[key] = i2

        for b in sreads:
            need(b.writer, allow_self=True)
        reads = list(reads) + list(sreads)
        for b in reads:
            need(b.writer, allow_self=b.ptrw)
        for b in writes:
            need(b.writer)
            for r in b.readers:
                need(r)
        idx = len(self.ops[eng])
        if SERIALIZE_TAGS and eng == "V" and idx > 0 and any(tag.startswith(p) for p in SERIALIZE_TAGS):
            if waits.get(eng, -1) < idx - 1:
                waits[eng] = idx - 1
        if waits.get(eng, -1) >= idx:
            del waits[eng]
        return self._append(eng, emit, waits, tag, reads, writes, ptrw)

    def _op(self, eng, emit, waits, tag=""):
        self.ops[eng].append({"emit": emit, "waits": waits, "tag": tag})
        return (eng, len(self.ops[eng]) - 1)

    def _append(self, eng, emit, waits, tag, reads, writes, ptrw):
        idx = len(self.ops[eng])
        self.ops[eng].append({"emit": emit, "waits": waits, "tag": tag})
        me = (eng, idx)
        for b in reads:
            b.readers.append(me)
        for b in writes:
            b.writer = me
            b.readers = []
            b.ptrw = ptrw
        return me

    def last_ops(self):
        out = []
        for e in ENGINES:
            n = len(self.ops[e])
            if not n:
                continue
            if e in _DMA_ENGS:
                for i in range(max(0, n - self.R_LANES), n):
                    out.append((e, i))
            else:
                out.append((e, n - 1))
        return out

    def finalize(self, nc):
        R = self.R_LANES
        targets = {e: set() for e in ENGINES}

        def tgt_of(key, i2):
            e2 = key if isinstance(key, str) else key[0]
            return e2, i2

        for e in ENGINES:
            for o in self.ops[e]:
                for key, i2 in o["waits"].items():
                    e2, i2 = tgt_of(key, i2)
                    targets[e2].add(i2)
        for e, i in self.last_ops():
            targets[e].add(i)
        for e in _DMA_ENGS:
            targets[e].update(range(len(self.ops[e])))
        inc_points = {e: sorted(targets[e]) for e in ENGINES}
        val_of = {}
        for e in ENGINES:
            vals = {}
            if e in _DMA_ENGS:
                for i in range(len(self.ops[e])):
                    vals[i] = (i // R + 1) * 16
            else:
                rank = 0
                pts = set(inc_points[e])
                for i in range(len(self.ops[e])):
                    if i in pts:
                        rank += 1
                    vals[i] = rank
            val_of[e] = vals

        sems = {}
        for e in ENGINES:
            if e in _DMA_ENGS:
                sems[e] = [nc.alloc_semaphore(f"sem_{e}_{j}") for j in range(R)]
            else:
                sems[e] = nc.alloc_semaphore(f"sem_{e}")

        def sem_and_val(e2, i2):
            if e2 in _DMA_ENGS:
                return sems[e2][i2 % R], val_of[e2][i2]
            return sems[e2], val_of[e2][i2]

        plan = self

        def body_for(e):
            def body(engine):
                waited = {}
                pts = set(inc_points[e])
                for i, o in enumerate(plan.ops[e]):
                    for key, i2 in sorted(o["waits"].items(), key=str):
                        e2 = key if isinstance(key, str) else key[0]
                        sem, v = sem_and_val(e2, i2)
                        wkey = (e2, i2 % R) if e2 in _DMA_ENGS else e2
                        if v > waited.get(wkey, 0):
                            engine.wait_ge(sem, v)
                            waited[wkey] = v
                    inst = o["emit"](engine)
                    if i in pts:
                        if inst is None:
                            raise RuntimeError(f"op {e}:{i} tag={o['tag']} returned None")
                        if e in _DMA_ENGS:
                            inst.then_inc(sems[e][i % R], 16)
                        else:
                            inst.then_inc(sems[e], 1)
                if e == "SQ":
                    for e2, i2 in plan.last_ops():
                        if e2 == "SQ" and i2 == len(plan.ops["SQ"]) - 1:
                            continue
                        sem, v = sem_and_val(e2, i2)
                        wkey = (e2, i2 % R) if e2 in _DMA_ENGS else e2
                        if v > waited.get(wkey, 0):
                            engine.wait_ge(sem, v)
                            waited[wkey] = v
            return body

        with nc.Block() as block:
            for e in ENGINES:
                if not self.ops[e] and e != "SQ":
                    continue
                getattr(block, _ENG_BLOCK[e])(body_for(e))


class Arena:
    def __init__(self, nc, nbytes=200 * 1024):
        self.pool = nc.alloc_sbuf_tensor("pool", [128, nbytes], U8)
        self.cap = nbytes
        self.off = 0
        self.stack = []
        self.hi = 0

    def alloc(self, shape, dt, name=""):
        isz = {F32: 4, BF16: 2, U8: 1}[dt]
        n = _prod(shape[1:]) * isz
        off = (self.off + 31) & ~31
        assert off + n <= self.cap, f"SBUF arena overflow: {name} needs {n}B at {off}"
        self.off = off + n
        self.hi = max(self.hi, self.off)
        ap = self.pool.ap()[: shape[0], off : off + n].bitcast(dt)
        if len(shape) > 2:
            dims = " ".join(f"d{i}" for i in range(1, len(shape)))
            kw = {f"d{i}": shape[i] for i in range(1, len(shape) - 1)}
            ap = ap.rearrange(f"p ({dims}) -> p {dims}", **kw)
        return ap

    def mark(self):
        self.stack.append(self.off)

    def release(self):
        self.off = self.stack.pop()


class K:
    def __init__(self, nc, dumps=()):
        self.nc = nc
        self.p = Plan()
        self.ar = Arena(nc)
        self.dumps = set(dumps)
        self.dump_names = []
        self.inp = {}
        self.hazard = []

    def buf(self, name=""):
        b = Buf(name)
        b.readers = list(self.hazard)
        return b

    def release_arena(self):
        self.ar.release()
        self.hazard = self.p.last_ops()

    def barrier(self):
        p = self.p
        join_waits = {}
        for e, i in p.last_ops():
            key = (e, i % p.R_LANES) if e in _DMA_ENGS else e
            join_waits[key] = max(join_waits.get(key, -1), i)
        ji = p._op("SQ", lambda s: s.nop(), join_waits, tag="bar_join")
        for e in ENGINES:
            if e == "SQ" or not p.ops[e]:
                continue
            p._op(e, lambda eng: eng.nop(), {"SQ": ji[1]}, tag="bar_leaf")
        self.hazard = p.last_ops()

    def din(self, name, shape, dt=F32):
        t = self.nc.dram_tensor(name, list(shape), dt, kind="ExternalInput")
        self.inp[name] = t
        return t

    def dscr(self, name, shape, dt=F32):
        kind = "ExternalOutput" if name in self.dumps else "Internal"
        if kind == "ExternalOutput":
            self.dump_names.append(name)
        return self.nc.dram_tensor(name, list(shape), dt, kind=kind)

    def dump(self, name, ap, reads, shape, dt=F32):
        if name not in self.dumps:
            return
        t = self.nc.dram_tensor("dump_" + name, list(shape), dt, kind="ExternalOutput")
        self.dump_names.append("dump_" + name)
        self.p.op("SQ", lambda s, t=t, ap=ap: s.dma_start(t.ap(), ap), reads=list(reads), tag="dump_" + name)

    def T(self, emit, reads=(), writes=(), tag=""):
        return self.p.op("T", emit, reads, writes, tag)

    def V(self, emit, reads=(), writes=(), tag="", sreads=(), ptrw=False):
        return self.p.op("V", emit, reads, writes, tag, sreads, ptrw)

    def S(self, emit, reads=(), writes=(), tag="", sreads=(), ptrw=False):
        return self.p.op("S", emit, reads, writes, tag, sreads, ptrw)

    def SQ(self, emit, reads=(), writes=(), tag=""):
        return self.p.op("SQ", emit, reads, writes, tag)

    def GQ(self, emit, reads=(), writes=(), tag=""):
        return self.p.op("GQ", emit, reads, writes, tag)



def build(nc, dumps=(), upto=99):
    k = K(nc, dumps)
    ar = k.ar
    nch = NCH

    xfm = k.din("xfm", (B_PER_CORE, D, T))
    xfm_b = Buf("xfm")
    ones_in = k.din("ones_bf", (128, 128), BF16)
    ident_in = k.din("ident_bf", (128, 128), BF16)

    wdq = k.din("wdq", (128, D))
    wuqT = k.din("wuqT", (128, D))
    wuq = k.din("wuq", (D, 128))
    wdkvT = k.din("wdkvT", (D, 128))
    wdqT = k.din("wdqT", (D, 128))
    wuk = k.din("wuk", (D, 128))
    wuv = k.din("wuv", (D, 128))
    woT = k.din("woT", (D, D))

    yout = nc.dram_tensor("yout", [B_PER_CORE, D, T], F32, kind="ExternalOutput")
    yout_b = Buf("yout")

    onesb = ar.alloc([128, 128], BF16, "onesb")
    identb = ar.alloc([128, 128], BF16, "identb")
    epsc = ar.alloc([128, 1], F32, "epsc")
    epsc_b = Buf("epsc")
    junk = ar.alloc([128, 8], F32, "junk")
    ar.mark()
    ckv = ar.alloc([128, TOK], BF16, "ckv")
    q1 = ar.alloc([128, TOK], BF16, "q1")
    ckv_tm = ar.alloc([128, 32, 128], BF16, "ckv_tm")
    bq = ar.alloc([128, 16, 128], BF16, "bq")
    v2 = ar.alloc([128, D], BF16, "v2")
    ohs_fm = ar.alloc([64, TOK], BF16, "ohs_fm")
    ohs_fm_b = Buf("ohs_fm")
    ar.mark()
    xn0 = ar.alloc([128, nch, TOK], BF16, "xn0")
    xn0_b = Buf("xn0")
    onesb_b = Buf("onesb")
    identb_b = Buf("identb")
    ckv_b = Buf("ckv")
    q1_b = Buf("q1")
    ckv_tm_b = Buf("ckv_tm")
    bq_b = Buf("bq")
    v2_b = Buf("v2")

    ps = [nc.alloc_psum_tensor(f"ps{i}", [128, 1024], F32) for i in range(4)]
    psx = [p.ap().bitcast(BF16) for p in ps]
    psb = [Buf(f"ps{i}") for i in range(4)]

    k.GQ(lambda g: g.dma_start(onesb[:], ones_in.ap()), writes=[onesb_b], tag="ld_ones")
    k.GQ(lambda g: g.dma_start(identb[:], ident_in.ap()), writes=[identb_b], tag="ld_ident")

    k.V(lambda v: v.memset(epsc[:], EPS), writes=[epsc_b], tag="epsc")
    k.p.nop_emit["V"] = lambda v: v.memset(junk[:], 0.0)
    k.p.nop_emit["S"] = lambda s: s.memzero(junk[:])

    def wload(name, t, shape, src_ap=None, tag=""):
        sb = ar.alloc(shape, BF16, name)
        b = k.buf(name)
        src = src_ap if src_ap is not None else t.ap()
        k.GQ(lambda g, sb=sb, src=src: g.dma_start(sb[:], src), writes=[b], tag=tag or ("ld_" + name))
        return sb, b

    def chunk_rows(t, m):
        return t.ap().rearrange("(c p) m -> p c m", p=128)

    wdkvT_sb, wdkvT_sbb = wload("wdkvT_sb", wdkvT, [128, nch, 128], chunk_rows(wdkvT, 128))
    wdqT_sb, wdqT_sbb = wload("wdqT_sb", wdqT, [128, nch, 128], chunk_rows(wdqT, 128))

    ar.mark()
    wdq_sb, wdq_sbb = wload("wdq_sb", wdq, [128, D])
    wuqT_sb, wuqT_sbb = wload("wuqT_sb", wuqT, [128, D])
    wuq_sb, wuq_sbb = wload("wuq_sb", wuq, [128, nch, 128], chunk_rows(wuq, 128))
    wuk_sb, wuk_sbb = wload("wuk_sb", wuk, [128, nch, 128], chunk_rows(wuk, 128))
    wuv_sb, wuv_sbb = wload("wuv_sb", wuv, [128, nch, 128], chunk_rows(wuv, 128))
    woT_sb, woT_sbb = wload("woT_sb", woT, [128, nch, D], chunk_rows(woT, D))

    d1t = ar.alloc([128, nch, D], BF16, "d1t")
    d1t_b = k.buf("d1t")
    for jc in range(nch):
        for n2 in range(2):
            k.T(lambda t, jc=jc, n2=n2: t.matmul(
                    ps[0][:, n2 * 512 : n2 * 512 + 512],
                    wuqT_sb[:, jc * 128 : jc * 128 + 128],
                    wdq_sb[:, n2 * 512 : n2 * 512 + 512],
                    start=True, stop=True),
                reads=[wuqT_sbb, wdq_sbb], writes=[psb[0]], tag="d1t_mm")
        k.V(lambda v, jc=jc: v.tensor_copy(d1t[:, jc, :], ps[0][:, 0:1024]),
            reads=[psb[0]], writes=[d1t_b], tag="d1t_ev")

    a_sb = ar.alloc([128, nch, 128], BF16, "a_sb")
    a_b = k.buf("a_sb")
    for dc in range(nch):
        for jc in range(nch):
            k.T(lambda t, dc=dc, jc=jc: t.matmul(
                    ps[1][:, 0:128],
                    d1t[:, jc, dc * 128 : dc * 128 + 128],
                    wuk_sb[:, jc, :],
                    start=(jc == 0), stop=(jc == nch - 1)),
                reads=[d1t_b, wuk_sbb], writes=[psb[1]], tag="a_mm")
        k.V(lambda v, dc=dc: v.tensor_copy(a_sb[:, dc, :], ps[1][:, 0:128]),
            reads=[psb[1]], writes=[a_b], tag="a_ev")

    for h in range(16):
        po = (h % 2) * 64
        c = h // 2
        k.T(lambda t, h=h, po=po, c=c: t.matmul(
                ps[2][:, 0:128],
                wuq_sb[po : po + 64, c, :],
                a_sb[po : po + 64, c, :],
                start=True, stop=True),
            reads=[wuq_sbb, a_b], writes=[psb[2]], tag="bq_mm")
        k.V(lambda v, h=h: v.tensor_copy(bq[:, h, :], ps[2][:, 0:128]),
            reads=[psb[2]], writes=[bq_b], tag="bq_ev")

    for n2 in range(2):
        for uc in range(nch):
            k.T(lambda t, n2=n2, uc=uc: t.matmul(
                    ps[3][:, 0:512],
                    wuv_sb[:, uc, :],
                    woT_sb[:, uc, n2 * 512 : n2 * 512 + 512],
                    start=(uc == 0), stop=(uc == nch - 1)),
                reads=[wuv_sbb, woT_sbb], writes=[psb[3]], tag="v2_mm")
        k.V(lambda v, n2=n2: v.tensor_copy(v2[:, n2 * 512 : n2 * 512 + 512], ps[3][:, 0:512]),
            reads=[psb[3]], writes=[v2_b], tag="v2_ev")

    k.dump("d1t", d1t[:], [d1t_b], (128, nch, D), BF16)
    k.dump("a", a_sb[:], [a_b], (128, nch, 128), BF16)
    k.dump("bq", bq[:], [bq_b], (128, 16, 128), BF16)
    k.dump("v2", v2[:], [v2_b], (128, D), BF16)
    k.release_arena()
    k.barrier()

    mur2_d = k.dscr("mur2_d", (2, TOK // BLK, 128, BLK), F32)
    mur2_d_b = Buf("mur2_d")
    nblk = TOK // BLK
    ar.mark()
    xt = [ar.alloc([128, nch, BLK], F32, f"xt{i}") for i in range(2)]
    xbt = [ar.alloc([128, nch, BLK], BF16, f"xbt{i}") for i in range(2)]
    sqt = ar.alloc([128, nch, BLK], BF16, "sqt")
    tmp512 = ar.alloc([128, BLK], F32, "tmp512")
    mut = [ar.alloc([128, BLK], F32, f"mut{i}") for i in range(2)]
    rrt = [ar.alloc([128, BLK], F32, f"rrt{i}") for i in range(2)]
    xt_b = [k.buf(f"xt{i}") for i in range(2)]
    xbt_b = [k.buf(f"xbt{i}") for i in range(2)]
    sqt_b = k.buf("sqt")
    tmp512_b = k.buf("tmp512")
    mut_b = [k.buf(f"mut{i}") for i in range(2)]
    rrt_b = [k.buf(f"rrt{i}") for i in range(2)]

    for blk in range(nblk):
        b, t0 = divmod(blk * BLK, T)
        g = blk * BLK
        i = blk % 2
        src = xfm.ap()[b].rearrange("(c p) t -> p c t", p=128)[:, :, t0 : t0 + BLK]
        k.SQ(lambda s, src=src, i=i: s.dma_start(xt[i][:], src),
             reads=[xfm_b], writes=[xt_b[i]], tag="p1_ldx")
        k.GQ(lambda gq, src=src, i=i: gq.dma_start(xbt[i][:], src),
             reads=[xfm_b], writes=[xbt_b[i]], tag="p1_ldxb")
        k.V(lambda v, i=i: v.tensor_mul(sqt[:].rearrange("p c t -> p (c t)"),
                                        xbt[i][:].rearrange("p c t -> p (c t)"),
                                        xbt[i][:].rearrange("p c t -> p (c t)")),
            reads=[xbt_b[i]], writes=[sqt_b], tag="p1_sq")
        for c in range(nch):
            k.T(lambda t, c=c, i=i: t.matmul(ps[0][:, 0:BLK], onesb[:], xbt[i][:, c, :],
                                             start=(c == 0), stop=(c == nch - 1)),
                reads=[xbt_b[i], onesb_b], writes=[psb[0]], tag="p1_mmS")
        for c in range(nch):
            k.T(lambda t, c=c: t.matmul(ps[1][:, 0:BLK], onesb[:], sqt[:, c, :],
                                        start=(c == 0), stop=(c == nch - 1)),
                reads=[sqt_b, onesb_b], writes=[psb[1]], tag="p1_mmS2")
        mu_ap = mut[i][:]
        r_ap = rrt[i][:]
        k.V(lambda v, mu_ap=mu_ap: v.tensor_scalar_mul(mu_ap, ps[0][:, 0:BLK], 1.0 / D),
            reads=[psb[0]], writes=[mut_b[i]], tag="p1_mu")
        k.V(lambda v, mu_ap=mu_ap: v.tensor_mul(tmp512[:], mu_ap, mu_ap),
            reads=[mut_b[i]], writes=[tmp512_b], tag="p1_musq")
        k.V(lambda v, r_ap=r_ap: v.scalar_tensor_tensor(r_ap, ps[1][:, 0:BLK], 1.0 / D,
                                                        tmp512[:], ALU.mult, ALU.subtract),
            reads=[psb[1], tmp512_b], writes=[rrt_b[i]], tag="p1_var")
        k.S(lambda s, r_ap=r_ap: s.activation(r_ap, r_ap, AF.Ln, bias=epsc[:]),
            reads=[rrt_b[i], epsc_b], writes=[rrt_b[i]], tag="p1_ln")
        k.S(lambda s, r_ap=r_ap: s.activation(r_ap, r_ap, AF.Exp, scale=-0.5),
            reads=[rrt_b[i]], writes=[rrt_b[i]], tag="p1_exp")
        k.SQ(lambda s, blk=blk, i=i: s.dma_start(mur2_d.ap()[0, blk], mut[i][:, :]),
             reads=[mut_b[i]], writes=[mur2_d_b], tag="p1_stmu")
        k.SQ(lambda s, blk=blk, i=i: s.dma_start(mur2_d.ap()[1, blk], rrt[i][:, :]),
             reads=[rrt_b[i]], writes=[mur2_d_b], tag="p1_str")
        for c in range(nch):
            k.V(lambda v, c=c, i=i, mu_ap=mu_ap: v.tensor_sub(xt[i][:, c, :], xt[i][:, c, :], mu_ap),
                reads=[mut_b[i]], writes=[xt_b[i]], tag="p1_sub")
            k.V(lambda v, c=c, i=i, g=g, r_ap=r_ap: v.tensor_mul(
                    xn0[:, c, g : g + BLK], xt[i][:, c, :], r_ap),
                reads=[xt_b[i], rrt_b[i]], writes=[xn0_b], tag="p1_xn")

    k.release_arena()
    k.barrier()

    k.dump("xn0", xn0[:], [xn0_b], (128, nch, TOK), BF16)

    if upto <= 1:
        k.p.finalize(nc)
        return k

    for nt in range(TOK // 512):
        for c in range(nch):
            k.T(lambda t, nt=nt, c=c: t.matmul(
                    ps[0][:, 0:512], wdkvT_sb[:, c, :], xn0[:, c, nt * 512 : nt * 512 + 512],
                    start=(c == 0), stop=(c == nch - 1)),
                reads=[wdkvT_sbb, xn0_b], writes=[psb[0]], tag="ckv_mm")
        k.V(lambda v, nt=nt: v.tensor_copy(ckv[:, nt * 512 : nt * 512 + 512], ps[0][:, 0:512]),
            reads=[psb[0]], writes=[ckv_b], tag="ckv_ev")
        for c in range(nch):
            k.T(lambda t, nt=nt, c=c: t.matmul(
                    ps[1][:, 0:512], wdqT_sb[:, c, :], xn0[:, c, nt * 512 : nt * 512 + 512],
                    start=(c == 0), stop=(c == nch - 1)),
                reads=[wdqT_sbb, xn0_b], writes=[psb[1]], tag="q1_mm")
        k.V(lambda v, nt=nt: v.tensor_copy(q1[:, nt * 512 : nt * 512 + 512], ps[1][:, 0:512]),
            reads=[psb[1]], writes=[q1_b], tag="q1_ev")

    for q4 in range(8):
        for j in range(4):
            st = q4 * 4 + j
            k.T(lambda t, st=st, j=j: t.transpose(
                    psx[2][:, j * 128 : j * 128 + 128],
                    ckv[:, st * 128 : st * 128 + 128], identb[:]),
                reads=[ckv_b, identb_b], writes=[psb[2]], tag="ckvtm_tr")
        k.V(lambda v, q4=q4: v.tensor_copy(
                ckv_tm[:].rearrange("p a b -> p (a b)")[:, q4 * 512 : q4 * 512 + 512],
                psx[2][:, 0:512]),
            reads=[psb[2]], writes=[ckv_tm_b], tag="ckvtm_ev")

    k.dump("ckv", ckv[:], [ckv_b], (128, TOK), BF16)
    k.dump("q1", q1[:], [q1_b], (128, TOK), BF16)
    k.dump("ckv_tm", ckv_tm[:], [ckv_tm_b], (128, 32, 128), BF16)

    if upto <= 2:
        k.p.finalize(nc)
        return k

    sigsrawT = k.din("sigsrawT", (D, 64))
    knots_in = k.din("knots", (8,))
    temp_in = k.din("temp", (1,))
    cw1_in = k.din("cw1", (D, 256))
    cb1_in = k.din("cb1", (128, 2))
    cw2_in = k.din("cw2", (256, 2))
    cb2_in = k.din("cb2", (2,))
    tab65_in = k.din("tab65", (128, 65))
    osc_in = k.din("oscale", (1,))
    iota8_in = k.din("iota8", (128, 8))
    i8m_in = k.din("iota8m", (128, 8))
    iota64_in = k.din("iota64", (128, 64))
    iotad8_in = k.din("iotad8", (128, 64))
    i64m_in = k.din("iota64m", (128, 64))

    ar.mark()
    srw = ar.alloc([128, nch, 64], F32, "srw")
    m72 = ar.alloc([128, nch, 72], BF16, "m72")
    qtmp = ar.alloc([128, nch, 64], F32, "qtmp")
    a1 = ar.alloc([128, nch, 32], F32, "a1")
    a2 = ar.alloc([128, nch, 16], F32, "a2")
    a3 = ar.alloc([128, nch, 8], F32, "a3")
    scs = ar.alloc([128, 32, 72], F32, "scs")
    gacc = ar.alloc([128, 32 * 72], F32, "gacc")
    camp = ar.alloc([128, 32 * 72], F32, "camp")
    gath = gacc[:, 0 : 32 * 65].rearrange("p (a b) -> p a b", a=32)
    h1sb = ar.alloc([128, 2, TOK], BF16, "h1sb")
    oh01_tm = ar.alloc([128, 32, 64], BF16, "oh01_tm")
    oh01_fm = ar.alloc([64, TOK], BF16, "oh01_fm")
    ohm_tm = ar.alloc([128, 64], BF16, "ohm_tm")
    cw1_sb = ar.alloc([128, nch, 256], BF16, "cw1_sb")
    cb1_sb = ar.alloc([128, 2], F32, "cb1_sb")
    cw2_sb = ar.alloc([128, 2, 2], BF16, "cw2_sb")
    cb2_sb = ar.alloc([128, 2], F32, "cb2_sb")
    tab65_sb = ar.alloc([128, 65], BF16, "tab65_sb")
    kb = ar.alloc([128, 8], F32, "kb")
    dk = ar.alloc([128, 7], F32, "dk")
    rt = ar.alloc([128, 1], F32, "rt")
    oscb = ar.alloc([128, 1], F32, "oscb")
    iota8c = ar.alloc([128, 8], F32, "iota8c")
    i8mc = ar.alloc([128, 8], F32, "i8mc")
    iota64c = ar.alloc([128, 64], F32, "iota64c")
    iotad8c = ar.alloc([128, 64], F32, "iotad8c")
    i64mc = ar.alloc([128, 64], F32, "i64mc")
    u_sb = ar.alloc([128, 512], F32, "u_sb")
    e_sb = ar.alloc([128, 512], F32, "e_sb")
    compsb = ar.alloc([128, 64], F32, "compsb")
    sm1 = ar.alloc([128, 64], F32, "sm1")
    sm2 = ar.alloc([128, 64], F32, "sm2")
    sm3 = ar.alloc([128, 16], F32, "sm3")
    clv = ar.alloc([128, 1], F32, "clv")
    mgv = ar.alloc([128, 1], F32, "mgv")

    srw_b = k.buf("srw"); m72_b = k.buf("m72"); qtmp_b = k.buf("qtmp")
    a1_b = k.buf("a1"); a2_b = k.buf("a2"); a3_b = k.buf("a3")
    scs_b = k.buf("scs"); gacc_b = k.buf("gacc"); camp_b = k.buf("camp")
    gath_b = gacc_b; cacc_b = gacc_b
    h1_b = k.buf("h1sb")
    oh01_tm_b = k.buf("oh01_tm"); oh01_fm_b = k.buf("oh01_fm"); ohm_tm_b = k.buf("ohm_tm")
    cw1_b = k.buf("cw1_sb"); cb1_b = k.buf("cb1_sb"); cw2_b = k.buf("cw2_sb")
    cb2_b = k.buf("cb2_sb"); tab65_b = k.buf("tab65_sb")
    kb_b = k.buf("kb"); dk_b = k.buf("dk"); rt_b = k.buf("rt"); oscb_b = k.buf("oscb")
    iota8c_b = k.buf("iota8c"); i8mc_b = k.buf("i8mc"); iota64c_b = k.buf("iota64c")
    iotad8c_b = k.buf("iotad8c"); i64mc_b = k.buf("i64mc")
    u_b = k.buf("u_sb"); e_b = k.buf("e_sb"); comp_b = k.buf("compsb")
    sm1_b = k.buf("sm1"); sm2_b = k.buf("sm2"); sm3_b = k.buf("sm3")
    clv_b = k.buf("clv"); mgv_b = k.buf("mgv")

    k.SQ(lambda s: s.dma_start(srw[:], sigsrawT.ap().rearrange("(c p) m -> p c m", p=128)),
         writes=[srw_b], tag="ld_srw")
    k.SQ(lambda s: s.dma_start(kb[:], knots_in.ap().unsqueeze(0).partition_broadcast(128)),
         writes=[kb_b], tag="ld_knots")
    k.SQ(lambda s: s.dma_start(rt[:], temp_in.ap().unsqueeze(0).partition_broadcast(128)),
         writes=[rt_b], tag="ld_temp")
    k.SQ(lambda s: s.dma_start(oscb[:], osc_in.ap().unsqueeze(0).partition_broadcast(128)),
         writes=[oscb_b], tag="ld_osc")
    k.GQ(lambda g: g.dma_start(cw1_sb[:], cw1_in.ap().rearrange("(c p) m -> p c m", p=128)),
         writes=[cw1_b], tag="ld_cw1")
    k.SQ(lambda s: s.dma_start(cb1_sb[:], cb1_in.ap()), writes=[cb1_b], tag="ld_cb1")
    k.GQ(lambda g: g.dma_start(cw2_sb[:], cw2_in.ap().rearrange("(c p) m -> p c m", p=128)),
         writes=[cw2_b], tag="ld_cw2")
    k.SQ(lambda s: s.dma_start(cb2_sb[:], cb2_in.ap().unsqueeze(0).partition_broadcast(128)),
         writes=[cb2_b], tag="ld_cb2")
    k.GQ(lambda g: g.dma_start(tab65_sb[:], tab65_in.ap()), writes=[tab65_b], tag="ld_tab")
    for nm, sb, bb, t_in in (("i8", iota8c, iota8c_b, iota8_in), ("i8m", i8mc, i8mc_b, i8m_in),
                             ("i64", iota64c, iota64c_b, iota64_in),
                             ("id8", iotad8c, iotad8c_b, iotad8_in), ("i64m", i64mc, i64mc_b, i64m_in)):
        k.SQ(lambda s, sb=sb, t_in=t_in: s.dma_start(sb[:], t_in.ap()), writes=[bb], tag="ld_" + nm)

    k.dump("dbg_i8m", i8mc[:], [i8mc_b], (128, 8), F32)
    k.dump("dbg_i64", iota64c[:], [iota64c_b], (128, 64), F32)
    k.dump("dbg_id8", iotad8c[:], [iotad8c_b], (128, 64), F32)
    fl = lambda ap: ap.rearrange("p c m -> p (c m)")
    k.V(lambda v: v.tensor_single_scalar(qtmp[:], srw[:], 0.3, ALU.is_gt),
        reads=[srw_b], writes=[qtmp_b], tag="q_gt")
    k.V(lambda v: v.tensor_single_scalar(srw[:], srw[:], -0.3, ALU.is_lt),
        reads=[srw_b], writes=[srw_b], tag="q_lt")
    k.V(lambda v: v.tensor_sub(m72[:, :, 8:72], qtmp[:], srw[:]),
        reads=[qtmp_b, srw_b], writes=[m72_b], tag="q_sub")
    k.V(lambda v: v.tensor_add(a1[:], m72[:, :, 8:72:2], m72[:, :, 9:72:2]),
        reads=[m72_b], writes=[a1_b], tag="cs_t1")
    k.V(lambda v: v.tensor_add(a2[:], a1[:, :, 0:32:2], a1[:, :, 1:32:2]),
        reads=[a1_b], writes=[a2_b], tag="cs_t2")
    k.V(lambda v: v.tensor_add(a3[:], a2[:, :, 0:16:2], a2[:, :, 1:16:2]),
        reads=[a2_b], writes=[a3_b], tag="cs_t3")
    k.V(lambda v: v.tensor_single_scalar(a1[:, :, 0:8], a3[:], 0.0, ALU.is_gt),
        reads=[a3_b], writes=[a1_b], tag="cs_gt")
    k.V(lambda v: v.tensor_single_scalar(a2[:, :, 0:8], a3[:], 0.0, ALU.is_lt),
        reads=[a3_b], writes=[a2_b], tag="cs_lt")
    k.V(lambda v: v.tensor_sub(m72[:, :, 0:8], a1[:, :, 0:8], a2[:, :, 0:8]),
        reads=[a1_b, a2_b], writes=[m72_b], tag="cs_sub")

    k.dump("m72", m72[:], [m72_b], (128, nch, 72), BF16)
    for tt in range(32):
        pt = tt % 4
        for c in range(nch):
            k.T(lambda t, tt=tt, c=c, pt=pt: t.matmul(
                    ps[pt][:, 0:72], xn0[:, c, tt * 128 : tt * 128 + 128], m72[:, c, :],
                    start=(c == 0), stop=(c == nch - 1)),
                reads=[xn0_b, m72_b], writes=[psb[pt]], tag="fs_mm")
        k.V(lambda v, tt=tt, pt=pt: v.tensor_copy(scs[:, tt, :], ps[pt][:, 0:72]),
            reads=[psb[pt]], writes=[scs_b], tag="fs_ev")

    k.dump("scs_raw", scs[:], [scs_b], (128, 32, 72), F32)
    k.V(lambda v: v.tensor_single_scalar(rt[:], rt[:], 1e-6, ALU.add),
        reads=[rt_b], writes=[rt_b], tag="cal_t1")
    k.V(lambda v: v.reciprocal(rt[:], rt[:]), reads=[rt_b], writes=[rt_b], tag="cal_t2")
    k.V(lambda v: v.tensor_sub(dk[:], kb[:, 1:8], kb[:, 0:7]), reads=[kb_b], writes=[dk_b], tag="cal_dk")
    scf = scs[:].rearrange("p a b -> p (a b)")
    k.S(lambda s: s.activation(scf, scf, AF.Sigmoid, scale=rt[:]),
        reads=[scs_b, rt_b], writes=[scs_b], tag="cal_sig")
    cacc = gacc
    for j in range(7):
        k.V(lambda v, j=j: v.tensor_scalar(camp[:], scf, 7.0, float(-j), ALU.mult, ALU.add),
            reads=[scs_b], writes=[camp_b], tag="cal_a")
        k.V(lambda v: v.tensor_scalar(camp[:], camp[:], 0.0, 1.0, ALU.max, ALU.min),
            reads=[camp_b], writes=[camp_b], tag="cal_b")
        if j == 0:
            k.V(lambda v: v.tensor_scalar(cacc[:], camp[:], dk[:, 0:1], kb[:, 0:1], ALU.mult, ALU.add),
                reads=[camp_b], writes=[cacc_b], tag="cal_c", sreads=[dk_b, kb_b], ptrw=True)
        else:
            k.V(lambda v, j=j: v.scalar_tensor_tensor(cacc[:], camp[:], dk[:, j : j + 1], cacc[:],
                                                      ALU.mult, ALU.add),
                reads=[camp_b, cacc_b], writes=[cacc_b], tag="cal_c", sreads=[dk_b], ptrw=True)
    gates = cacc.rearrange("p (a b) -> p a b", a=32)
    k.dump("gates", cacc[:], [cacc_b], (128, 32 * 72), F32)
    k.dump("sigm", scs[:], [scs_b], (128, 32, 72), F32)

    for tt in range(32):
        k.V(lambda v, tt=tt: v.reduce_max(clv[:], gates[:, tt, 0:8], mybir.AxisListType.X),
            reads=[cacc_b], writes=[clv_b], tag="am_max1")
        k.V(lambda v, tt=tt: v.tensor_scalar(sm1[:, 0:8], gates[:, tt, 0:8], clv[:], None, ALU.is_equal),
            reads=[cacc_b], writes=[sm1_b], tag="am_eq1", sreads=[clv_b], ptrw=True)
        if tt == 0:
            k.dump("dbg_max1", clv[:], [clv_b], (128, 1), F32)
            k.dump("dbg_eq1", sm1[:], [sm1_b], (128, 64), F32)
        k.V(lambda v: v.tensor_mul(sm2[:, 0:8], sm1[:, 0:8], i8mc[:]),
            reads=[sm1_b, i8mc_b], writes=[sm2_b], tag="am_sel1")
        k.V(lambda v: v.tensor_scalar_add(sm2[:, 0:8], sm2[:, 0:8], 8.0),
            reads=[sm2_b], writes=[sm2_b], tag="am_sel1b")
        if tt == 0:
            k.dump("dbg_sel1", sm2[:], [sm2_b], (128, 64), F32)
        k.V(lambda v: v.tensor_reduce(clv[:], sm2[:, 0:8], mybir.AxisListType.X, ALU.min),
            reads=[sm2_b], writes=[clv_b], tag="am_min1")
        if tt == 0:
            k.dump("dbg_cid0", clv[:], [clv_b], (128, 1), F32)
        k.V(lambda v: v.tensor_scalar(sm1[:], iotad8c[:], clv[:], None, ALU.is_equal),
            reads=[iotad8c_b], writes=[sm1_b], tag="am_eq2", sreads=[clv_b], ptrw=True)
        k.V(lambda v: v.tensor_scalar(sm1[:], sm1[:], -1.0, 1e30, ALU.add, ALU.mult),
            reads=[sm1_b], writes=[sm1_b], tag="am_pen")
        k.V(lambda v, tt=tt: v.tensor_add(sm2[:], gates[:, tt, 8:72], sm1[:]),
            reads=[cacc_b, sm1_b], writes=[sm2_b], tag="am_g2")
        k.V(lambda v: v.reduce_max(clv[:], sm2[:], mybir.AxisListType.X),
            reads=[sm2_b], writes=[clv_b], tag="am_max2")
        k.V(lambda v: v.tensor_scalar(sm1[:], sm2[:], clv[:], None, ALU.is_equal),
            reads=[sm2_b], writes=[sm1_b], tag="am_eq3", sreads=[clv_b], ptrw=True)
        k.V(lambda v: v.tensor_mul(sm2[:], sm1[:], i64mc[:]),
            reads=[sm1_b, i64mc_b], writes=[sm2_b], tag="am_sel2")
        k.V(lambda v: v.tensor_scalar_add(sm2[:], sm2[:], 64.0),
            reads=[sm2_b], writes=[sm2_b], tag="am_sel2b")
        k.V(lambda v: v.tensor_reduce(clv[:], sm2[:], mybir.AxisListType.X, ALU.min),
            reads=[sm2_b], writes=[clv_b], tag="am_min2")
        if tt == 0:
            k.dump("dbg_tidx0", clv[:], [clv_b], (128, 1), F32)
            k.dump("dbg_g2", sm2[:], [sm2_b], (128, 64), F32)
        k.V(lambda v, tt=tt: v.tensor_scalar(oh01_tm[:, tt, :], iota64c[:], clv[:], None, ALU.is_equal),
            reads=[iota64c_b], writes=[oh01_tm_b], tag="am_oh", sreads=[clv_b], ptrw=True)

    k.dump("oh01_tm", oh01_tm[:], [oh01_tm_b], (128, 32, 64), BF16)
    for q4 in range(8):
        for j in range(4):
            tt = q4 * 4 + j
            k.T(lambda t, tt=tt, j=j: t.transpose(
                    psx[0][0:64, j * 128 : j * 128 + 128], oh01_tm[:, tt, :], identb[:]),
                reads=[oh01_tm_b, identb_b], writes=[psb[0]], tag="oh_tr")
        k.V(lambda v, q4=q4: v.tensor_copy(oh01_fm[:, q4 * 512 : q4 * 512 + 512], psx[0][0:64, 0:512]),
            reads=[psb[0]], writes=[oh01_fm_b], tag="oh_ev")

    for tt in range(32):
        pt = tt % 4
        k.T(lambda t, tt=tt, pt=pt: t.matmul(
                ps[pt][:, 0:65], oh01_fm[:, tt * 128 : tt * 128 + 128], tab65_sb[0:64, :],
                start=True, stop=True),
            reads=[oh01_fm_b, tab65_b], writes=[psb[pt]], tag="ga_mm")
        k.V(lambda v, tt=tt, pt=pt: v.tensor_copy(gath[:, tt, :], ps[pt][:, 0:65]),
            reads=[psb[pt]], writes=[gath_b], tag="ga_ev")

    k.V(lambda v: v.tensor_scalar_mul(cw2_sb[:].rearrange("p a b -> p (a b)"),
                                      cw2_sb[:].rearrange("p a b -> p (a b)"), 0.5),
        reads=[cw2_b], writes=[cw2_b], tag="cw2_half")
    for h2c in range(2):
        for nt in range(TOK // 512):
            pt = nt % 2
            for c in range(nch):
                k.T(lambda t, h2c=h2c, nt=nt, c=c, pt=pt: t.matmul(
                        ps[pt][:, 0:512], cw1_sb[:, c, h2c * 128 : h2c * 128 + 128],
                        xn0[:, c, nt * 512 : nt * 512 + 512],
                        start=(c == 0), stop=(c == nch - 1)),
                    reads=[cw1_b, xn0_b], writes=[psb[pt]], tag="cp_mm")
            k.V(lambda v, h2c=h2c, pt=pt: v.tensor_scalar(u_sb[:], ps[pt][:, 0:512],
                                                          cb1_sb[:, h2c : h2c + 1], None, ALU.add),
                reads=[psb[pt]], writes=[u_b], tag="cp_u", sreads=[cb1_b], ptrw=True)
            k.S(lambda s: s.activation(e_sb[:], u_sb[:], AF.Erf, scale=0.7071067811865476),
                reads=[u_b], writes=[e_b], tag="cp_erf")
            k.V(lambda v, h2c=h2c, nt=nt: v.scalar_tensor_tensor(
                    h1sb[:, h2c, nt * 512 : nt * 512 + 512], e_sb[:], 1.0, u_sb[:],
                    ALU.add, ALU.mult),
                reads=[e_b, u_b], writes=[h1_b], tag="cp_h1")

    for tt in range(32):
        for h2c in range(2):
            k.T(lambda t, tt=tt, h2c=h2c: t.matmul(
                    ps[2][:, 2 * tt : 2 * tt + 2], h1sb[:, h2c, tt * 128 : tt * 128 + 128],
                    cw2_sb[:, h2c, :], start=(h2c == 0), stop=(h2c == 1)),
                reads=[h1_b, cw2_b], writes=[psb[2]], tag="co_mm")
    for i2 in range(2):
        k.V(lambda v, i2=i2: v.tensor_scalar(compsb[:, i2::2], ps[2][:, i2:64:2],
                                             cb2_sb[:, i2 : i2 + 1], None, ALU.add),
            reads=[psb[2]], writes=[comp_b], tag="co_b", sreads=[cb2_b], ptrw=True)
    k.S(lambda s: s.activation(compsb[:], compsb[:], AF.Tanh),
        reads=[comp_b], writes=[comp_b], tag="co_tanh")

    for tt in range(32):
        k.V(lambda v, tt=tt: v.tensor_scalar(sm3[:], gath[:, tt, 0:16],
                                             compsb[:, 2 * tt : 2 * tt + 1], None, ALU.mult),
            reads=[gath_b], writes=[sm3_b], tag="sp_m1", sreads=[comp_b], ptrw=True)
        k.V(lambda v, tt=tt: v.scalar_tensor_tensor(sm3[:], gath[:, tt, 16:32],
                                                    compsb[:, 2 * tt + 1 : 2 * tt + 2], sm3[:],
                                                    ALU.mult, ALU.add),
            reads=[gath_b, sm3_b], writes=[sm3_b], tag="sp_m2", sreads=[comp_b], ptrw=True)
        k.V(lambda v, tt=tt: v.tensor_add(sm3[:], sm3[:], gath[:, tt, 32:48]),
            reads=[gath_b, sm3_b], writes=[sm3_b], tag="sp_b1")
        k.V(lambda v: v.tensor_scalar_max(sm3[:], sm3[:], 0.0),
            reads=[sm3_b], writes=[sm3_b], tag="sp_relu")
        k.V(lambda v, tt=tt: v.tensor_mul(sm3[:], sm3[:], gath[:, tt, 48:64]),
            reads=[gath_b, sm3_b], writes=[sm3_b], tag="sp_w2")
        k.V(lambda v: v.reduce_sum(mgv[:], sm3[:], mybir.AxisListType.X),
            reads=[sm3_b], writes=[mgv_b], tag="sp_sum")
        k.V(lambda v, tt=tt: v.tensor_scalar(mgv[:], mgv[:], gath[:, tt, 64:65], oscb[:],
                                             ALU.add, ALU.mult),
            reads=[mgv_b], writes=[mgv_b], tag="sp_mag", sreads=[gath_b, oscb_b], ptrw=True)
        k.V(lambda v, tt=tt: v.tensor_scalar(ohm_tm[:], oh01_tm[:, tt, :], mgv[:], None, ALU.mult),
            reads=[oh01_tm_b], writes=[ohm_tm_b], tag="sp_ohm", sreads=[mgv_b], ptrw=True)
        k.T(lambda t, tt=tt: t.transpose(psx[3][0:64, (tt % 4) * 128 : (tt % 4) * 128 + 128],
                                         ohm_tm[:], identb[:]),
            reads=[ohm_tm_b, identb_b], writes=[psb[3]], tag="sp_tr")
        if tt % 4 == 3:
            q4 = tt // 4
            k.V(lambda v, q4=q4: v.tensor_copy(ohs_fm[:, q4 * 512 : q4 * 512 + 512],
                                               psx[3][0:64, 0:512]),
                reads=[psb[3]], writes=[ohs_fm_b], tag="sp_ev")

    k.dump("ohs_fm", ohs_fm[:], [ohs_fm_b], (64, TOK), BF16)
    k.dump("oh01_fm", oh01_fm[:], [oh01_fm_b], (64, TOK), BF16)
    k.dump("gath", gath[:], [gath_b], (128, 32, 65), F32)
    k.dump("compsb", compsb[:], [comp_b], (128, 64), F32)
    k.release_arena()
    k.release_arena()
    k.barrier()

    if upto <= 3:
        k.p.finalize(nc)
        return k

    import os as _os
    _skip_p4 = bool(int(_os.environ.get("SKIP_P4", "0")))
    s0_d = k.dscr("s0_d", (B_PER_CORE, D, T), F32)
    s0_d_b = Buf("s0_d")

    ar.mark()
    ql_sb = ar.alloc([128, T], BF16, "ql_sb")
    p_sb = [ar.alloc([128, 1024], BF16, f"p_sb{i}") for i in range(2)]
    ctx_sb = ar.alloc([128, 1024], BF16, "ctx_sb")
    s0_sb = ar.alloc([128, T], F32, "s0_sb")
    rz64 = ar.alloc([64, 1024], F32, "rz64")
    trilb = ar.alloc([128, 128], BF16, "trilb")
    ql_b = k.buf("ql_sb")
    p_b = [k.buf(f"p_sb{i}") for i in range(2)]
    ctx_sb_b = k.buf("ctx_sb")
    s0_sb_b = k.buf("s0_sb")
    rz64_b = k.buf("rz64")
    trilb_b = k.buf("trilb")
    tril_in = k.din("tril_bf", (128, 128), BF16)
    k.GQ(lambda g: g.dma_start(trilb[:], tril_in.ap()), writes=[trilb_b], tag="ld_tril")

    for b in range(B_PER_CORE if not _skip_p4 else 0):
        for h in range(16):
            for half in range(2):
                for n2 in range(2):
                    k.T(lambda t, b=b, h=h, half=half, n2=n2: t.matmul(
                            ps[half][:, n2 * 512 : n2 * 512 + 512],
                            bq[:, h, :],
                            q1[:, b * T + half * 1024 + n2 * 512 : b * T + half * 1024 + n2 * 512 + 512],
                            start=True, stop=True),
                        reads=[bq_b, q1_b], writes=[psb[half]], tag="ql_mm")
                k.V(lambda v, half=half: v.tensor_copy(
                        ql_sb[:, half * 1024 : half * 1024 + 1024], ps[half][:, :]),
                    reads=[psb[half]], writes=[ql_b], tag="ql_ev")
            for half in range(2):
                h0 = half * 1024
                nsig = 8 * (half + 1)
                for sig in range(nsig):
                    vstart = max(sig * 128, h0)
                    nv = h0 + 1024 - vstart
                    rel = vstart - h0
                    pi = sig % 2
                    last = sig == nsig - 1
                    segs = []
                    if rel < 512:
                        segs.append((rel, 512 - rel))
                        segs.append((512, 512))
                    else:
                        segs.append((rel, 1024 - rel))
                    for o, w in segs:
                        k.T(lambda t, b=b, sig=sig, pi=pi, o=o, w=w, h0=h0: t.matmul(
                                ps[pi][:, o : o + w],
                                ckv[:, b * T + sig * 128 : b * T + sig * 128 + 128],
                                ql_sb[:, h0 + o : h0 + o + w],
                                start=True, stop=True),
                            reads=[ckv_b, ql_b], writes=[psb[pi]], tag="sc_mm")
                    k.S(lambda s, pi=pi, rel=rel, nv=nv: s.activation(
                            p_sb[pi][:, rel : rel + nv], ps[pi][:, rel : rel + nv],
                            AF.Exp, scale=1.0 / 8.0),
                        reads=[psb[pi]], writes=[p_b[pi]], tag="sc_exp")
                    if sig * 128 >= h0:
                        k.V(lambda v, pi=pi, rel=rel: v.tensor_mul(
                                p_sb[pi][:, rel : rel + 128], p_sb[pi][:, rel : rel + 128], trilb[:]),
                            reads=[trilb_b, p_b[pi]], writes=[p_b[pi]], tag="sc_mask")
                    for o, w in segs:
                        k.T(lambda t, pi=pi, o=o, w=w, sig=sig, last=last: t.matmul(
                                ps[2][:, o : o + w], onesb[:], p_sb[pi][:, o : o + w],
                                start=(sig == 0), stop=last),
                            reads=[onesb_b, p_b[pi]], writes=[psb[2]], tag="z_mm")
                    for o, w in segs:
                        k.T(lambda t, b=b, sig=sig, pi=pi, o=o, w=w, last=last: t.matmul(
                                ps[3][:, o : o + w], ckv_tm[:, b * 16 + sig, :],
                                p_sb[pi][:, o : o + w],
                                start=(sig == 0), stop=last),
                            reads=[ckv_tm_b, p_b[pi]], writes=[psb[3]], tag="ctx_mm")
                k.S(lambda s: s.activation(rz64[:, :], ps[2][0:64, :], AF.Ln),
                    reads=[psb[2]], writes=[rz64_b], tag="lnz")
                k.S(lambda s: s.activation(rz64[:, :], rz64[:, :], AF.Exp, scale=-1.0),
                    reads=[rz64_b], writes=[rz64_b], tag="rz")
                k.V(lambda v: v.tensor_copy(ctx_sb[:, :], ps[3][:, :]),
                    reads=[psb[3]], writes=[ctx_sb_b], tag="ctx_ev")
                for n2 in range(2):
                    k.T(lambda t, h=h, n2=n2: t.matmul(
                            ps[2][0:64, n2 * 512 : n2 * 512 + 512],
                            v2[:, h * 64 : h * 64 + 64],
                            ctx_sb[:, n2 * 512 : n2 * 512 + 512],
                            start=True, stop=True),
                        reads=[v2_b, ctx_sb_b], writes=[psb[2]], tag="y_mm")
                po = (h % 2) * 64
                k.V(lambda v, half=half, po=po: v.tensor_mul(
                        s0_sb[po : po + 64, half * 1024 : half * 1024 + 1024],
                        ps[2][0:64, :], rz64[:, :]),
                    reads=[psb[2], rz64_b], writes=[s0_sb_b], tag="y_ev")
            if h % 2 == 1:
                k.SQ(lambda s, b=b, h=h: s.dma_start(
                        s0_d.ap()[b, (h // 2) * 128 : (h // 2) * 128 + 128, :], s0_sb[:, :]),
                    reads=[s0_sb_b], writes=[s0_d_b], tag="st_s0")
    k.release_arena()
    k.barrier()

    if upto <= 4:
        k.p.finalize(nc)
        return k

    dirs_in = k.din("dirs", (128, D))
    gam0_in = k.din("gamma0", (128, NCH))
    bet0_in = k.din("beta0", (128, NCH))
    y_d = k.dscr("y_d", (B_PER_CORE, D, T), BF16)
    x1_d = k.dscr("x1_d", (B_PER_CORE, D, T), F32)
    xn1_d = k.dscr("xn1_d", (B_PER_CORE, D, T), BF16)
    y_d_b = Buf("y_d")
    x1_d_b = Buf("x1_d")
    xn1_d_b = Buf("xn1_d")

    ar.mark()
    dirs_sb = ar.alloc([128, D], BF16, "dirs_sb")
    gam0_sb = ar.alloc([128, nch], F32, "gam0_sb")
    bet0_sb = ar.alloc([128, nch], F32, "bet0_sb")
    mu_bc = [ar.alloc([128, BLK], F32, f"mu_bc{i}") for i in range(2)]
    r_bc = [ar.alloc([128, BLK], F32, f"r_bc{i}") for i in range(2)]
    xti = [ar.alloc([128, BLK], F32, f"xti{i}") for i in range(2)]
    ybt = [ar.alloc([128, BLK], BF16, f"ybt{i}") for i in range(2)]
    sqb = ar.alloc([128, BLK], BF16, "sqb")
    s0t = [ar.alloc([128, BLK], F32, f"s0t{i}") for i in range(2)]
    muy = ar.alloc([128, 4, BLK], F32, "muy")
    ry = ar.alloc([128, 4, BLK], F32, "ry")
    yint = [ar.alloc([128, BLK], BF16, f"yint{i}") for i in range(2)]
    g0t = [ar.alloc([128, BLK], F32, f"g0t{i}") for i in range(2)]
    x1t = [ar.alloc([128, BLK], F32, f"x1t{i}") for i in range(2)]
    x1b = ar.alloc([128, nch, T], BF16, "x1b")
    mu1 = ar.alloc([128, 4, BLK], F32, "mu1")
    r1 = ar.alloc([128, 4, BLK], F32, "r1")
    tmpb = ar.alloc([128, BLK], F32, "tmpb")
    xn1t = [ar.alloc([128, BLK], BF16, f"xn1t{i}") for i in range(2)]

    dirs_b = k.buf("dirs_sb"); gam0_b = k.buf("gam0_sb"); bet0_b = k.buf("bet0_sb")
    mu_bc_b = [k.buf(f"mu_bc{i}") for i in range(2)]
    r_bc_b = [k.buf(f"r_bc{i}") for i in range(2)]
    xti_b = [k.buf(f"xti{i}") for i in range(2)]
    ybt_b = [k.buf(f"ybt{i}") for i in range(2)]
    sqb_b = k.buf("sqb")
    s0t_b = [k.buf(f"s0t{i}") for i in range(2)]
    muy_b = k.buf("muy"); ry_b = k.buf("ry")
    yint_b = [k.buf(f"yint{i}") for i in range(2)]
    g0t_b = [k.buf(f"g0t{i}") for i in range(2)]
    x1t_b = [k.buf(f"x1t{i}") for i in range(2)]
    x1b_b = k.buf("x1b")
    mu1_b = k.buf("mu1"); r1_b = k.buf("r1"); tmpb_b = k.buf("tmpb")
    xn1t_b = [k.buf(f"xn1t{i}") for i in range(2)]

    k.GQ(lambda g: g.dma_start(dirs_sb[:], dirs_in.ap()), writes=[dirs_b], tag="ld_dirs")
    k.SQ(lambda s: s.dma_start(gam0_sb[:], gam0_in.ap()), writes=[gam0_b], tag="ld_g0")
    k.SQ(lambda s: s.dma_start(bet0_sb[:], bet0_in.ap()), writes=[bet0_b], tag="ld_b0")

    nblk_b = T // BLK
    for b in range(B_PER_CORE):
        for blk in range(nblk_b):
            t0 = blk * BLK
            g = b * T + t0
            i = blk % 2
            gb = b * nblk_b + blk
            k.GQ(lambda gq, gb=gb, i=i: gq.dma_start(mu_bc[i][:], mur2_d.ap()[0, gb]),
                 reads=[mur2_d_b], writes=[mu_bc_b[i]], tag="p5_mubc")
            k.GQ(lambda gq, gb=gb, i=i: gq.dma_start(r_bc[i][:], mur2_d.ap()[1, gb]),
                 reads=[mur2_d_b], writes=[r_bc_b[i]], tag="p5_rbc")
            for c in range(nch):
                ic = (blk * nch + c) % 2
                k.T(lambda t, c=c, g=g: t.matmul(
                        ps[0][:, 0:BLK], dirs_sb[0:64, c * 128 : c * 128 + 128],
                        ohs_fm[:, g : g + BLK], start=True, stop=True),
                    reads=[dirs_b, ohs_fm_b], writes=[psb[0]], tag="p5_dir")
                k.SQ(lambda s, b=b, c=c, t0=t0, ic=ic: s.dma_start(
                        xti[ic][:], xfm.ap()[b, c * 128 : c * 128 + 128, t0 : t0 + BLK]),
                     reads=[xfm_b], writes=[xti_b[ic]], tag="p5_ldx")
                k.V(lambda v, ic=ic, i=i: v.tensor_sub(xti[ic][:], xti[ic][:], mu_bc[i][:]),
                    reads=[mu_bc_b[i]], writes=[xti_b[ic]], tag="p5_xnsub")
                k.V(lambda v, ic=ic, i=i: v.tensor_mul(xti[ic][:], xti[ic][:], r_bc[i][:]),
                    reads=[r_bc_b[i]], writes=[xti_b[ic]], tag="p5_xnmul")
                import os as _os2
                if int(_os2.environ.get("NO_PSUM_Y", "0")):
                    k.V(lambda v, ic=ic: v.tensor_copy(ybt[ic][:], xti[ic][:]),
                        reads=[xti_b[ic]], writes=[ybt_b[ic]], tag="p5_y")
                else:
                    k.V(lambda v, ic=ic: v.tensor_add(ybt[ic][:], xti[ic][:], ps[0][:, 0:BLK]),
                        reads=[xti_b[ic], psb[0]], writes=[ybt_b[ic]], tag="p5_y")
                k.SQ(lambda s, b=b, c=c, t0=t0, ic=ic: s.dma_start(
                        y_d.ap()[b, c * 128 : c * 128 + 128, t0 : t0 + BLK], ybt[ic][:]),
                     reads=[ybt_b[ic]], writes=[y_d_b], tag="p5_sty")
                k.V(lambda v, ic=ic: v.tensor_mul(sqb[:], ybt[ic][:], ybt[ic][:]),
                    reads=[ybt_b[ic]], writes=[sqb_b], tag="p5_sq")
                k.T(lambda t, c=c, ic=ic: t.matmul(ps[1][:, 0:BLK], onesb[:], ybt[ic][:],
                                                   start=(c == 0), stop=(c == nch - 1)),
                    reads=[ybt_b[ic], onesb_b], writes=[psb[1]], tag="p5_mmS")
                k.T(lambda t, c=c: t.matmul(ps[2][:, 0:BLK], onesb[:], sqb[:],
                                            start=(c == 0), stop=(c == nch - 1)),
                    reads=[sqb_b, onesb_b], writes=[psb[2]], tag="p5_mmS2")

            k.V(lambda v, blk=blk: v.tensor_scalar_mul(muy[:, blk, :], ps[1][:, 0:BLK], 1.0 / D),
                reads=[psb[1]], writes=[muy_b], tag="p5_muy")
            k.V(lambda v, blk=blk: v.tensor_mul(tmpb[:], muy[:, blk, :], muy[:, blk, :]),
                reads=[muy_b], writes=[tmpb_b], tag="p5_musq")
            k.V(lambda v, blk=blk: v.scalar_tensor_tensor(ry[:, blk, :], ps[2][:, 0:BLK], 1.0 / D,
                                                          tmpb[:], ALU.mult, ALU.subtract),
                reads=[psb[2], tmpb_b], writes=[ry_b], tag="p5_vary")
            k.S(lambda s, blk=blk: s.activation(ry[:, blk, :], ry[:, blk, :], AF.Ln, bias=epsc[:]),
                reads=[ry_b, epsc_b], writes=[ry_b], tag="p5_lny")
            k.S(lambda s, blk=blk: s.activation(ry[:, blk, :], ry[:, blk, :], AF.Exp, scale=-0.5),
                reads=[ry_b], writes=[ry_b], tag="p5_expy")
        if b == 1:
            k.dump("dbg_ry1", ry[:], [ry_b], (128, 4, BLK), F32)
            k.dump("dbg_muy1", muy[:], [muy_b], (128, 4, BLK), F32)
        for blk in range(nblk_b):
            t0 = blk * BLK
            for c in range(nch):
                ic = (blk * nch + c) % 2
                k.SQ(lambda s, b=b, c=c, t0=t0, ic=ic: s.dma_start(
                        yint[ic][:], y_d.ap()[b, c * 128 : c * 128 + 128, t0 : t0 + BLK]),
                     reads=[y_d_b], writes=[yint_b[ic]], tag="p5_ldy")
                k.V(lambda v, blk=blk, ic=ic: v.tensor_sub(g0t[ic][:], yint[ic][:], muy[:, blk, :]),
                    reads=[yint_b[ic], muy_b], writes=[g0t_b[ic]], tag="p5_ynsub")
                k.V(lambda v, blk=blk, ic=ic: v.tensor_mul(g0t[ic][:], g0t[ic][:], ry[:, blk, :]),
                    reads=[ry_b], writes=[g0t_b[ic]], tag="p5_ynmul")
                k.S(lambda s, c=c, ic=ic: s.activation(g0t[ic][:], g0t[ic][:], AF.Silu,
                                                       bias=bet0_sb[:, c : c + 1],
                                                       scale=gam0_sb[:, c : c + 1]),
                    reads=[g0t_b[ic]], writes=[g0t_b[ic]], tag="p5_silu",
                    sreads=[gam0_b, bet0_b])
                k.SQ(lambda s, b=b, c=c, t0=t0, ic=ic: s.dma_start(
                        x1t[ic][:], xfm.ap()[b, c * 128 : c * 128 + 128, t0 : t0 + BLK]),
                     reads=[xfm_b], writes=[x1t_b[ic]], tag="p5_ldx2")
                k.SQ(lambda s, b=b, c=c, t0=t0, ic=ic: s.dma_start(
                        s0t[ic][:], s0_d.ap()[b, c * 128 : c * 128 + 128, t0 : t0 + BLK]),
                     reads=[s0_d_b], writes=[s0t_b[ic]], tag="p5_lds0")
                k.V(lambda v, ic=ic: v.tensor_mul(g0t[ic][:], g0t[ic][:], s0t[ic][:]),
                    reads=[s0t_b[ic]], writes=[g0t_b[ic]], tag="p5_inc")
                k.V(lambda v, ic=ic: v.tensor_add(x1t[ic][:], x1t[ic][:], g0t[ic][:]),
                    reads=[g0t_b[ic]], writes=[x1t_b[ic]], tag="p5_x1")
                k.SQ(lambda s, b=b, c=c, t0=t0, ic=ic: s.dma_start(
                        x1_d.ap()[b, c * 128 : c * 128 + 128, t0 : t0 + BLK], x1t[ic][:]),
                     reads=[x1t_b[ic]], writes=[x1_d_b], tag="p5_stx1")
                k.V(lambda v, c=c, t0=t0, ic=ic: v.tensor_copy(x1b[:, c, t0 : t0 + BLK], x1t[ic][:]),
                    reads=[x1t_b[ic]], writes=[x1b_b], tag="p5_x1b")
                k.V(lambda v, c=c, t0=t0: v.tensor_mul(sqb[:], x1b[:, c, t0 : t0 + BLK],
                                                       x1b[:, c, t0 : t0 + BLK]),
                    reads=[x1b_b], writes=[sqb_b], tag="p5_sq1")
                k.T(lambda t, c=c, t0=t0: t.matmul(ps[1][:, 0:BLK], onesb[:],
                                                   x1b[:, c, t0 : t0 + BLK],
                                                   start=(c == 0), stop=(c == nch - 1)),
                    reads=[x1b_b, onesb_b], writes=[psb[1]], tag="p5_mmT")
                k.T(lambda t, c=c: t.matmul(ps[2][:, 0:BLK], onesb[:], sqb[:],
                                            start=(c == 0), stop=(c == nch - 1)),
                    reads=[sqb_b, onesb_b], writes=[psb[2]], tag="p5_mmT2")
            k.V(lambda v, blk=blk: v.tensor_scalar_mul(mu1[:, blk, :], ps[1][:, 0:BLK], 1.0 / D),
                reads=[psb[1]], writes=[mu1_b], tag="p5_mu1")
            k.V(lambda v, blk=blk: v.tensor_mul(tmpb[:], mu1[:, blk, :], mu1[:, blk, :]),
                reads=[mu1_b], writes=[tmpb_b], tag="p5_musq1")
            k.V(lambda v, blk=blk: v.scalar_tensor_tensor(r1[:, blk, :], ps[2][:, 0:BLK], 1.0 / D,
                                                          tmpb[:], ALU.mult, ALU.subtract),
                reads=[psb[2], tmpb_b], writes=[r1_b], tag="p5_var1")
        for blk in range(nblk_b):
            t0 = blk * BLK
            k.S(lambda s, blk=blk: s.activation(r1[:, blk, :], r1[:, blk, :], AF.Ln, bias=epsc[:]),
                reads=[r1_b, epsc_b], writes=[r1_b], tag="p5_ln1")
            k.S(lambda s, blk=blk: s.activation(r1[:, blk, :], r1[:, blk, :], AF.Exp, scale=-0.5),
                reads=[r1_b], writes=[r1_b], tag="p5_exp1")
            for c in range(nch):
                ic = (blk * nch + c) % 2
                k.V(lambda v, blk=blk, c=c, t0=t0, ic=ic: v.tensor_sub(
                        xn1t[ic][:], x1b[:, c, t0 : t0 + BLK], mu1[:, blk, :]),
                    reads=[x1b_b, mu1_b], writes=[xn1t_b[ic]], tag="p5_xn1s")
                k.V(lambda v, blk=blk, ic=ic: v.tensor_mul(xn1t[ic][:], xn1t[ic][:], r1[:, blk, :]),
                    reads=[r1_b], writes=[xn1t_b[ic]], tag="p5_xn1m")
                k.SQ(lambda s, b=b, c=c, t0=t0, ic=ic: s.dma_start(
                        xn1_d.ap()[b, c * 128 : c * 128 + 128, t0 : t0 + BLK], xn1t[ic][:]),
                     reads=[xn1t_b[ic]], writes=[xn1_d_b], tag="p5_stxn1")
    k.release_arena()
    k.release_arena()
    k.barrier()

    if upto <= 5:
        k.p.finalize(nc)
        return k

    env = dict(locals())
    _build_p7(k, env)
    if upto <= 7:
        k.p.finalize(nc)
        return k
    _build_p8(k, env)
    k.p.finalize(nc)
    return k


def _build_p7(k, env):
    nc = k.nc
    ar = k.ar
    nch = NCH
    ps, psx, psb = env["ps"], env["psx"], env["psb"]
    onesb, onesb_b = env["onesb"], env["onesb_b"]
    epsc, epsc_b = env["epsc"], env["epsc_b"]
    xn1_d, xn1_d_b = env["xn1_d"], env["xn1_d_b"]

    qwT = k.din("qwT", (D, D))
    kwT = k.din("kwT", (D, D))
    vwT = k.din("vwT", (D, D))
    owT = k.din("owT", (D, D))
    qb_in = k.din("qb", (128, nch))
    kb_in = k.din("kb", (128, nch))
    ob_in = k.din("ob", (128, nch))
    vbrow_in = k.din("vbrow", (128, D))
    nw_in = k.din("nw", (128, nch))
    nb_in = k.din("nb", (128, nch))
    g1_d = k.dscr("g1_d", (B_PER_CORE, D, T), F32)
    g1_d_b = Buf("g1_d")
    env["g1_d"] = g1_d
    env["g1_d_b"] = g1_d_b

    ar.mark()
    xnl_a = ar.alloc([128, 4, TOK], BF16, "xnl_a")
    xnl_b4 = ar.alloc([128, 4, 1024], BF16, "xnl_b4")
    vw_sb = ar.alloc([128, nch, D], BF16, "vw_sb")
    qkw = ar.alloc([128, nch, 256], BF16, "qkw")
    oww = ar.alloc([128, nch, 128], BF16, "oww")
    qb_sb = ar.alloc([128, nch], F32, "qb_sb")
    kb_sb = ar.alloc([128, nch], F32, "kb_sb")
    ob_sb = ar.alloc([128, nch], F32, "ob_sb")
    vbr_sb = ar.alloc([128, D], BF16, "vbr_sb")
    nw_sb = ar.alloc([128, nch], F32, "nw_sb")
    nb_sb = ar.alloc([128, nch], F32, "nb_sb")
    vtm = ar.alloc([128, 16, D], BF16, "vtm")
    aob = ar.alloc([128, nch, T], BF16, "aob")
    qh = ar.alloc([128, T], BF16, "qh")
    kh = ar.alloc([128, T], BF16, "kh")
    p7sb = [ar.alloc([128, 1024], BF16, f"p7sb{i}") for i in range(2)]
    rzf = ar.alloc([128, 1024], F32, "rzf")
    x2t = [ar.alloc([128, nch, BLK], BF16, f"p7xn{i}") for i in range(2)]
    sq7 = ar.alloc([128, BLK], BF16, "sq7")
    mue = [ar.alloc([128, BLK], F32, f"mue{i}") for i in range(2)]
    ree = [ar.alloc([128, BLK], F32, f"ree{i}") for i in range(2)]
    tmp7 = ar.alloc([128, BLK], F32, "tmp7")
    xnt7 = [ar.alloc([128, BLK], F32, f"xnt7{i}") for i in range(2)]
    g1t = [ar.alloc([128, BLK], F32, f"g1t{i}") for i in range(2)]
    xt7 = [ar.alloc([128, BLK], BF16, f"xt7{i}") for i in range(2)]

    xnl_a_b = k.buf("xnl_a"); xnl_b4_b = k.buf("xnl_b4")
    vw_b = k.buf("vw_sb"); qkw_qb = k.buf("qkw_q"); qkw_kb = k.buf("qkw_k"); oww_b = k.buf("oww")
    qb_b = k.buf("qb_sb"); kb_b = k.buf("kb_sb"); ob_b = k.buf("ob_sb")
    vbr_b = k.buf("vbr_sb"); nw_b = k.buf("nw_sb"); nb_b = k.buf("nb_sb")
    vtm_b = k.buf("vtm"); aob_b = k.buf("aob")
    qh_b = k.buf("qh"); kh_b = k.buf("kh")
    p7sb_b = [k.buf(f"p7sb{i}") for i in range(2)]
    rzf_b = k.buf("rzf")
    x2t_b = [k.buf(f"p7xn{i}") for i in range(2)]
    sq7_b = k.buf("sq7")
    mue_b = [k.buf(f"mue{i}") for i in range(2)]
    ree_b = [k.buf(f"ree{i}") for i in range(2)]
    tmp7_b = k.buf("tmp7")
    xnt7_b = [k.buf(f"xnt7{i}") for i in range(2)]
    g1t_b = [k.buf(f"g1t{i}") for i in range(2)]
    xt7_b = [k.buf(f"xt7{i}") for i in range(2)]

    ch_rows = lambda t: t.ap().rearrange("(c p) m -> p c m", p=128)
    k.GQ(lambda g: g.dma_start(vw_sb[:], ch_rows(vwT)), writes=[vw_b], tag="ld_vw")
    for nm, sb, bb, ti in (("qb", qb_sb, qb_b, qb_in), ("kb", kb_sb, kb_b, kb_in),
                           ("ob", ob_sb, ob_b, ob_in),
                           ("nw", nw_sb, nw_b, nw_in), ("nb", nb_sb, nb_b, nb_in)):
        k.SQ(lambda s, sb=sb, ti=ti: s.dma_start(sb[:], ti.ap()), writes=[bb], tag="ld7_" + nm)
    k.GQ(lambda g: g.dma_start(vbr_sb[:], vbrow_in.ap()), writes=[vbr_b], tag="ld7_vbr")

    MC = [8, 4, 2, 1]

    for blk in range(TOK // BLK):
        b, t0 = divmod(blk * BLK, T)
        i = blk % 2
        k.SQ(lambda s, b=b, t0=t0, i=i: s.dma_start(
                x2t[i][:], xn1_d.ap()[b].rearrange("(c p) t -> p c t", p=128)[:, :, t0 : t0 + BLK]),
             reads=[xn1_d_b], writes=[x2t_b[i]], tag="p7_ldxn")
        for c in range(nch):
            k.V(lambda v, i=i, c=c: v.tensor_mul(sq7[:], x2t[i][:, c, :], x2t[i][:, c, :]),
                reads=[x2t_b[i]], writes=[sq7_b], tag="p7_sq")
            k.T(lambda t, c=c, i=i: t.matmul(ps[1][:, 0:BLK], onesb[:], x2t[i][:, c, :],
                                             start=(c == 0), stop=(c == nch - 1)),
                reads=[x2t_b[i], onesb_b], writes=[psb[1]], tag="p7_mmS")
            k.T(lambda t, c=c: t.matmul(ps[2][:, 0:BLK], onesb[:], sq7[:],
                                        start=(c == 0), stop=(c == nch - 1)),
                reads=[sq7_b, onesb_b], writes=[psb[2]], tag="p7_mmS2")
        k.V(lambda v, i=i: v.tensor_scalar_mul(mue[i][:], ps[1][:, 0:BLK], 1.0 / D),
            reads=[psb[1]], writes=[mue_b[i]], tag="p7_mu")
        k.V(lambda v, i=i: v.tensor_mul(tmp7[:], mue[i][:], mue[i][:]),
            reads=[mue_b[i]], writes=[tmp7_b], tag="p7_musq")
        k.V(lambda v, i=i: v.scalar_tensor_tensor(ree[i][:], ps[2][:, 0:BLK], 1.0 / D,
                                                  tmp7[:], ALU.mult, ALU.subtract),
            reads=[psb[2], tmp7_b], writes=[ree_b[i]], tag="p7_var")
        k.S(lambda s, i=i: s.activation(ree[i][:], ree[i][:], AF.Ln, bias=epsc[:]),
            reads=[ree_b[i], epsc_b], writes=[ree_b[i]], tag="p7_ln")
        k.S(lambda s, i=i: s.activation(ree[i][:], ree[i][:], AF.Exp, scale=-0.5),
            reads=[ree_b[i]], writes=[ree_b[i]], tag="p7_exp")
        e = t0 // 512
        nchunks = nch if e == 0 else MC[e]
        for c in range(nchunks):
            k.V(lambda v, i=i, c=c: v.tensor_sub(xnt7[i][:], x2t[i][:, c, :], mue[i][:]),
                reads=[x2t_b[i], mue_b[i]], writes=[xnt7_b[i]], tag="p7_xs")
            k.V(lambda v, i=i: v.tensor_mul(xnt7[i][:], xnt7[i][:], ree[i][:]),
                reads=[ree_b[i]], writes=[xnt7_b[i]], tag="p7_xm")
            if c < 4:
                dst = xnl_a[:, c, blk * BLK : blk * BLK + BLK]
                dbuf = xnl_a_b
            else:
                dst = xnl_b4[:, c - 4, b * BLK : b * BLK + BLK]
                dbuf = xnl_b4_b
            k.S(lambda s, i=i, c=c, dst=dst: s.activation(dst, xnt7[i][:], AF.Identity,
                                                          bias=nb_sb[:, c : c + 1],
                                                          scale=nw_sb[:, c : c + 1]),
                reads=[xnt7_b[i]], writes=[dbuf], tag="p7_gb", sreads=[nw_b, nb_b])

    def xnl(c, cols, b):
        if c < 4:
            return xnl_a[:, c, cols[0] : cols[1]]
        off = (cols[0] - b * T) + b * BLK
        return xnl_b4[:, c - 4, off : off + (cols[1] - cols[0])]

    for b in range(B_PER_CORE):
        for e in range(4):
            tg0 = b * T + e * 512
            for tt in range(4):
                c0 = tg0 + tt * 128
                ti = e * 4 + tt
                for n2 in range(2):
                    for mc in range(MC[e]):
                        k.T(lambda t, ti=ti, c0=c0, mc=mc, n2=n2, b=b: t.matmul(
                                ps[0][:, n2 * 512 : n2 * 512 + 512],
                                xnl(mc, (c0, c0 + 128), b),
                                vw_sb[:, mc, n2 * 512 : n2 * 512 + 512],
                                start=(mc == 0), stop=False),
                            reads=[xnl_a_b, xnl_b4_b, vw_b], writes=[psb[0]], tag="p7v_mm")
                    k.T(lambda t, n2=n2: t.matmul(
                            ps[0][:, n2 * 512 : n2 * 512 + 512],
                            onesb[0:1, :], vbr_sb[0:1, n2 * 512 : n2 * 512 + 512],
                            start=False, stop=True),
                        reads=[onesb_b, vbr_b], writes=[psb[0]], tag="p7v_bias")
                k.V(lambda v, ti=ti: v.tensor_copy(vtm[:, ti, :], ps[0][:, 0:1024]),
                    reads=[psb[0]], writes=[vtm_b], tag="p7v_ev")
        for h in range(8):
            k.GQ(lambda g, h=h: g.dma_start(
                    qkw[:, :, 0:128],
                    qwT.ap().rearrange("(c p) m -> p c m", p=128)[:, :, h * 128 : h * 128 + 128]),
                 writes=[qkw_qb], tag="p7_ldqw")
            k.GQ(lambda g, h=h: g.dma_start(
                    qkw[:, :, 128:256],
                    kwT.ap().rearrange("(c p) m -> p c m", p=128)[:, :, h * 128 : h * 128 + 128]),
                 writes=[qkw_kb], tag="p7_ldkw")
            for e in range(4):
                cg0 = b * T + e * 512
                for off, dstw, dbuf, bias, bbuf, tg in (
                        (0, qh, qh_b, qb_sb, qb_b, "p7_qmm"),
                        (128, kh, kh_b, kb_sb, kb_b, "p7_kmm")):
                    wb = qkw_qb if off == 0 else qkw_kb
                    for mc in range(MC[e]):
                        k.T(lambda t, e=e, cg0=cg0, mc=mc, off=off, b=b: t.matmul(
                                ps[3][:, 0:512], qkw[:, mc, off : off + 128],
                                xnl(mc, (cg0, cg0 + 512), b),
                                start=(mc == 0), stop=(mc == MC[e] - 1)),
                            reads=[wb, xnl_a_b, xnl_b4_b], writes=[psb[3]], tag=tg)
                    k.V(lambda v, e=e, dstw=dstw, bias=bias, h=h: v.tensor_scalar(
                            dstw[:, e * 512 : e * 512 + 512], ps[3][:, 0:512],
                            bias[:, h : h + 1], None, ALU.add),
                        reads=[psb[3]], writes=[dbuf], tag=tg + "e", sreads=[bbuf], ptrw=True)
            for half in range(2):
                hc0 = half * 1024
                for sig in range(16):
                    pi = sig % 2
                    for n2 in range(2):
                        k.T(lambda t, sig=sig, pi=pi, n2=n2, hc0=hc0: t.matmul(
                                ps[pi][:, n2 * 512 : n2 * 512 + 512],
                                kh[:, sig * 128 : sig * 128 + 128],
                                qh[:, hc0 + n2 * 512 : hc0 + n2 * 512 + 512],
                                start=True, stop=True),
                            reads=[kh_b, qh_b], writes=[psb[pi]], tag="p7_sc")
                    k.S(lambda s, pi=pi: s.activation(p7sb[pi][:], ps[pi][:, :], AF.Exp,
                                                      scale=0.03125),
                        reads=[psb[pi]], writes=[p7sb_b[pi]], tag="p7_exp2")
                    for n2 in range(2):
                        k.T(lambda t, pi=pi, n2=n2, sig=sig: t.matmul(
                                ps[2][:, n2 * 512 : n2 * 512 + 512], onesb[:],
                                p7sb[pi][:, n2 * 512 : n2 * 512 + 512],
                                start=(sig == 0), stop=(sig == 15)),
                            reads=[onesb_b, p7sb_b[pi]], writes=[psb[2]], tag="p7_z")
                        k.T(lambda t, pi=pi, n2=n2, sig=sig, h=h: t.matmul(
                                ps[3][:, n2 * 512 : n2 * 512 + 512],
                                vtm[:, sig, h * 128 : h * 128 + 128],
                                p7sb[pi][:, n2 * 512 : n2 * 512 + 512],
                                start=(sig == 0), stop=(sig == 15)),
                            reads=[vtm_b, p7sb_b[pi]], writes=[psb[3]], tag="p7_ctx")
                k.S(lambda s: s.activation(rzf[:], ps[2][:, :], AF.Ln),
                    reads=[psb[2]], writes=[rzf_b], tag="p7_lnz")
                k.S(lambda s: s.activation(rzf[:], rzf[:], AF.Exp, scale=-1.0),
                    reads=[rzf_b], writes=[rzf_b], tag="p7_rz")
                k.V(lambda v, h=h, hc0=hc0, b=b: v.tensor_mul(
                        aob[:, h, hc0 : hc0 + 1024], ps[3][:, :], rzf[:]),
                    reads=[psb[3], rzf_b], writes=[aob_b], tag="p7_cev")
        for e in range(4):
            t0 = e * 512
            for oc in range(nch):
                i = oc % 2
                if oc < MC[e]:
                    k.GQ(lambda g, e=e, oc=oc: g.dma_start(
                            oww[:, 0 : MC[e], :],
                            owT.ap().rearrange("(c p) m -> p c m", p=128)[:, 0 : MC[e],
                                                                         oc * 128 : oc * 128 + 128]),
                         writes=[oww_b], tag="p7_ldow")
                    for mc in range(MC[e]):
                        k.T(lambda t, e=e, mc=mc, t0=t0, b=b: t.matmul(
                                ps[0][:, 0:512], oww[:, mc, :], aob[:, mc, t0 : t0 + 512],
                                start=(mc == 0), stop=(mc == MC[e] - 1)),
                            reads=[oww_b, aob_b], writes=[psb[0]], tag="p7_omm")
                k.SQ(lambda s, b=b, oc=oc, t0=t0, i=i: s.dma_start(
                        xt7[i][:], xn1_d.ap()[b, oc * 128 : oc * 128 + 128, t0 : t0 + 512]),
                     reads=[xn1_d_b], writes=[xt7_b[i]], tag="p7_ldxn2")
                if oc < MC[e]:
                    k.V(lambda v, oc=oc, i=i: v.tensor_scalar(g1t[i][:], ps[0][:, 0:512],
                                                              ob_sb[:, oc : oc + 1], None, ALU.add),
                        reads=[psb[0]], writes=[g1t_b[i]], tag="p7_gb1", sreads=[ob_b], ptrw=True)
                    k.V(lambda v, i=i: v.tensor_add(g1t[i][:], g1t[i][:], xt7[i][:]),
                        reads=[xt7_b[i]], writes=[g1t_b[i]], tag="p7_gb2")
                    k.S(lambda s, i=i: s.activation(g1t[i][:], g1t[i][:], AF.Silu),
                        reads=[g1t_b[i]], writes=[g1t_b[i]], tag="p7_silu")
                else:
                    k.S(lambda s, i=i: s.activation(g1t[i][:], xt7[i][:], AF.Silu),
                        reads=[xt7_b[i]], writes=[g1t_b[i]], tag="p7_silu")
                k.SQ(lambda s, b=b, oc=oc, t0=t0, i=i: s.dma_start(
                        g1_d.ap()[b, oc * 128 : oc * 128 + 128, t0 : t0 + 512], g1t[i][:]),
                     reads=[g1t_b[i]], writes=[g1_d_b], tag="p7_stg1")
    k.release_arena()
    k.barrier()


def _build_p8(k, env):
    nc = k.nc
    ar = k.ar
    nch = NCH
    ps, psx, psb = env["ps"], env["psx"], env["psb"]
    onesb, onesb_b = env["onesb"], env["onesb_b"]
    identb, identb_b = env["identb"], env["identb_b"]
    xn1_d, xn1_d_b = env["xn1_d"], env["xn1_d_b"]
    x1_d, x1_d_b = env["x1_d"], env["x1_d_b"]
    g1_d, g1_d_b = env["g1_d"], env["g1_d_b"]
    yout, yout_b = env["yout"], env["yout_b"]
    tril_in = env["tril_in"]

    wqkv_in = k.din("wqkv", (D, 3 * D))
    bqkvp_in = k.din("bqkvp", (128, 24))
    wout_in = k.din("wout", (D, D))
    bout_in = k.din("bout", (128, nch))

    ar.mark()
    wqkv_sb = ar.alloc([128, nch, 3 * D], BF16, "wqkv_sb")
    wout_sb = ar.alloc([128, nch, D], BF16, "wout_sb")
    bqkvp = ar.alloc([128, 24], F32, "bqkvp")
    bout_sb = ar.alloc([128, nch], F32, "bout_sb")
    tril8 = ar.alloc([128, 128], BF16, "tril8")
    qsb = [ar.alloc([64, T], BF16, f"qsb{j}") for j in range(2)]
    ksb = [ar.alloc([64, T], BF16, f"ksb{j}") for j in range(2)]
    vfb = [ar.alloc([64, T], BF16, f"vfb{j}") for j in range(2)]
    vtm8 = [ar.alloc([128, 16, 64], BF16, f"vtm8{j}") for j in range(2)]
    xtl = [ar.alloc([128, nch, 128], BF16, f"xtl{j}") for j in range(2)]
    sail = [ar.alloc([128, nch, 128], BF16, f"sail{j}") for j in range(2)]
    s1t = [ar.alloc([128, nch, 128], F32, f"s1t{j}") for j in range(2)]
    x1t8 = [ar.alloc([128, nch, 128], F32, f"x1t8{j}") for j in range(2)]
    g1t8 = [ar.alloc([128, nch, 128], F32, f"g1t8{j}") for j in range(2)]
    p8sb = [ar.alloc([128, 1024], BF16, f"p8sb{i}") for i in range(2)]
    rz8 = ar.alloc([64, 1024], F32, "rz8")

    wqkv_b = k.buf("wqkv_sb")
    wout_b = k.buf("wout_sb")
    bqkvp_b = k.buf("bqkvp")
    bout_b = k.buf("bout_sb")
    tril8_b = k.buf("tril8")
    qsb_b = [k.buf(f"qsb{j}") for j in range(2)]
    ksb_b = [k.buf(f"ksb{j}") for j in range(2)]
    vfb_b = [k.buf(f"vfb{j}") for j in range(2)]
    vtm8_b = [k.buf(f"vtm8{j}") for j in range(2)]
    xtl_b = [k.buf(f"xtl{j}") for j in range(2)]
    sail_b = [k.buf(f"sail{j}") for j in range(2)]
    s1t_b = [k.buf(f"s1t{j}") for j in range(2)]
    x1t8_b = [k.buf(f"x1t8{j}") for j in range(2)]
    g1t8_b = [k.buf(f"g1t8{j}") for j in range(2)]
    p8sb_b = [k.buf(f"p8sb{i}") for i in range(2)]
    rz8_b = k.buf("rz8")

    k.GQ(lambda g: g.dma_start(wqkv_sb[:], wqkv_in.ap().rearrange("(c p) m -> p c m", p=128)),
         writes=[wqkv_b], tag="ld8_wqkv")
    k.GQ(lambda g: g.dma_start(wout_sb[:], wout_in.ap().rearrange("(c p) m -> p c m", p=128)),
         writes=[wout_b], tag="ld8_wout")
    k.SQ(lambda s: s.dma_start(bqkvp[:], bqkvp_in.ap()), writes=[bqkvp_b], tag="ld8_bq")
    k.SQ(lambda s: s.dma_start(bout_sb[:], bout_in.ap()), writes=[bout_b], tag="ld8_bo")
    k.GQ(lambda g: g.dma_start(tril8[:], tril_in.ap()), writes=[tril8_b], tag="ld8_tril")

    SCALE = (2.0 * D) ** -0.5

    for b in range(B_PER_CORE):
        for pair in range(8):
            blks = (2 * pair, 2 * pair + 1)
            for j, blk in enumerate(blks):
                k.SQ(lambda s, b=b, blk=blk, j=j: s.dma_start(
                        xtl[j][:], xn1_d.ap()[b].rearrange("(c p) t -> p c t", p=128)
                        [:, :, blk * 128 : blk * 128 + 128]),
                     reads=[xn1_d_b], writes=[xtl_b[j]], tag="p8_ldxn")
            for mp in range(24):
                o = mp // 8
                jg = (mp % 8) * 2
                dsts = (qsb, qsb_b) if o == 0 else (ksb, ksb_b) if o == 1 else (vfb, vfb_b)
                for j in range(2):
                    slot = (mp % 2) * 2 + j
                    for kc in range(nch):
                        k.T(lambda t, mp=mp, j=j, kc=kc, slot=slot: t.matmul(
                                ps[0][:, slot * 128 : slot * 128 + 128],
                                wqkv_sb[:, kc, mp * 128 : mp * 128 + 128],
                                xtl[j][:, kc, :],
                                start=(kc == 0), stop=(kc == nch - 1)),
                            reads=[wqkv_b, xtl_b[j]], writes=[psb[0]], tag="p8_qkv")
                    for half64 in range(2):
                        jj = jg + half64
                        k.V(lambda v, mp=mp, j=j, slot=slot, half64=half64, jj=jj,
                                   dstw=dsts[0]: v.tensor_scalar(
                                dstw[j][0:64, jj : T : 16],
                                ps[0][half64 * 64 : half64 * 64 + 64,
                                      slot * 128 : slot * 128 + 128],
                                bqkvp[half64 * 64 : half64 * 64 + 64, mp : mp + 1],
                                None, ALU.add),
                            reads=[psb[0]], writes=[dsts[1][j]], tag="p8_ev",
                            sreads=[bqkvp_b], ptrw=True)
            for j in range(2):
                for sg in range(16):
                    k.T(lambda t, j=j, sg=sg: t.transpose(
                            psx[1][:, sg * 64 : sg * 64 + 64],
                            vfb[j][0:64, sg * 128 : sg * 128 + 128], identb[0:64, 0:64]),
                        reads=[vfb_b[j], identb_b], writes=[psb[1]], tag="p8_vtr")
                k.V(lambda v, j=j: v.tensor_copy(
                        vtm8[j][:].rearrange("p a b -> p (a b)"), psx[1][:, 0:1024]),
                    reads=[psb[1]], writes=[vtm8_b[j]], tag="p8_vev")
            for j in range(2):
                for half in range(2):
                    h0 = half * 1024
                    nsig = 8 * (half + 1)
                    for sig in range(nsig):
                        vstart = max(sig * 128, h0)
                        nv = h0 + 1024 - vstart
                        rel = vstart - h0
                        pi = sig % 2
                        last = sig == nsig - 1
                        segs = ([(rel, 512 - rel), (512, 512)] if rel < 512
                                else [(rel, 1024 - rel)])
                        for o2, w in segs:
                            k.T(lambda t, j=j, sig=sig, pi=pi, o2=o2, w=w, h0=h0: t.matmul(
                                    ps[pi][:, o2 : o2 + w],
                                    ksb[j][0:64, sig * 128 : sig * 128 + 128],
                                    qsb[j][0:64, h0 + o2 : h0 + o2 + w],
                                    start=True, stop=True),
                                reads=[ksb_b[j], qsb_b[j]], writes=[psb[pi]], tag="p8_sc")
                        k.S(lambda s, pi=pi, rel=rel, nv=nv: s.activation(
                                p8sb[pi][:, rel : rel + nv], ps[pi][:, rel : rel + nv],
                                AF.Exp, scale=SCALE),
                            reads=[psb[pi]], writes=[p8sb_b[pi]], tag="p8_exp")
                        if sig * 128 >= h0:
                            k.V(lambda v, pi=pi, rel=rel: v.tensor_mul(
                                    p8sb[pi][:, rel : rel + 128],
                                    p8sb[pi][:, rel : rel + 128], tril8[:]),
                                reads=[tril8_b, p8sb_b[pi]], writes=[p8sb_b[pi]], tag="p8_mask")
                        for o2, w in segs:
                            k.T(lambda t, pi=pi, o2=o2, w=w, sig=sig, last=last: t.matmul(
                                    ps[2][:, o2 : o2 + w], onesb[:], p8sb[pi][:, o2 : o2 + w],
                                    start=(sig == 0), stop=last),
                                reads=[onesb_b, p8sb_b[pi]], writes=[psb[2]], tag="p8_z")
                        for o2, w in segs:
                            k.T(lambda t, j=j, sig=sig, pi=pi, o2=o2, w=w, last=last: t.matmul(
                                    ps[3][0:64, o2 : o2 + w], vtm8[j][:, sig, :],
                                    p8sb[pi][:, o2 : o2 + w],
                                    start=(sig == 0), stop=last),
                                reads=[vtm8_b[j], p8sb_b[pi]], writes=[psb[3]], tag="p8_ctx")
                    k.S(lambda s: s.activation(rz8[:], ps[2][0:64, :], AF.Ln),
                        reads=[psb[2]], writes=[rz8_b], tag="p8_lnz")
                    k.S(lambda s: s.activation(rz8[:], rz8[:], AF.Exp, scale=-1.0),
                        reads=[rz8_b], writes=[rz8_b], tag="p8_rz")
                    for jg in range(16):
                        k.V(lambda v, j=j, jg=jg, half=half: v.tensor_mul(
                                sail[j][(jg % 2) * 64 : (jg % 2) * 64 + 64, jg // 2,
                                        half * 64 : half * 64 + 64],
                                ps[3][0:64, jg : 1024 : 16], rz8[0:64, jg : 1024 : 16]),
                            reads=[psb[3], rz8_b], writes=[sail_b[j]], tag="p8_dei")
            for oc in range(nch):
                for j in range(2):
                    slot = (oc % 2) * 2 + j
                    for kc in range(nch):
                        k.T(lambda t, oc=oc, j=j, kc=kc, slot=slot: t.matmul(
                                ps[0][:, slot * 128 : slot * 128 + 128],
                                wout_sb[:, kc, oc * 128 : oc * 128 + 128],
                                sail[j][:, kc, :],
                                start=(kc == 0), stop=(kc == nch - 1)),
                            reads=[wout_b, sail_b[j]], writes=[psb[0]], tag="p8_wo")
                    k.V(lambda v, oc=oc, j=j, slot=slot: v.tensor_scalar(
                            s1t[j][:, oc, :],
                            ps[0][:, slot * 128 : slot * 128 + 128],
                            bout_sb[:, oc : oc + 1], None, ALU.add),
                        reads=[psb[0]], writes=[s1t_b[j]], tag="p8_woev",
                        sreads=[bout_b], ptrw=True)
            for j, blk in enumerate(blks):
                cols = (blk * 128, blk * 128 + 128)
                k.SQ(lambda s, b=b, cols=cols, j=j: s.dma_start(
                        x1t8[j][:], x1_d.ap()[b].rearrange("(c p) t -> p c t", p=128)
                        [:, :, cols[0] : cols[1]]),
                     reads=[x1_d_b], writes=[x1t8_b[j]], tag="p8_ldx1")
                k.SQ(lambda s, b=b, cols=cols, j=j: s.dma_start(
                        g1t8[j][:], g1_d.ap()[b].rearrange("(c p) t -> p c t", p=128)
                        [:, :, cols[0] : cols[1]]),
                     reads=[g1_d_b], writes=[g1t8_b[j]], tag="p8_ldg1")
                fl = lambda ap: ap.rearrange("p a b -> p (a b)")
                k.V(lambda v, j=j, fl=fl: v.tensor_mul(fl(s1t[j][:]), fl(s1t[j][:]), fl(g1t8[j][:])),
                    reads=[g1t8_b[j]], writes=[s1t_b[j]], tag="p8_sg")
                k.V(lambda v, j=j, fl=fl: v.tensor_add(fl(s1t[j][:]), fl(s1t[j][:]), fl(x1t8[j][:])),
                    reads=[x1t8_b[j]], writes=[s1t_b[j]], tag="p8_x2")
                k.SQ(lambda s, b=b, cols=cols, j=j: s.dma_start(
                        yout.ap()[b].rearrange("(c p) t -> p c t", p=128)[:, :, cols[0] : cols[1]],
                        s1t[j][:]),
                     reads=[s1t_b[j]], writes=[yout_b], tag="p8_sty")
    k.release_arena()



def _to_bf16(a):
    return a.astype(ml_dtypes.bfloat16)


def _pack_chan(v):
    return np.ascontiguousarray(np.asarray(v).reshape(-1, 128).T)


def _host_inputs(inputs, core):
    x = inputs["L0_attn_x"]
    d = {}
    xs = x[2 * core : 2 * core + 2]
    d["xfm"] = np.ascontiguousarray(xs.transpose(0, 2, 1))
    d["ones_bf"] = np.ones((128, 128), dtype=ml_dtypes.bfloat16)
    d["ident_bf"] = np.eye(128, dtype=np.float32).astype(ml_dtypes.bfloat16)
    d["tril_bf"] = np.triu(np.ones((128, 128), dtype=np.float32)).astype(ml_dtypes.bfloat16)
    d["wdq"] = np.ascontiguousarray(inputs["L0_attn_W_dq"])
    d["wuqT"] = np.ascontiguousarray(inputs["L0_attn_W_uq"].T)
    d["wuq"] = np.ascontiguousarray(inputs["L0_attn_W_uq"])
    d["wdkvT"] = np.ascontiguousarray(inputs["L0_attn_W_dkv"].T)
    d["wdqT"] = np.ascontiguousarray(inputs["L0_attn_W_dq"].T)
    d["wuk"] = np.ascontiguousarray(inputs["L0_attn_W_uk"])
    d["wuv"] = np.ascontiguousarray(inputs["L0_attn_W_uv"])
    d["woT"] = np.ascontiguousarray(inputs["L0_attn_W_o"].T)
    d["sigsrawT"] = np.ascontiguousarray(inputs["L0_ffn_signatures_raw"].T)
    d["knots"] = np.ascontiguousarray(inputs["L0_ffn_knot_values"])
    d["temp"] = np.ascontiguousarray(inputs["L0_ffn_temperature"])
    d["cw1"] = np.ascontiguousarray(inputs["L0_ffn_compress_w1"])
    d["cb1"] = _pack_chan(inputs["L0_ffn_compress_b1"])
    d["cw2"] = np.ascontiguousarray(inputs["L0_ffn_compress_w2"])
    d["cb2"] = np.ascontiguousarray(inputs["L0_ffn_compress_b2"])
    t65 = np.concatenate([
        inputs["L0_ffn_spline_w1"].reshape(64, 32),
        inputs["L0_ffn_spline_b1"].reshape(64, 16),
        inputs["L0_ffn_spline_w2"].reshape(64, 16),
        inputs["L0_ffn_spline_b2"].reshape(64, 1)], axis=1)
    d["tab65"] = np.ascontiguousarray(np.concatenate([t65, np.zeros_like(t65)], axis=0))
    d["oscale"] = np.ascontiguousarray(inputs["L0_ffn_output_scale"])
    d["dirs"] = np.ascontiguousarray(np.concatenate(
        [inputs["L0_ffn_directions"], np.zeros_like(inputs["L0_ffn_directions"])], axis=0))
    d["gamma0"] = _pack_chan(inputs["L0_ffn_norm_gamma"])
    d["beta0"] = _pack_chan(inputs["L0_ffn_norm_beta"])
    d["iota8"] = np.tile(np.arange(8, dtype=np.float32), (128, 1))
    d["iota8m"] = np.tile(np.arange(8, dtype=np.float32) - 8.0, (128, 1))
    d["iota64"] = np.tile(np.arange(64, dtype=np.float32), (128, 1))
    d["iotad8"] = np.tile((np.arange(64) // 8).astype(np.float32), (128, 1))
    d["iota64m"] = np.tile(np.arange(64, dtype=np.float32) - 64.0, (128, 1))
    d["qwT"] = np.ascontiguousarray(inputs["L1_ffn_q_w"].T)
    d["kwT"] = np.ascontiguousarray(inputs["L1_ffn_k_w"].T)
    d["vwT"] = np.ascontiguousarray(inputs["L1_ffn_v_w"].T)
    d["owT"] = np.ascontiguousarray(inputs["L1_ffn_o_w"].T)
    d["qb"] = _pack_chan(inputs["L1_ffn_q_b"])
    d["kb"] = _pack_chan(inputs["L1_ffn_k_b"])
    d["ob"] = _pack_chan(inputs["L1_ffn_o_b"])
    vbr = np.zeros((128, 1024), dtype=np.float32)
    vbr[0] = inputs["L1_ffn_v_b"]
    d["vbrow"] = vbr
    d["nw"] = _pack_chan(inputs["L1_ffn_norm_w"])
    d["nb"] = _pack_chan(inputs["L1_ffn_norm_b"])
    perm = np.empty(3072, dtype=np.int64)
    idx = 0
    for o in range(3):
        for jj in range(16):
            base = 192 * jj + 64 * o
            perm[idx : idx + 64] = np.arange(base, base + 64)
            idx += 64
    d["wqkv"] = np.ascontiguousarray(inputs["L1_attn_W_qkv"][:, perm])
    d["bqkvp"] = _pack_chan(inputs["L1_attn_b_qkv"][perm])
    d["wout"] = np.ascontiguousarray(inputs["L1_attn_W_out"])
    d["bout"] = _pack_chan(inputs["L1_attn_b_out"])
    return d


def run_dev(inputs, dumps=(), upto=99, cores=(0,)):
    nc = bass.Bass("TRN2", target_bir_lowering=False)
    build(nc, dumps=dumps, upto=upto)
    in_maps = [_host_inputs(inputs, c) for c in cores]
    res = run_bass_kernel_spmd(nc, in_maps, core_ids=list(range(len(cores))))
    return res.results


def kernel(**inputs):
    nc = bass.Bass("TRN2", target_bir_lowering=False)
    build(nc)
    in_maps = [_host_inputs(inputs, c) for c in range(NCORES)]
    res = run_bass_kernel_spmd(nc, in_maps, core_ids=list(range(NCORES)))
    out = np.empty((16, T, D), dtype=np.float32)
    for c in range(NCORES):
        y = res.results[c]["yout"]
        out[2 * c : 2 * c + 2] = y.transpose(0, 2, 1)
    return out


# version 04
# speedup vs baseline: 1.1107x; 1.1107x over previous
import numpy as np
import ml_dtypes
import concourse.bass as bass
import concourse.mybir as mybir
from concourse.bass_utils import run_bass_kernel_spmd

F32 = mybir.dt.float32
BF16 = mybir.dt.bfloat16
U8 = mybir.dt.uint8
AF = mybir.ActivationFunctionType
ALU = mybir.AluOpType

NCORES = 8
B_PER_CORE = 2
T = 2048
D = 1024
NCH = D // 128
TOK = B_PER_CORE * T
BLK = 512
EPS = 1e-5

ENGINES = ("T", "V", "S", "SQ", "GQ")
SERIALIZE_TAGS = ("am_", "sp_", "cal_")
_ENG_BLOCK = {"T": "tensor", "V": "vector", "S": "scalar", "SQ": "sync", "GQ": "gpsimd"}
_DMA_ENGS = ("SQ", "GQ")


def _prod(xs):
    p = 1
    for x in xs:
        p *= x
    return p


class Buf:
    __slots__ = ("name", "writer", "readers", "ptrw")

    def __init__(self, name=""):
        self.name = name
        self.writer = None
        self.readers = []
        self.ptrw = False


class Plan:
    R_LANES = 12

    def __init__(self):
        self.ops = {e: [] for e in ENGINES}
        self.nop_emit = {}

    def op(self, eng, emit, reads=(), writes=(), tag="", sreads=(), ptrw=False):
        waits = {}

        def need(dep, allow_self=False):
            if dep is None:
                return
            e2, i2 = dep
            if e2 == eng and not allow_self:
                return
            key = (e2, i2 % self.R_LANES) if e2 in _DMA_ENGS else e2
            if waits.get(key, -1) < i2:
                waits[key] = i2

        for b in sreads:
            need(b.writer, allow_self=True)
        reads = list(reads) + list(sreads)
        for b in reads:
            need(b.writer, allow_self=b.ptrw)
        for b in writes:
            need(b.writer)
            for r in b.readers:
                need(r)
        idx = len(self.ops[eng])
        if SERIALIZE_TAGS and eng == "V" and idx > 0 and any(tag.startswith(p) for p in SERIALIZE_TAGS):
            if waits.get(eng, -1) < idx - 1:
                waits[eng] = idx - 1
        if waits.get(eng, -1) >= idx:
            del waits[eng]
        return self._append(eng, emit, waits, tag, reads, writes, ptrw)

    def _op(self, eng, emit, waits, tag=""):
        self.ops[eng].append({"emit": emit, "waits": waits, "tag": tag})
        return (eng, len(self.ops[eng]) - 1)

    def _append(self, eng, emit, waits, tag, reads, writes, ptrw):
        idx = len(self.ops[eng])
        self.ops[eng].append({"emit": emit, "waits": waits, "tag": tag})
        me = (eng, idx)
        for b in reads:
            b.readers.append(me)
        for b in writes:
            b.writer = me
            b.readers = []
            b.ptrw = ptrw
        return me

    def last_ops(self):
        out = []
        for e in ENGINES:
            n = len(self.ops[e])
            if not n:
                continue
            if e in _DMA_ENGS:
                for i in range(max(0, n - self.R_LANES), n):
                    out.append((e, i))
            else:
                out.append((e, n - 1))
        return out

    def finalize(self, nc):
        R = self.R_LANES
        targets = {e: set() for e in ENGINES}

        def tgt_of(key, i2):
            e2 = key if isinstance(key, str) else key[0]
            return e2, i2

        for e in ENGINES:
            for o in self.ops[e]:
                for key, i2 in o["waits"].items():
                    e2, i2 = tgt_of(key, i2)
                    targets[e2].add(i2)
        for e, i in self.last_ops():
            targets[e].add(i)
        for e in _DMA_ENGS:
            targets[e].update(range(len(self.ops[e])))
        inc_points = {e: sorted(targets[e]) for e in ENGINES}
        val_of = {}
        for e in ENGINES:
            vals = {}
            if e in _DMA_ENGS:
                for i in range(len(self.ops[e])):
                    vals[i] = (i // R + 1) * 16
            else:
                rank = 0
                pts = set(inc_points[e])
                for i in range(len(self.ops[e])):
                    if i in pts:
                        rank += 1
                    vals[i] = rank
            val_of[e] = vals

        sems = {}
        for e in ENGINES:
            if e in _DMA_ENGS:
                sems[e] = [nc.alloc_semaphore(f"sem_{e}_{j}") for j in range(R)]
            else:
                sems[e] = nc.alloc_semaphore(f"sem_{e}")

        def sem_and_val(e2, i2):
            if e2 in _DMA_ENGS:
                return sems[e2][i2 % R], val_of[e2][i2]
            return sems[e2], val_of[e2][i2]

        plan = self

        def body_for(e):
            def body(engine):
                waited = {}
                pts = set(inc_points[e])
                for i, o in enumerate(plan.ops[e]):
                    for key, i2 in sorted(o["waits"].items(), key=str):
                        e2 = key if isinstance(key, str) else key[0]
                        sem, v = sem_and_val(e2, i2)
                        wkey = (e2, i2 % R) if e2 in _DMA_ENGS else e2
                        if v > waited.get(wkey, 0):
                            engine.wait_ge(sem, v)
                            waited[wkey] = v
                    inst = o["emit"](engine)
                    if i in pts:
                        if inst is None:
                            raise RuntimeError(f"op {e}:{i} tag={o['tag']} returned None")
                        if e in _DMA_ENGS:
                            inst.then_inc(sems[e][i % R], 16)
                        else:
                            inst.then_inc(sems[e], 1)
                if e == "SQ":
                    for e2, i2 in plan.last_ops():
                        if e2 == "SQ" and i2 == len(plan.ops["SQ"]) - 1:
                            continue
                        sem, v = sem_and_val(e2, i2)
                        wkey = (e2, i2 % R) if e2 in _DMA_ENGS else e2
                        if v > waited.get(wkey, 0):
                            engine.wait_ge(sem, v)
                            waited[wkey] = v
            return body

        with nc.Block() as block:
            for e in ENGINES:
                if not self.ops[e] and e != "SQ":
                    continue
                getattr(block, _ENG_BLOCK[e])(body_for(e))


class Arena:
    def __init__(self, nc, nbytes=200 * 1024):
        self.pool = nc.alloc_sbuf_tensor("pool", [128, nbytes], U8)
        self.cap = nbytes
        self.off = 0
        self.stack = []
        self.hi = 0

    def alloc(self, shape, dt, name=""):
        isz = {F32: 4, BF16: 2, U8: 1}[dt]
        n = _prod(shape[1:]) * isz
        off = (self.off + 31) & ~31
        assert off + n <= self.cap, f"SBUF arena overflow: {name} needs {n}B at {off}"
        self.off = off + n
        self.hi = max(self.hi, self.off)
        ap = self.pool.ap()[: shape[0], off : off + n].bitcast(dt)
        if len(shape) > 2:
            dims = " ".join(f"d{i}" for i in range(1, len(shape)))
            kw = {f"d{i}": shape[i] for i in range(1, len(shape) - 1)}
            ap = ap.rearrange(f"p ({dims}) -> p {dims}", **kw)
        return ap

    def mark(self):
        self.stack.append(self.off)

    def release(self):
        self.off = self.stack.pop()


class K:
    def __init__(self, nc, dumps=()):
        self.nc = nc
        self.p = Plan()
        self.ar = Arena(nc)
        self.dumps = set(dumps)
        self.dump_names = []
        self.inp = {}
        self.hazard = []

    def buf(self, name=""):
        b = Buf(name)
        b.readers = list(self.hazard)
        return b

    def release_arena(self):
        self.ar.release()
        self.hazard = self.p.last_ops()

    def barrier(self):
        p = self.p
        join_waits = {}
        for e, i in p.last_ops():
            key = (e, i % p.R_LANES) if e in _DMA_ENGS else e
            join_waits[key] = max(join_waits.get(key, -1), i)
        ji = p._op("SQ", lambda s: s.nop(), join_waits, tag="bar_join")
        for e in ENGINES:
            if e == "SQ" or not p.ops[e]:
                continue
            p._op(e, lambda eng: eng.nop(), {"SQ": ji[1]}, tag="bar_leaf")
        self.hazard = p.last_ops()

    def din(self, name, shape, dt=F32):
        t = self.nc.dram_tensor(name, list(shape), dt, kind="ExternalInput")
        self.inp[name] = t
        return t

    def dscr(self, name, shape, dt=F32):
        kind = "ExternalOutput" if name in self.dumps else "Internal"
        if kind == "ExternalOutput":
            self.dump_names.append(name)
        return self.nc.dram_tensor(name, list(shape), dt, kind=kind)

    def dump(self, name, ap, reads, shape, dt=F32):
        if name not in self.dumps:
            return
        t = self.nc.dram_tensor("dump_" + name, list(shape), dt, kind="ExternalOutput")
        self.dump_names.append("dump_" + name)
        self.p.op("SQ", lambda s, t=t, ap=ap: s.dma_start(t.ap(), ap), reads=list(reads), tag="dump_" + name)

    def T(self, emit, reads=(), writes=(), tag=""):
        return self.p.op("T", emit, reads, writes, tag)

    def V(self, emit, reads=(), writes=(), tag="", sreads=(), ptrw=False):
        return self.p.op("V", emit, reads, writes, tag, sreads, ptrw)

    def S(self, emit, reads=(), writes=(), tag="", sreads=(), ptrw=False):
        return self.p.op("S", emit, reads, writes, tag, sreads, ptrw)

    def SQ(self, emit, reads=(), writes=(), tag=""):
        return self.p.op("SQ", emit, reads, writes, tag)

    def GQ(self, emit, reads=(), writes=(), tag=""):
        return self.p.op("GQ", emit, reads, writes, tag)



def build(nc, dumps=(), upto=99):
    k = K(nc, dumps)
    ar = k.ar
    nch = NCH

    xfm = k.din("xfm", (B_PER_CORE, D, T))
    xfm_b = Buf("xfm")
    ones_in = k.din("ones_bf", (128, 128), BF16)
    ident_in = k.din("ident_bf", (128, 128), BF16)

    wdq = k.din("wdq", (128, D))
    wuqT = k.din("wuqT", (128, D))
    wuq = k.din("wuq", (D, 128))
    wdkvT = k.din("wdkvT", (D, 128))
    wdqT = k.din("wdqT", (D, 128))
    wuk = k.din("wuk", (D, 128))
    wuv = k.din("wuv", (D, 128))
    woT = k.din("woT", (D, D))

    yout = nc.dram_tensor("yout", [B_PER_CORE, D, T], F32, kind="ExternalOutput")
    yout_b = Buf("yout")

    onesb = ar.alloc([128, 128], BF16, "onesb")
    identb = ar.alloc([128, 128], BF16, "identb")
    epsc = ar.alloc([128, 1], F32, "epsc")
    epsc_b = Buf("epsc")
    junk = ar.alloc([128, 8], F32, "junk")
    ar.mark()
    ckv = ar.alloc([128, TOK], BF16, "ckv")
    q1 = ar.alloc([128, TOK], BF16, "q1")
    ckv_tm = ar.alloc([128, 32, 128], BF16, "ckv_tm")
    bq = ar.alloc([128, 16, 128], BF16, "bq")
    v2 = ar.alloc([128, D], BF16, "v2")
    ohs_fm = ar.alloc([64, TOK], BF16, "ohs_fm")
    ohs_fm_b = Buf("ohs_fm")
    ar.mark()
    xn0 = ar.alloc([128, nch, TOK], BF16, "xn0")
    xn0_b = Buf("xn0")
    onesb_b = Buf("onesb")
    identb_b = Buf("identb")
    ckv_b = Buf("ckv")
    q1_b = Buf("q1")
    ckv_tm_b = Buf("ckv_tm")
    bq_b = Buf("bq")
    v2_b = Buf("v2")

    psA = nc.alloc_psum_tensor("psA", [128, 2048], F32)
    psB = nc.alloc_psum_tensor("psB", [128, 1024], F32)
    psC = nc.alloc_psum_tensor("psC", [128, 1024], F32)
    ps = [psA.ap()[:, 0:1024], psA.ap()[:, 1024:2048], psB.ap(), psC.ap()]
    psx = [psA.ap().bitcast(BF16)[:, 0:2048], psA.ap().bitcast(BF16)[:, 2048:4096],
           psB.ap().bitcast(BF16), psC.ap().bitcast(BF16)]
    psAf = psA.ap()
    psb = [Buf(f"ps{i}") for i in range(4)]

    k.GQ(lambda g: g.dma_start(onesb[:], ones_in.ap()), writes=[onesb_b], tag="ld_ones")
    k.GQ(lambda g: g.dma_start(identb[:], ident_in.ap()), writes=[identb_b], tag="ld_ident")

    k.V(lambda v: v.memset(epsc[:], EPS), writes=[epsc_b], tag="epsc")
    k.p.nop_emit["V"] = lambda v: v.memset(junk[:], 0.0)
    k.p.nop_emit["S"] = lambda s: s.memzero(junk[:])

    def wload(name, t, shape, src_ap=None, tag=""):
        sb = ar.alloc(shape, BF16, name)
        b = k.buf(name)
        src = src_ap if src_ap is not None else t.ap()
        k.GQ(lambda g, sb=sb, src=src: g.dma_start(sb[:], src), writes=[b], tag=tag or ("ld_" + name))
        return sb, b

    def chunk_rows(t, m):
        return t.ap().rearrange("(c p) m -> p c m", p=128)

    wdkvT_sb, wdkvT_sbb = wload("wdkvT_sb", wdkvT, [128, nch, 128], chunk_rows(wdkvT, 128))
    wdqT_sb, wdqT_sbb = wload("wdqT_sb", wdqT, [128, nch, 128], chunk_rows(wdqT, 128))

    ar.mark()
    wdq_sb, wdq_sbb = wload("wdq_sb", wdq, [128, D])
    wuqT_sb, wuqT_sbb = wload("wuqT_sb", wuqT, [128, D])
    wuq_sb, wuq_sbb = wload("wuq_sb", wuq, [128, nch, 128], chunk_rows(wuq, 128))
    wuk_sb, wuk_sbb = wload("wuk_sb", wuk, [128, nch, 128], chunk_rows(wuk, 128))
    wuv_sb, wuv_sbb = wload("wuv_sb", wuv, [128, nch, 128], chunk_rows(wuv, 128))
    woT_sb, woT_sbb = wload("woT_sb", woT, [128, nch, D], chunk_rows(woT, D))

    d1t = ar.alloc([128, nch, D], BF16, "d1t")
    d1t_b = k.buf("d1t")
    for jc in range(nch):
        for n2 in range(2):
            k.T(lambda t, jc=jc, n2=n2: t.matmul(
                    ps[0][:, n2 * 512 : n2 * 512 + 512],
                    wuqT_sb[:, jc * 128 : jc * 128 + 128],
                    wdq_sb[:, n2 * 512 : n2 * 512 + 512],
                    start=True, stop=True),
                reads=[wuqT_sbb, wdq_sbb], writes=[psb[0]], tag="d1t_mm")
        k.V(lambda v, jc=jc: v.tensor_copy(d1t[:, jc, :], ps[0][:, 0:1024]),
            reads=[psb[0]], writes=[d1t_b], tag="d1t_ev")

    a_sb = ar.alloc([128, nch, 128], BF16, "a_sb")
    a_b = k.buf("a_sb")
    for dc in range(nch):
        for jc in range(nch):
            k.T(lambda t, dc=dc, jc=jc: t.matmul(
                    ps[1][:, 0:128],
                    d1t[:, jc, dc * 128 : dc * 128 + 128],
                    wuk_sb[:, jc, :],
                    start=(jc == 0), stop=(jc == nch - 1)),
                reads=[d1t_b, wuk_sbb], writes=[psb[1]], tag="a_mm")
        k.V(lambda v, dc=dc: v.tensor_copy(a_sb[:, dc, :], ps[1][:, 0:128]),
            reads=[psb[1]], writes=[a_b], tag="a_ev")

    for h in range(16):
        po = (h % 2) * 64
        c = h // 2
        k.T(lambda t, h=h, po=po, c=c: t.matmul(
                ps[2][:, 0:128],
                wuq_sb[po : po + 64, c, :],
                a_sb[po : po + 64, c, :],
                start=True, stop=True),
            reads=[wuq_sbb, a_b], writes=[psb[2]], tag="bq_mm")
        k.V(lambda v, h=h: v.tensor_copy(bq[:, h, :], ps[2][:, 0:128]),
            reads=[psb[2]], writes=[bq_b], tag="bq_ev")

    for n2 in range(2):
        for uc in range(nch):
            k.T(lambda t, n2=n2, uc=uc: t.matmul(
                    ps[3][:, 0:512],
                    wuv_sb[:, uc, :],
                    woT_sb[:, uc, n2 * 512 : n2 * 512 + 512],
                    start=(uc == 0), stop=(uc == nch - 1)),
                reads=[wuv_sbb, woT_sbb], writes=[psb[3]], tag="v2_mm")
        k.V(lambda v, n2=n2: v.tensor_copy(v2[:, n2 * 512 : n2 * 512 + 512], ps[3][:, 0:512]),
            reads=[psb[3]], writes=[v2_b], tag="v2_ev")

    k.dump("d1t", d1t[:], [d1t_b], (128, nch, D), BF16)
    k.dump("a", a_sb[:], [a_b], (128, nch, 128), BF16)
    k.dump("bq", bq[:], [bq_b], (128, 16, 128), BF16)
    k.dump("v2", v2[:], [v2_b], (128, D), BF16)
    k.release_arena()
    k.barrier()

    mur2_d = k.dscr("mur2_d", (2, TOK // BLK, 128, BLK), F32)
    mur2_d_b = Buf("mur2_d")
    nblk = TOK // BLK
    ar.mark()
    xt = [ar.alloc([128, nch, BLK], F32, f"xt{i}") for i in range(2)]
    xbt = [ar.alloc([128, nch, BLK], BF16, f"xbt{i}") for i in range(2)]
    sqt = ar.alloc([128, nch, BLK], BF16, "sqt")
    tmp512 = ar.alloc([128, BLK], F32, "tmp512")
    mut = [ar.alloc([128, BLK], F32, f"mut{i}") for i in range(2)]
    rrt = [ar.alloc([128, BLK], F32, f"rrt{i}") for i in range(2)]
    xt_b = [k.buf(f"xt{i}") for i in range(2)]
    xbt_b = [k.buf(f"xbt{i}") for i in range(2)]
    sqt_b = k.buf("sqt")
    tmp512_b = k.buf("tmp512")
    mut_b = [k.buf(f"mut{i}") for i in range(2)]
    rrt_b = [k.buf(f"rrt{i}") for i in range(2)]

    for blk in range(nblk):
        b, t0 = divmod(blk * BLK, T)
        g = blk * BLK
        i = blk % 2
        src = xfm.ap()[b].rearrange("(c p) t -> p c t", p=128)[:, :, t0 : t0 + BLK]
        k.SQ(lambda s, src=src, i=i: s.dma_start(xt[i][:], src),
             reads=[xfm_b], writes=[xt_b[i]], tag="p1_ldx")
        k.GQ(lambda gq, src=src, i=i: gq.dma_start(xbt[i][:], src),
             reads=[xfm_b], writes=[xbt_b[i]], tag="p1_ldxb")
        k.V(lambda v, i=i: v.tensor_mul(sqt[:].rearrange("p c t -> p (c t)"),
                                        xbt[i][:].rearrange("p c t -> p (c t)"),
                                        xbt[i][:].rearrange("p c t -> p (c t)")),
            reads=[xbt_b[i]], writes=[sqt_b], tag="p1_sq")
        pst = (blk % 2) * 2
        for c in range(nch):
            k.T(lambda t, c=c, i=i, pst=pst: t.matmul(ps[pst][:, 0:BLK], onesb[:], xbt[i][:, c, :],
                                             start=(c == 0), stop=(c == nch - 1)),
                reads=[xbt_b[i], onesb_b], writes=[psb[pst]], tag="p1_mmS")
        for c in range(nch):
            k.T(lambda t, c=c, pst=pst: t.matmul(ps[pst + 1][:, 0:BLK], onesb[:], sqt[:, c, :],
                                        start=(c == 0), stop=(c == nch - 1)),
                reads=[sqt_b, onesb_b], writes=[psb[pst + 1]], tag="p1_mmS2")
        mu_ap = mut[i][:]
        r_ap = rrt[i][:]
        k.V(lambda v, mu_ap=mu_ap, pst=pst: v.tensor_scalar_mul(mu_ap, ps[pst][:, 0:BLK], 1.0 / D),
            reads=[psb[pst]], writes=[mut_b[i]], tag="p1_mu")
        k.V(lambda v, mu_ap=mu_ap: v.tensor_mul(tmp512[:], mu_ap, mu_ap),
            reads=[mut_b[i]], writes=[tmp512_b], tag="p1_musq")
        k.V(lambda v, r_ap=r_ap, pst=pst: v.scalar_tensor_tensor(r_ap, ps[pst + 1][:, 0:BLK], 1.0 / D,
                                                        tmp512[:], ALU.mult, ALU.subtract),
            reads=[psb[pst + 1], tmp512_b], writes=[rrt_b[i]], tag="p1_var")
        k.S(lambda s, r_ap=r_ap: s.activation(r_ap, r_ap, AF.Ln, bias=epsc[:]),
            reads=[rrt_b[i], epsc_b], writes=[rrt_b[i]], tag="p1_ln")
        k.S(lambda s, r_ap=r_ap: s.activation(r_ap, r_ap, AF.Exp, scale=-0.5),
            reads=[rrt_b[i]], writes=[rrt_b[i]], tag="p1_exp")
        k.SQ(lambda s, blk=blk, i=i: s.dma_start(mur2_d.ap()[0, blk], mut[i][:, :]),
             reads=[mut_b[i]], writes=[mur2_d_b], tag="p1_stmu")
        k.SQ(lambda s, blk=blk, i=i: s.dma_start(mur2_d.ap()[1, blk], rrt[i][:, :]),
             reads=[rrt_b[i]], writes=[mur2_d_b], tag="p1_str")
        for c in range(nch):
            k.V(lambda v, c=c, i=i, mu_ap=mu_ap: v.tensor_sub(xt[i][:, c, :], xt[i][:, c, :], mu_ap),
                reads=[mut_b[i]], writes=[xt_b[i]], tag="p1_sub")
            k.V(lambda v, c=c, i=i, g=g, r_ap=r_ap: v.tensor_mul(
                    xn0[:, c, g : g + BLK], xt[i][:, c, :], r_ap),
                reads=[xt_b[i], rrt_b[i]], writes=[xn0_b], tag="p1_xn")

    k.release_arena()
    k.barrier()

    k.dump("xn0", xn0[:], [xn0_b], (128, nch, TOK), BF16)

    if upto <= 1:
        k.p.finalize(nc)
        return k

    for nt in range(TOK // 512):
        pa = (nt % 2) * 2
        pb2 = (nt % 2) * 2 + 1
        for c in range(nch):
            k.T(lambda t, nt=nt, c=c, pa=pa: t.matmul(
                    ps[pa][:, 0:512], wdkvT_sb[:, c, :], xn0[:, c, nt * 512 : nt * 512 + 512],
                    start=(c == 0), stop=(c == nch - 1)),
                reads=[wdkvT_sbb, xn0_b], writes=[psb[pa]], tag="ckv_mm")
        k.V(lambda v, nt=nt, pa=pa: v.tensor_copy(ckv[:, nt * 512 : nt * 512 + 512], ps[pa][:, 0:512]),
            reads=[psb[pa]], writes=[ckv_b], tag="ckv_ev")
        for c in range(nch):
            k.T(lambda t, nt=nt, c=c, pb2=pb2: t.matmul(
                    ps[pb2][:, 0:512], wdqT_sb[:, c, :], xn0[:, c, nt * 512 : nt * 512 + 512],
                    start=(c == 0), stop=(c == nch - 1)),
                reads=[wdqT_sbb, xn0_b], writes=[psb[pb2]], tag="q1_mm")
        k.V(lambda v, nt=nt, pb2=pb2: v.tensor_copy(q1[:, nt * 512 : nt * 512 + 512], ps[pb2][:, 0:512]),
            reads=[psb[pb2]], writes=[q1_b], tag="q1_ev")

    for q4 in range(8):
        pq = 2 + (q4 % 2)
        for j in range(4):
            st = q4 * 4 + j
            k.T(lambda t, st=st, j=j, pq=pq: t.transpose(
                    psx[pq][:, j * 128 : j * 128 + 128],
                    ckv[:, st * 128 : st * 128 + 128], identb[:]),
                reads=[ckv_b, identb_b], writes=[psb[pq]], tag="ckvtm_tr")
        k.V(lambda v, q4=q4, pq=pq: v.tensor_copy(
                ckv_tm[:].rearrange("p a b -> p (a b)")[:, q4 * 512 : q4 * 512 + 512],
                psx[pq][:, 0:512]),
            reads=[psb[pq]], writes=[ckv_tm_b], tag="ckvtm_ev")

    k.dump("ckv", ckv[:], [ckv_b], (128, TOK), BF16)
    k.dump("q1", q1[:], [q1_b], (128, TOK), BF16)
    k.dump("ckv_tm", ckv_tm[:], [ckv_tm_b], (128, 32, 128), BF16)

    if upto <= 2:
        k.p.finalize(nc)
        return k

    sigsrawT = k.din("sigsrawT", (D, 64))
    knots_in = k.din("knots", (8,))
    temp_in = k.din("temp", (1,))
    cw1_in = k.din("cw1", (D, 256))
    cb1_in = k.din("cb1", (128, 2))
    cw2_in = k.din("cw2", (256, 2))
    cb2_in = k.din("cb2", (2,))
    tab65_in = k.din("tab65", (128, 65))
    osc_in = k.din("oscale", (1,))
    iota8_in = k.din("iota8", (128, 8))
    i8m_in = k.din("iota8m", (128, 8))
    iota64_in = k.din("iota64", (128, 64))
    iotad8_in = k.din("iotad8", (128, 64))
    i64m_in = k.din("iota64m", (128, 64))

    ar.mark()
    srw = ar.alloc([128, nch, 64], F32, "srw")
    m72 = ar.alloc([128, nch, 72], BF16, "m72")
    qtmp = ar.alloc([128, nch, 64], F32, "qtmp")
    a1 = ar.alloc([128, nch, 32], F32, "a1")
    a2 = ar.alloc([128, nch, 16], F32, "a2")
    a3 = ar.alloc([128, nch, 8], F32, "a3")
    scs = ar.alloc([128, 32, 72], F32, "scs")
    gacc = ar.alloc([128, 32 * 72], F32, "gacc")
    camp = ar.alloc([128, 32 * 72], F32, "camp")
    gath = gacc[:, 0 : 32 * 65].rearrange("p (a b) -> p a b", a=32)
    h1sb = ar.alloc([128, 2, TOK], BF16, "h1sb")
    oh01_tm = ar.alloc([128, 32, 64], BF16, "oh01_tm")
    oh01_fm = ar.alloc([64, TOK], BF16, "oh01_fm")
    ohm_tm = ar.alloc([128, 64], BF16, "ohm_tm")
    cw1_sb = ar.alloc([128, nch, 256], BF16, "cw1_sb")
    cb1_sb = ar.alloc([128, 2], F32, "cb1_sb")
    cw2_sb = ar.alloc([128, 2, 2], BF16, "cw2_sb")
    cb2_sb = ar.alloc([128, 2], F32, "cb2_sb")
    tab65_sb = ar.alloc([128, 65], BF16, "tab65_sb")
    kb = ar.alloc([128, 8], F32, "kb")
    dk = ar.alloc([128, 7], F32, "dk")
    rt = ar.alloc([128, 1], F32, "rt")
    oscb = ar.alloc([128, 1], F32, "oscb")
    iota8c = ar.alloc([128, 8], F32, "iota8c")
    i8mc = ar.alloc([128, 8], F32, "i8mc")
    iota64c = ar.alloc([128, 64], F32, "iota64c")
    iotad8c = ar.alloc([128, 64], F32, "iotad8c")
    i64mc = ar.alloc([128, 64], F32, "i64mc")
    u_sb = ar.alloc([128, 512], F32, "u_sb")
    e_sb = ar.alloc([128, 512], F32, "e_sb")
    compsb = ar.alloc([128, 64], F32, "compsb")
    sm1 = ar.alloc([128, 64], F32, "sm1")
    sm2 = ar.alloc([128, 64], F32, "sm2")
    sm3 = ar.alloc([128, 16], F32, "sm3")
    clv = ar.alloc([128, 1], F32, "clv")
    mgv = ar.alloc([128, 1], F32, "mgv")

    srw_b = k.buf("srw"); m72_b = k.buf("m72"); qtmp_b = k.buf("qtmp")
    a1_b = k.buf("a1"); a2_b = k.buf("a2"); a3_b = k.buf("a3")
    scs_b = k.buf("scs"); gacc_b = k.buf("gacc"); camp_b = k.buf("camp")
    gath_b = gacc_b; cacc_b = gacc_b
    h1_b = k.buf("h1sb")
    oh01_tm_b = k.buf("oh01_tm"); oh01_fm_b = k.buf("oh01_fm"); ohm_tm_b = k.buf("ohm_tm")
    cw1_b = k.buf("cw1_sb"); cb1_b = k.buf("cb1_sb"); cw2_b = k.buf("cw2_sb")
    cb2_b = k.buf("cb2_sb"); tab65_b = k.buf("tab65_sb")
    kb_b = k.buf("kb"); dk_b = k.buf("dk"); rt_b = k.buf("rt"); oscb_b = k.buf("oscb")
    iota8c_b = k.buf("iota8c"); i8mc_b = k.buf("i8mc"); iota64c_b = k.buf("iota64c")
    iotad8c_b = k.buf("iotad8c"); i64mc_b = k.buf("i64mc")
    u_b = k.buf("u_sb"); e_b = k.buf("e_sb"); comp_b = k.buf("compsb")
    sm1_b = k.buf("sm1"); sm2_b = k.buf("sm2"); sm3_b = k.buf("sm3")
    clv_b = k.buf("clv"); mgv_b = k.buf("mgv")

    k.SQ(lambda s: s.dma_start(srw[:], sigsrawT.ap().rearrange("(c p) m -> p c m", p=128)),
         writes=[srw_b], tag="ld_srw")
    k.SQ(lambda s: s.dma_start(kb[:], knots_in.ap().unsqueeze(0).partition_broadcast(128)),
         writes=[kb_b], tag="ld_knots")
    k.SQ(lambda s: s.dma_start(rt[:], temp_in.ap().unsqueeze(0).partition_broadcast(128)),
         writes=[rt_b], tag="ld_temp")
    k.SQ(lambda s: s.dma_start(oscb[:], osc_in.ap().unsqueeze(0).partition_broadcast(128)),
         writes=[oscb_b], tag="ld_osc")
    k.GQ(lambda g: g.dma_start(cw1_sb[:], cw1_in.ap().rearrange("(c p) m -> p c m", p=128)),
         writes=[cw1_b], tag="ld_cw1")
    k.SQ(lambda s: s.dma_start(cb1_sb[:], cb1_in.ap()), writes=[cb1_b], tag="ld_cb1")
    k.GQ(lambda g: g.dma_start(cw2_sb[:], cw2_in.ap().rearrange("(c p) m -> p c m", p=128)),
         writes=[cw2_b], tag="ld_cw2")
    k.SQ(lambda s: s.dma_start(cb2_sb[:], cb2_in.ap().unsqueeze(0).partition_broadcast(128)),
         writes=[cb2_b], tag="ld_cb2")
    k.GQ(lambda g: g.dma_start(tab65_sb[:], tab65_in.ap()), writes=[tab65_b], tag="ld_tab")
    for nm, sb, bb, t_in in (("i8", iota8c, iota8c_b, iota8_in), ("i8m", i8mc, i8mc_b, i8m_in),
                             ("i64", iota64c, iota64c_b, iota64_in),
                             ("id8", iotad8c, iotad8c_b, iotad8_in), ("i64m", i64mc, i64mc_b, i64m_in)):
        k.SQ(lambda s, sb=sb, t_in=t_in: s.dma_start(sb[:], t_in.ap()), writes=[bb], tag="ld_" + nm)

    k.dump("dbg_i8m", i8mc[:], [i8mc_b], (128, 8), F32)
    k.dump("dbg_i64", iota64c[:], [iota64c_b], (128, 64), F32)
    k.dump("dbg_id8", iotad8c[:], [iotad8c_b], (128, 64), F32)
    fl = lambda ap: ap.rearrange("p c m -> p (c m)")
    k.V(lambda v: v.tensor_single_scalar(qtmp[:], srw[:], 0.3, ALU.is_gt),
        reads=[srw_b], writes=[qtmp_b], tag="q_gt")
    k.V(lambda v: v.tensor_single_scalar(srw[:], srw[:], -0.3, ALU.is_lt),
        reads=[srw_b], writes=[srw_b], tag="q_lt")
    k.V(lambda v: v.tensor_sub(m72[:, :, 8:72], qtmp[:], srw[:]),
        reads=[qtmp_b, srw_b], writes=[m72_b], tag="q_sub")
    k.V(lambda v: v.tensor_add(a1[:], m72[:, :, 8:72:2], m72[:, :, 9:72:2]),
        reads=[m72_b], writes=[a1_b], tag="cs_t1")
    k.V(lambda v: v.tensor_add(a2[:], a1[:, :, 0:32:2], a1[:, :, 1:32:2]),
        reads=[a1_b], writes=[a2_b], tag="cs_t2")
    k.V(lambda v: v.tensor_add(a3[:], a2[:, :, 0:16:2], a2[:, :, 1:16:2]),
        reads=[a2_b], writes=[a3_b], tag="cs_t3")
    k.V(lambda v: v.tensor_single_scalar(a1[:, :, 0:8], a3[:], 0.0, ALU.is_gt),
        reads=[a3_b], writes=[a1_b], tag="cs_gt")
    k.V(lambda v: v.tensor_single_scalar(a2[:, :, 0:8], a3[:], 0.0, ALU.is_lt),
        reads=[a3_b], writes=[a2_b], tag="cs_lt")
    k.V(lambda v: v.tensor_sub(m72[:, :, 0:8], a1[:, :, 0:8], a2[:, :, 0:8]),
        reads=[a1_b, a2_b], writes=[m72_b], tag="cs_sub")

    k.dump("m72", m72[:], [m72_b], (128, nch, 72), BF16)
    for tt in range(32):
        pt = tt % 4
        for c in range(nch):
            k.T(lambda t, tt=tt, c=c, pt=pt: t.matmul(
                    ps[pt][:, 0:72], xn0[:, c, tt * 128 : tt * 128 + 128], m72[:, c, :],
                    start=(c == 0), stop=(c == nch - 1)),
                reads=[xn0_b, m72_b], writes=[psb[pt]], tag="fs_mm")
        k.V(lambda v, tt=tt, pt=pt: v.tensor_copy(scs[:, tt, :], ps[pt][:, 0:72]),
            reads=[psb[pt]], writes=[scs_b], tag="fs_ev")

    k.dump("scs_raw", scs[:], [scs_b], (128, 32, 72), F32)
    k.V(lambda v: v.tensor_single_scalar(rt[:], rt[:], 1e-6, ALU.add),
        reads=[rt_b], writes=[rt_b], tag="cal_t1")
    k.V(lambda v: v.reciprocal(rt[:], rt[:]), reads=[rt_b], writes=[rt_b], tag="cal_t2")
    k.V(lambda v: v.tensor_sub(dk[:], kb[:, 1:8], kb[:, 0:7]), reads=[kb_b], writes=[dk_b], tag="cal_dk")
    scf = scs[:].rearrange("p a b -> p (a b)")
    k.S(lambda s: s.activation(scf, scf, AF.Sigmoid, scale=rt[:]),
        reads=[scs_b, rt_b], writes=[scs_b], tag="cal_sig")
    cacc = gacc
    for j in range(7):
        k.V(lambda v, j=j: v.tensor_scalar(camp[:], scf, 7.0, float(-j), ALU.mult, ALU.add),
            reads=[scs_b], writes=[camp_b], tag="cal_a")
        k.V(lambda v: v.tensor_scalar(camp[:], camp[:], 0.0, 1.0, ALU.max, ALU.min),
            reads=[camp_b], writes=[camp_b], tag="cal_b")
        if j == 0:
            k.V(lambda v: v.tensor_scalar(cacc[:], camp[:], dk[:, 0:1], kb[:, 0:1], ALU.mult, ALU.add),
                reads=[camp_b], writes=[cacc_b], tag="cal_c", sreads=[dk_b, kb_b], ptrw=True)
        else:
            k.V(lambda v, j=j: v.scalar_tensor_tensor(cacc[:], camp[:], dk[:, j : j + 1], cacc[:],
                                                      ALU.mult, ALU.add),
                reads=[camp_b, cacc_b], writes=[cacc_b], tag="cal_c", sreads=[dk_b], ptrw=True)
    gates = cacc.rearrange("p (a b) -> p a b", a=32)
    k.dump("gates", cacc[:], [cacc_b], (128, 32 * 72), F32)
    k.dump("sigm", scs[:], [scs_b], (128, 32, 72), F32)

    for tt in range(32):
        k.V(lambda v, tt=tt: v.reduce_max(clv[:], gates[:, tt, 0:8], mybir.AxisListType.X),
            reads=[cacc_b], writes=[clv_b], tag="am_max1")
        k.V(lambda v, tt=tt: v.tensor_scalar(sm1[:, 0:8], gates[:, tt, 0:8], clv[:], None, ALU.is_equal),
            reads=[cacc_b], writes=[sm1_b], tag="am_eq1", sreads=[clv_b], ptrw=True)
        if tt == 0:
            k.dump("dbg_max1", clv[:], [clv_b], (128, 1), F32)
            k.dump("dbg_eq1", sm1[:], [sm1_b], (128, 64), F32)
        k.V(lambda v: v.tensor_mul(sm2[:, 0:8], sm1[:, 0:8], i8mc[:]),
            reads=[sm1_b, i8mc_b], writes=[sm2_b], tag="am_sel1")
        k.V(lambda v: v.tensor_scalar_add(sm2[:, 0:8], sm2[:, 0:8], 8.0),
            reads=[sm2_b], writes=[sm2_b], tag="am_sel1b")
        if tt == 0:
            k.dump("dbg_sel1", sm2[:], [sm2_b], (128, 64), F32)
        k.V(lambda v: v.tensor_reduce(clv[:], sm2[:, 0:8], mybir.AxisListType.X, ALU.min),
            reads=[sm2_b], writes=[clv_b], tag="am_min1")
        if tt == 0:
            k.dump("dbg_cid0", clv[:], [clv_b], (128, 1), F32)
        k.V(lambda v: v.tensor_scalar(sm1[:], iotad8c[:], clv[:], None, ALU.is_equal),
            reads=[iotad8c_b], writes=[sm1_b], tag="am_eq2", sreads=[clv_b], ptrw=True)
        k.V(lambda v: v.tensor_scalar(sm1[:], sm1[:], -1.0, 1e30, ALU.add, ALU.mult),
            reads=[sm1_b], writes=[sm1_b], tag="am_pen")
        k.V(lambda v, tt=tt: v.tensor_add(sm2[:], gates[:, tt, 8:72], sm1[:]),
            reads=[cacc_b, sm1_b], writes=[sm2_b], tag="am_g2")
        k.V(lambda v: v.reduce_max(clv[:], sm2[:], mybir.AxisListType.X),
            reads=[sm2_b], writes=[clv_b], tag="am_max2")
        k.V(lambda v: v.tensor_scalar(sm1[:], sm2[:], clv[:], None, ALU.is_equal),
            reads=[sm2_b], writes=[sm1_b], tag="am_eq3", sreads=[clv_b], ptrw=True)
        k.V(lambda v: v.tensor_mul(sm2[:], sm1[:], i64mc[:]),
            reads=[sm1_b, i64mc_b], writes=[sm2_b], tag="am_sel2")
        k.V(lambda v: v.tensor_scalar_add(sm2[:], sm2[:], 64.0),
            reads=[sm2_b], writes=[sm2_b], tag="am_sel2b")
        k.V(lambda v: v.tensor_reduce(clv[:], sm2[:], mybir.AxisListType.X, ALU.min),
            reads=[sm2_b], writes=[clv_b], tag="am_min2")
        if tt == 0:
            k.dump("dbg_tidx0", clv[:], [clv_b], (128, 1), F32)
            k.dump("dbg_g2", sm2[:], [sm2_b], (128, 64), F32)
        k.V(lambda v, tt=tt: v.tensor_scalar(oh01_tm[:, tt, :], iota64c[:], clv[:], None, ALU.is_equal),
            reads=[iota64c_b], writes=[oh01_tm_b], tag="am_oh", sreads=[clv_b], ptrw=True)

    k.dump("oh01_tm", oh01_tm[:], [oh01_tm_b], (128, 32, 64), BF16)
    for q4 in range(8):
        for j in range(4):
            tt = q4 * 4 + j
            k.T(lambda t, tt=tt, j=j: t.transpose(
                    psx[0][0:64, j * 128 : j * 128 + 128], oh01_tm[:, tt, :], identb[:]),
                reads=[oh01_tm_b, identb_b], writes=[psb[0]], tag="oh_tr")
        k.V(lambda v, q4=q4: v.tensor_copy(oh01_fm[:, q4 * 512 : q4 * 512 + 512], psx[0][0:64, 0:512]),
            reads=[psb[0]], writes=[oh01_fm_b], tag="oh_ev")

    for tt in range(32):
        pt = tt % 4
        k.T(lambda t, tt=tt, pt=pt: t.matmul(
                ps[pt][:, 0:65], oh01_fm[:, tt * 128 : tt * 128 + 128], tab65_sb[0:64, :],
                start=True, stop=True),
            reads=[oh01_fm_b, tab65_b], writes=[psb[pt]], tag="ga_mm")
        k.V(lambda v, tt=tt, pt=pt: v.tensor_copy(gath[:, tt, :], ps[pt][:, 0:65]),
            reads=[psb[pt]], writes=[gath_b], tag="ga_ev")

    k.V(lambda v: v.tensor_scalar_mul(cw2_sb[:].rearrange("p a b -> p (a b)"),
                                      cw2_sb[:].rearrange("p a b -> p (a b)"), 0.5),
        reads=[cw2_b], writes=[cw2_b], tag="cw2_half")
    for h2c in range(2):
        for nt in range(TOK // 512):
            pt = nt % 2
            for c in range(nch):
                k.T(lambda t, h2c=h2c, nt=nt, c=c, pt=pt: t.matmul(
                        ps[pt][:, 0:512], cw1_sb[:, c, h2c * 128 : h2c * 128 + 128],
                        xn0[:, c, nt * 512 : nt * 512 + 512],
                        start=(c == 0), stop=(c == nch - 1)),
                    reads=[cw1_b, xn0_b], writes=[psb[pt]], tag="cp_mm")
            k.V(lambda v, h2c=h2c, pt=pt: v.tensor_scalar(u_sb[:], ps[pt][:, 0:512],
                                                          cb1_sb[:, h2c : h2c + 1], None, ALU.add),
                reads=[psb[pt]], writes=[u_b], tag="cp_u", sreads=[cb1_b], ptrw=True)
            k.S(lambda s: s.activation(e_sb[:], u_sb[:], AF.Erf, scale=0.7071067811865476),
                reads=[u_b], writes=[e_b], tag="cp_erf")
            k.V(lambda v, h2c=h2c, nt=nt: v.scalar_tensor_tensor(
                    h1sb[:, h2c, nt * 512 : nt * 512 + 512], e_sb[:], 1.0, u_sb[:],
                    ALU.add, ALU.mult),
                reads=[e_b, u_b], writes=[h1_b], tag="cp_h1")

    for tt in range(32):
        for h2c in range(2):
            k.T(lambda t, tt=tt, h2c=h2c: t.matmul(
                    ps[2][:, 2 * tt : 2 * tt + 2], h1sb[:, h2c, tt * 128 : tt * 128 + 128],
                    cw2_sb[:, h2c, :], start=(h2c == 0), stop=(h2c == 1)),
                reads=[h1_b, cw2_b], writes=[psb[2]], tag="co_mm")
    for i2 in range(2):
        k.V(lambda v, i2=i2: v.tensor_scalar(compsb[:, i2::2], ps[2][:, i2:64:2],
                                             cb2_sb[:, i2 : i2 + 1], None, ALU.add),
            reads=[psb[2]], writes=[comp_b], tag="co_b", sreads=[cb2_b], ptrw=True)
    k.S(lambda s: s.activation(compsb[:], compsb[:], AF.Tanh),
        reads=[comp_b], writes=[comp_b], tag="co_tanh")

    for tt in range(32):
        k.V(lambda v, tt=tt: v.tensor_scalar(sm3[:], gath[:, tt, 0:16],
                                             compsb[:, 2 * tt : 2 * tt + 1], None, ALU.mult),
            reads=[gath_b], writes=[sm3_b], tag="sp_m1", sreads=[comp_b], ptrw=True)
        k.V(lambda v, tt=tt: v.scalar_tensor_tensor(sm3[:], gath[:, tt, 16:32],
                                                    compsb[:, 2 * tt + 1 : 2 * tt + 2], sm3[:],
                                                    ALU.mult, ALU.add),
            reads=[gath_b, sm3_b], writes=[sm3_b], tag="sp_m2", sreads=[comp_b], ptrw=True)
        k.V(lambda v, tt=tt: v.tensor_add(sm3[:], sm3[:], gath[:, tt, 32:48]),
            reads=[gath_b, sm3_b], writes=[sm3_b], tag="sp_b1")
        k.V(lambda v: v.tensor_scalar_max(sm3[:], sm3[:], 0.0),
            reads=[sm3_b], writes=[sm3_b], tag="sp_relu")
        k.V(lambda v, tt=tt: v.tensor_mul(sm3[:], sm3[:], gath[:, tt, 48:64]),
            reads=[gath_b, sm3_b], writes=[sm3_b], tag="sp_w2")
        k.V(lambda v: v.reduce_sum(mgv[:], sm3[:], mybir.AxisListType.X),
            reads=[sm3_b], writes=[mgv_b], tag="sp_sum")
        k.V(lambda v, tt=tt: v.tensor_scalar(mgv[:], mgv[:], gath[:, tt, 64:65], oscb[:],
                                             ALU.add, ALU.mult),
            reads=[mgv_b], writes=[mgv_b], tag="sp_mag", sreads=[gath_b, oscb_b], ptrw=True)
        k.V(lambda v, tt=tt: v.tensor_scalar(ohm_tm[:], oh01_tm[:, tt, :], mgv[:], None, ALU.mult),
            reads=[oh01_tm_b], writes=[ohm_tm_b], tag="sp_ohm", sreads=[mgv_b], ptrw=True)
        k.T(lambda t, tt=tt: t.transpose(psx[3][0:64, (tt % 4) * 128 : (tt % 4) * 128 + 128],
                                         ohm_tm[:], identb[:]),
            reads=[ohm_tm_b, identb_b], writes=[psb[3]], tag="sp_tr")
        if tt % 4 == 3:
            q4 = tt // 4
            k.V(lambda v, q4=q4: v.tensor_copy(ohs_fm[:, q4 * 512 : q4 * 512 + 512],
                                               psx[3][0:64, 0:512]),
                reads=[psb[3]], writes=[ohs_fm_b], tag="sp_ev")

    k.dump("ohs_fm", ohs_fm[:], [ohs_fm_b], (64, TOK), BF16)
    k.dump("oh01_fm", oh01_fm[:], [oh01_fm_b], (64, TOK), BF16)
    k.dump("gath", gath[:], [gath_b], (128, 32, 65), F32)
    k.dump("compsb", compsb[:], [comp_b], (128, 64), F32)
    k.release_arena()
    k.release_arena()
    k.barrier()

    if upto <= 3:
        k.p.finalize(nc)
        return k

    import os as _os
    _skip_p4 = bool(int(_os.environ.get("SKIP_P4", "0")))
    s0_d = k.dscr("s0_d", (B_PER_CORE, D, T), F32)
    s0_d_b = Buf("s0_d")

    ar.mark()
    ql_sb = ar.alloc([128, T], BF16, "ql_sb")
    p_sb = [ar.alloc([128, 2048], BF16, f"p_sb{i}") for i in range(2)]
    ctx_sb = ar.alloc([128, 1024], BF16, "ctx_sb")
    s0_sb = ar.alloc([128, T], F32, "s0_sb")
    rz64 = ar.alloc([64, 1024], F32, "rz64")
    trilb = ar.alloc([128, 128], BF16, "trilb")
    ql_b = k.buf("ql_sb")
    p_b = [k.buf(f"p_sb{i}") for i in range(2)]
    ctx_sb_b = k.buf("ctx_sb")
    s0_sb_b = k.buf("s0_sb")
    rz64_b = k.buf("rz64")
    trilb_b = k.buf("trilb")
    tril_in = k.din("tril_bf", (128, 128), BF16)
    k.GQ(lambda g: g.dma_start(trilb[:], tril_in.ap()), writes=[trilb_b], tag="ld_tril")

    for b in range(B_PER_CORE if not _skip_p4 else 0):
        for h in range(16):
            for half in range(2):
                for n2 in range(2):
                    k.T(lambda t, b=b, h=h, half=half, n2=n2: t.matmul(
                            ps[half][:, n2 * 512 : n2 * 512 + 512],
                            bq[:, h, :],
                            q1[:, b * T + half * 1024 + n2 * 512 : b * T + half * 1024 + n2 * 512 + 512],
                            start=True, stop=True),
                        reads=[bq_b, q1_b], writes=[psb[half]], tag="ql_mm")
                k.V(lambda v, half=half: v.tensor_copy(
                        ql_sb[:, half * 1024 : half * 1024 + 1024], ps[half][:, :]),
                    reads=[psb[half]], writes=[ql_b], tag="ql_ev")
            for half in range(2):
                h0 = half * 1024
                nsig = 8 * (half + 1)
                for m in range(nsig // 2):
                    se, so = 2 * m, 2 * m + 1
                    rel_e = max(se * 128, h0) - h0
                    rel_o = max(so * 128, h0) - h0
                    pm = m % 2
                    for sig, rel in ((se, rel_e), (so, rel_o)):
                        pi = sig % 2
                        segs = ([(rel, 512 - rel), (512, 512)] if rel < 512
                                else [(rel, 1024 - rel)])
                        for o, w in segs:
                            k.T(lambda t, b=b, sig=sig, pi=pi, o=o, w=w, h0=h0: t.matmul(
                                    ps[pi][:, o : o + w],
                                    ckv[:, b * T + sig * 128 : b * T + sig * 128 + 128],
                                    ql_sb[:, h0 + o : h0 + o + w],
                                    start=True, stop=True),
                                reads=[ckv_b, ql_b], writes=[psb[pi]], tag="sc_mm")
                    k.S(lambda s, pm=pm, rel_e=rel_e: s.activation(
                            p_sb[pm][:, rel_e : 2048], psAf[:, rel_e : 2048],
                            AF.Exp, scale=1.0 / 8.0),
                        reads=[psb[0], psb[1]], writes=[p_b[pm]], tag="sc_exp")
                    if rel_o > rel_e:
                        k.V(lambda v, pm=pm, rel_e=rel_e, rel_o=rel_o: v.memset(
                                p_sb[pm][:, 1024 + rel_e : 1024 + rel_o], 0.0),
                            reads=[], writes=[p_b[pm]], tag="sc_inv")
                    if se * 128 >= h0:
                        k.V(lambda v, pm=pm, rel_e=rel_e: v.tensor_mul(
                                p_sb[pm][:, rel_e : rel_e + 128],
                                p_sb[pm][:, rel_e : rel_e + 128], trilb[:]),
                            reads=[trilb_b], writes=[p_b[pm]], tag="sc_mask")
                    if so * 128 >= h0:
                        k.V(lambda v, pm=pm, rel_o=rel_o: v.tensor_mul(
                                p_sb[pm][:, 1024 + rel_o : 1024 + rel_o + 128],
                                p_sb[pm][:, 1024 + rel_o : 1024 + rel_o + 128], trilb[:]),
                            reads=[trilb_b], writes=[p_b[pm]], tag="sc_mask")
                    for sig, rel, base in ((se, rel_e, 0), (so, rel_o, 1024)):
                        segs = ([(rel, 512 - rel), (512, 512)] if rel < 512
                                else [(rel, 1024 - rel)])
                        last = sig == nsig - 1
                        for o, w in segs:
                            k.T(lambda t, pm=pm, o=o, w=w, sig=sig, last=last, base=base: t.matmul(
                                    ps[2][:, o : o + w], onesb[:],
                                    p_sb[pm][:, base + o : base + o + w],
                                    start=(sig == 0), stop=last),
                                reads=[onesb_b, p_b[pm]], writes=[psb[2]], tag="z_mm")
                        for o, w in segs:
                            k.T(lambda t, b=b, sig=sig, pm=pm, o=o, w=w, last=last, base=base: t.matmul(
                                    ps[3][:, o : o + w], ckv_tm[:, b * 16 + sig, :],
                                    p_sb[pm][:, base + o : base + o + w],
                                    start=(sig == 0), stop=last),
                                reads=[ckv_tm_b, p_b[pm]], writes=[psb[3]], tag="ctx_mm")
                k.S(lambda s: s.activation(rz64[:, :], ps[2][0:64, :], AF.Ln),
                    reads=[psb[2]], writes=[rz64_b], tag="lnz")
                k.S(lambda s: s.activation(rz64[:, :], rz64[:, :], AF.Exp, scale=-1.0),
                    reads=[rz64_b], writes=[rz64_b], tag="rz")
                k.V(lambda v: v.tensor_copy(ctx_sb[:, :], ps[3][:, :]),
                    reads=[psb[3]], writes=[ctx_sb_b], tag="ctx_ev")
                for n2 in range(2):
                    k.T(lambda t, h=h, n2=n2: t.matmul(
                            ps[2][0:64, n2 * 512 : n2 * 512 + 512],
                            v2[:, h * 64 : h * 64 + 64],
                            ctx_sb[:, n2 * 512 : n2 * 512 + 512],
                            start=True, stop=True),
                        reads=[v2_b, ctx_sb_b], writes=[psb[2]], tag="y_mm")
                po = (h % 2) * 64
                k.V(lambda v, half=half, po=po: v.tensor_mul(
                        s0_sb[po : po + 64, half * 1024 : half * 1024 + 1024],
                        ps[2][0:64, :], rz64[:, :]),
                    reads=[psb[2], rz64_b], writes=[s0_sb_b], tag="y_ev")
            if h % 2 == 1:
                k.SQ(lambda s, b=b, h=h: s.dma_start(
                        s0_d.ap()[b, (h // 2) * 128 : (h // 2) * 128 + 128, :], s0_sb[:, :]),
                    reads=[s0_sb_b], writes=[s0_d_b], tag="st_s0")
    k.release_arena()
    k.barrier()

    if upto <= 4:
        k.p.finalize(nc)
        return k

    dirs_in = k.din("dirs", (128, D))
    gam0_in = k.din("gamma0", (128, NCH))
    bet0_in = k.din("beta0", (128, NCH))
    y_d = k.dscr("y_d", (B_PER_CORE, D, T), BF16)
    x1_d = k.dscr("x1_d", (B_PER_CORE, D, T), F32)
    xn1_d = k.dscr("xn1_d", (B_PER_CORE, D, T), BF16)
    y_d_b = Buf("y_d")
    x1_d_b = Buf("x1_d")
    xn1_d_b = Buf("xn1_d")

    ar.mark()
    dirs_sb = ar.alloc([128, D], BF16, "dirs_sb")
    gam0_sb = ar.alloc([128, nch], F32, "gam0_sb")
    bet0_sb = ar.alloc([128, nch], F32, "bet0_sb")
    mu_bc = [ar.alloc([128, BLK], F32, f"mu_bc{i}") for i in range(2)]
    r_bc = [ar.alloc([128, BLK], F32, f"r_bc{i}") for i in range(2)]
    xti = [ar.alloc([128, BLK], F32, f"xti{i}") for i in range(2)]
    ybt = [ar.alloc([128, BLK], BF16, f"ybt{i}") for i in range(2)]
    sqb = ar.alloc([128, BLK], BF16, "sqb")
    s0t = [ar.alloc([128, BLK], F32, f"s0t{i}") for i in range(2)]
    muy = ar.alloc([128, 4, BLK], F32, "muy")
    ry = ar.alloc([128, 4, BLK], F32, "ry")
    yint = [ar.alloc([128, BLK], BF16, f"yint{i}") for i in range(2)]
    g0t = [ar.alloc([128, BLK], F32, f"g0t{i}") for i in range(2)]
    x1t = [ar.alloc([128, BLK], F32, f"x1t{i}") for i in range(2)]
    x1b = ar.alloc([128, nch, T], BF16, "x1b")
    mu1 = ar.alloc([128, 4, BLK], F32, "mu1")
    r1 = ar.alloc([128, 4, BLK], F32, "r1")
    tmpb = ar.alloc([128, BLK], F32, "tmpb")
    xn1t = [ar.alloc([128, BLK], BF16, f"xn1t{i}") for i in range(2)]

    dirs_b = k.buf("dirs_sb"); gam0_b = k.buf("gam0_sb"); bet0_b = k.buf("bet0_sb")
    mu_bc_b = [k.buf(f"mu_bc{i}") for i in range(2)]
    r_bc_b = [k.buf(f"r_bc{i}") for i in range(2)]
    xti_b = [k.buf(f"xti{i}") for i in range(2)]
    ybt_b = [k.buf(f"ybt{i}") for i in range(2)]
    sqb_b = k.buf("sqb")
    s0t_b = [k.buf(f"s0t{i}") for i in range(2)]
    muy_b = k.buf("muy"); ry_b = k.buf("ry")
    yint_b = [k.buf(f"yint{i}") for i in range(2)]
    g0t_b = [k.buf(f"g0t{i}") for i in range(2)]
    x1t_b = [k.buf(f"x1t{i}") for i in range(2)]
    x1b_b = k.buf("x1b")
    mu1_b = k.buf("mu1"); r1_b = k.buf("r1"); tmpb_b = k.buf("tmpb")
    xn1t_b = [k.buf(f"xn1t{i}") for i in range(2)]

    k.GQ(lambda g: g.dma_start(dirs_sb[:], dirs_in.ap()), writes=[dirs_b], tag="ld_dirs")
    k.SQ(lambda s: s.dma_start(gam0_sb[:], gam0_in.ap()), writes=[gam0_b], tag="ld_g0")
    k.SQ(lambda s: s.dma_start(bet0_sb[:], bet0_in.ap()), writes=[bet0_b], tag="ld_b0")

    nblk_b = T // BLK
    for b in range(B_PER_CORE):
        for blk in range(nblk_b):
            t0 = blk * BLK
            g = b * T + t0
            i = blk % 2
            gb = b * nblk_b + blk
            k.GQ(lambda gq, gb=gb, i=i: gq.dma_start(mu_bc[i][:], mur2_d.ap()[0, gb]),
                 reads=[mur2_d_b], writes=[mu_bc_b[i]], tag="p5_mubc")
            k.GQ(lambda gq, gb=gb, i=i: gq.dma_start(r_bc[i][:], mur2_d.ap()[1, gb]),
                 reads=[mur2_d_b], writes=[r_bc_b[i]], tag="p5_rbc")
            for c in range(nch):
                ic = (blk * nch + c) % 2
                pd = 3 * (c % 2)
                k.T(lambda t, c=c, g=g, pd=pd: t.matmul(
                        ps[pd][:, 0:BLK], dirs_sb[0:64, c * 128 : c * 128 + 128],
                        ohs_fm[:, g : g + BLK], start=True, stop=True),
                    reads=[dirs_b, ohs_fm_b], writes=[psb[pd]], tag="p5_dir")
                k.SQ(lambda s, b=b, c=c, t0=t0, ic=ic: s.dma_start(
                        xti[ic][:], xfm.ap()[b, c * 128 : c * 128 + 128, t0 : t0 + BLK]),
                     reads=[xfm_b], writes=[xti_b[ic]], tag="p5_ldx")
                k.V(lambda v, ic=ic, i=i: v.tensor_sub(xti[ic][:], xti[ic][:], mu_bc[i][:]),
                    reads=[mu_bc_b[i]], writes=[xti_b[ic]], tag="p5_xnsub")
                k.V(lambda v, ic=ic, i=i: v.tensor_mul(xti[ic][:], xti[ic][:], r_bc[i][:]),
                    reads=[r_bc_b[i]], writes=[xti_b[ic]], tag="p5_xnmul")
                import os as _os2
                if int(_os2.environ.get("NO_PSUM_Y", "0")):
                    k.V(lambda v, ic=ic: v.tensor_copy(ybt[ic][:], xti[ic][:]),
                        reads=[xti_b[ic]], writes=[ybt_b[ic]], tag="p5_y")
                else:
                    k.V(lambda v, ic=ic: v.tensor_add(ybt[ic][:], xti[ic][:], ps[0][:, 0:BLK]),
                        reads=[xti_b[ic], psb[0]], writes=[ybt_b[ic]], tag="p5_y")
                k.SQ(lambda s, b=b, c=c, t0=t0, ic=ic: s.dma_start(
                        y_d.ap()[b, c * 128 : c * 128 + 128, t0 : t0 + BLK], ybt[ic][:]),
                     reads=[ybt_b[ic]], writes=[y_d_b], tag="p5_sty")
                k.V(lambda v, ic=ic: v.tensor_mul(sqb[:], ybt[ic][:], ybt[ic][:]),
                    reads=[ybt_b[ic]], writes=[sqb_b], tag="p5_sq")
                k.T(lambda t, c=c, ic=ic: t.matmul(ps[1][:, 0:BLK], onesb[:], ybt[ic][:],
                                                   start=(c == 0), stop=(c == nch - 1)),
                    reads=[ybt_b[ic], onesb_b], writes=[psb[1]], tag="p5_mmS")
                k.T(lambda t, c=c: t.matmul(ps[2][:, 0:BLK], onesb[:], sqb[:],
                                            start=(c == 0), stop=(c == nch - 1)),
                    reads=[sqb_b, onesb_b], writes=[psb[2]], tag="p5_mmS2")

            k.V(lambda v, blk=blk: v.tensor_scalar_mul(muy[:, blk, :], ps[1][:, 0:BLK], 1.0 / D),
                reads=[psb[1]], writes=[muy_b], tag="p5_muy")
            k.V(lambda v, blk=blk: v.tensor_mul(tmpb[:], muy[:, blk, :], muy[:, blk, :]),
                reads=[muy_b], writes=[tmpb_b], tag="p5_musq")
            k.V(lambda v, blk=blk: v.scalar_tensor_tensor(ry[:, blk, :], ps[2][:, 0:BLK], 1.0 / D,
                                                          tmpb[:], ALU.mult, ALU.subtract),
                reads=[psb[2], tmpb_b], writes=[ry_b], tag="p5_vary")
            k.S(lambda s, blk=blk: s.activation(ry[:, blk, :], ry[:, blk, :], AF.Ln, bias=epsc[:]),
                reads=[ry_b, epsc_b], writes=[ry_b], tag="p5_lny")
            k.S(lambda s, blk=blk: s.activation(ry[:, blk, :], ry[:, blk, :], AF.Exp, scale=-0.5),
                reads=[ry_b], writes=[ry_b], tag="p5_expy")
        if b == 1:
            k.dump("dbg_ry1", ry[:], [ry_b], (128, 4, BLK), F32)
            k.dump("dbg_muy1", muy[:], [muy_b], (128, 4, BLK), F32)
        for blk in range(nblk_b):
            t0 = blk * BLK
            for c in range(nch):
                ic = (blk * nch + c) % 2
                k.SQ(lambda s, b=b, c=c, t0=t0, ic=ic: s.dma_start(
                        yint[ic][:], y_d.ap()[b, c * 128 : c * 128 + 128, t0 : t0 + BLK]),
                     reads=[y_d_b], writes=[yint_b[ic]], tag="p5_ldy")
                k.V(lambda v, blk=blk, ic=ic: v.tensor_sub(g0t[ic][:], yint[ic][:], muy[:, blk, :]),
                    reads=[yint_b[ic], muy_b], writes=[g0t_b[ic]], tag="p5_ynsub")
                k.V(lambda v, blk=blk, ic=ic: v.tensor_mul(g0t[ic][:], g0t[ic][:], ry[:, blk, :]),
                    reads=[ry_b], writes=[g0t_b[ic]], tag="p5_ynmul")
                k.S(lambda s, c=c, ic=ic: s.activation(g0t[ic][:], g0t[ic][:], AF.Silu,
                                                       bias=bet0_sb[:, c : c + 1],
                                                       scale=gam0_sb[:, c : c + 1]),
                    reads=[g0t_b[ic]], writes=[g0t_b[ic]], tag="p5_silu",
                    sreads=[gam0_b, bet0_b])
                k.SQ(lambda s, b=b, c=c, t0=t0, ic=ic: s.dma_start(
                        x1t[ic][:], xfm.ap()[b, c * 128 : c * 128 + 128, t0 : t0 + BLK]),
                     reads=[xfm_b], writes=[x1t_b[ic]], tag="p5_ldx2")
                k.SQ(lambda s, b=b, c=c, t0=t0, ic=ic: s.dma_start(
                        s0t[ic][:], s0_d.ap()[b, c * 128 : c * 128 + 128, t0 : t0 + BLK]),
                     reads=[s0_d_b], writes=[s0t_b[ic]], tag="p5_lds0")
                k.V(lambda v, ic=ic: v.tensor_mul(g0t[ic][:], g0t[ic][:], s0t[ic][:]),
                    reads=[s0t_b[ic]], writes=[g0t_b[ic]], tag="p5_inc")
                k.V(lambda v, ic=ic: v.tensor_add(x1t[ic][:], x1t[ic][:], g0t[ic][:]),
                    reads=[g0t_b[ic]], writes=[x1t_b[ic]], tag="p5_x1")
                k.SQ(lambda s, b=b, c=c, t0=t0, ic=ic: s.dma_start(
                        x1_d.ap()[b, c * 128 : c * 128 + 128, t0 : t0 + BLK], x1t[ic][:]),
                     reads=[x1t_b[ic]], writes=[x1_d_b], tag="p5_stx1")
                k.V(lambda v, c=c, t0=t0, ic=ic: v.tensor_copy(x1b[:, c, t0 : t0 + BLK], x1t[ic][:]),
                    reads=[x1t_b[ic]], writes=[x1b_b], tag="p5_x1b")
                k.V(lambda v, c=c, t0=t0: v.tensor_mul(sqb[:], x1b[:, c, t0 : t0 + BLK],
                                                       x1b[:, c, t0 : t0 + BLK]),
                    reads=[x1b_b], writes=[sqb_b], tag="p5_sq1")
                k.T(lambda t, c=c, t0=t0: t.matmul(ps[1][:, 0:BLK], onesb[:],
                                                   x1b[:, c, t0 : t0 + BLK],
                                                   start=(c == 0), stop=(c == nch - 1)),
                    reads=[x1b_b, onesb_b], writes=[psb[1]], tag="p5_mmT")
                k.T(lambda t, c=c: t.matmul(ps[2][:, 0:BLK], onesb[:], sqb[:],
                                            start=(c == 0), stop=(c == nch - 1)),
                    reads=[sqb_b, onesb_b], writes=[psb[2]], tag="p5_mmT2")
            k.V(lambda v, blk=blk: v.tensor_scalar_mul(mu1[:, blk, :], ps[1][:, 0:BLK], 1.0 / D),
                reads=[psb[1]], writes=[mu1_b], tag="p5_mu1")
            k.V(lambda v, blk=blk: v.tensor_mul(tmpb[:], mu1[:, blk, :], mu1[:, blk, :]),
                reads=[mu1_b], writes=[tmpb_b], tag="p5_musq1")
            k.V(lambda v, blk=blk: v.scalar_tensor_tensor(r1[:, blk, :], ps[2][:, 0:BLK], 1.0 / D,
                                                          tmpb[:], ALU.mult, ALU.subtract),
                reads=[psb[2], tmpb_b], writes=[r1_b], tag="p5_var1")
        for blk in range(nblk_b):
            t0 = blk * BLK
            k.S(lambda s, blk=blk: s.activation(r1[:, blk, :], r1[:, blk, :], AF.Ln, bias=epsc[:]),
                reads=[r1_b, epsc_b], writes=[r1_b], tag="p5_ln1")
            k.S(lambda s, blk=blk: s.activation(r1[:, blk, :], r1[:, blk, :], AF.Exp, scale=-0.5),
                reads=[r1_b], writes=[r1_b], tag="p5_exp1")
            for c in range(nch):
                ic = (blk * nch + c) % 2
                k.V(lambda v, blk=blk, c=c, t0=t0, ic=ic: v.tensor_sub(
                        xn1t[ic][:], x1b[:, c, t0 : t0 + BLK], mu1[:, blk, :]),
                    reads=[x1b_b, mu1_b], writes=[xn1t_b[ic]], tag="p5_xn1s")
                k.V(lambda v, blk=blk, ic=ic: v.tensor_mul(xn1t[ic][:], xn1t[ic][:], r1[:, blk, :]),
                    reads=[r1_b], writes=[xn1t_b[ic]], tag="p5_xn1m")
                k.SQ(lambda s, b=b, c=c, t0=t0, ic=ic: s.dma_start(
                        xn1_d.ap()[b, c * 128 : c * 128 + 128, t0 : t0 + BLK], xn1t[ic][:]),
                     reads=[xn1t_b[ic]], writes=[xn1_d_b], tag="p5_stxn1")
    k.release_arena()
    k.release_arena()
    k.barrier()

    if upto <= 5:
        k.p.finalize(nc)
        return k

    env = dict(locals())
    _build_p7(k, env)
    if upto <= 7:
        k.p.finalize(nc)
        return k
    _build_p8(k, env)
    k.p.finalize(nc)
    return k


def _build_p7(k, env):
    nc = k.nc
    ar = k.ar
    nch = NCH
    ps, psx, psb = env["ps"], env["psx"], env["psb"]
    psAf = env["psAf"]
    onesb, onesb_b = env["onesb"], env["onesb_b"]
    epsc, epsc_b = env["epsc"], env["epsc_b"]
    xn1_d, xn1_d_b = env["xn1_d"], env["xn1_d_b"]

    qwT = k.din("qwT", (D, D))
    kwT = k.din("kwT", (D, D))
    vwT = k.din("vwT", (D, D))
    owT = k.din("owT", (D, D))
    qb_in = k.din("qb", (128, nch))
    kb_in = k.din("kb", (128, nch))
    ob_in = k.din("ob", (128, nch))
    vbrow_in = k.din("vbrow", (128, D))
    nw_in = k.din("nw", (128, nch))
    nb_in = k.din("nb", (128, nch))
    g1_d = k.dscr("g1_d", (B_PER_CORE, D, T), F32)
    g1_d_b = Buf("g1_d")
    env["g1_d"] = g1_d
    env["g1_d_b"] = g1_d_b

    ar.mark()
    xnl_a = ar.alloc([128, 4, TOK], BF16, "xnl_a")
    xnl_b4 = ar.alloc([128, 4, 1024], BF16, "xnl_b4")
    vw_sb = ar.alloc([128, nch, D], BF16, "vw_sb")
    qkw = ar.alloc([128, nch, 256], BF16, "qkw")
    oww = ar.alloc([128, nch, 128], BF16, "oww")
    qb_sb = ar.alloc([128, nch], F32, "qb_sb")
    kb_sb = ar.alloc([128, nch], F32, "kb_sb")
    ob_sb = ar.alloc([128, nch], F32, "ob_sb")
    vbr_sb = ar.alloc([128, D], BF16, "vbr_sb")
    nw_sb = ar.alloc([128, nch], F32, "nw_sb")
    nb_sb = ar.alloc([128, nch], F32, "nb_sb")
    vtm = ar.alloc([128, 16, D], BF16, "vtm")
    aob = ar.alloc([128, nch, T], BF16, "aob")
    qh = ar.alloc([128, T], BF16, "qh")
    kh = ar.alloc([128, T], BF16, "kh")
    p7sb = [ar.alloc([128, 2048], BF16, f"p7sb{i}") for i in range(2)]
    rzf = ar.alloc([128, 1024], F32, "rzf")
    x2t = [ar.alloc([128, nch, BLK], BF16, f"p7xn{i}") for i in range(2)]
    sq7 = ar.alloc([128, BLK], BF16, "sq7")
    mue = [ar.alloc([128, BLK], F32, f"mue{i}") for i in range(2)]
    ree = [ar.alloc([128, BLK], F32, f"ree{i}") for i in range(2)]
    tmp7 = ar.alloc([128, BLK], F32, "tmp7")
    xnt7 = [ar.alloc([128, BLK], F32, f"xnt7{i}") for i in range(2)]
    g1t = [ar.alloc([128, BLK], F32, f"g1t{i}") for i in range(2)]
    xt7 = [ar.alloc([128, BLK], BF16, f"xt7{i}") for i in range(2)]

    xnl_a_b = k.buf("xnl_a"); xnl_b4_b = k.buf("xnl_b4")
    vw_b = k.buf("vw_sb"); qkw_qb = k.buf("qkw_q"); qkw_kb = k.buf("qkw_k"); oww_b = k.buf("oww")
    qb_b = k.buf("qb_sb"); kb_b = k.buf("kb_sb"); ob_b = k.buf("ob_sb")
    vbr_b = k.buf("vbr_sb"); nw_b = k.buf("nw_sb"); nb_b = k.buf("nb_sb")
    vtm_b = k.buf("vtm"); aob_b = k.buf("aob")
    qh_b = k.buf("qh"); kh_b = k.buf("kh")
    p7sb_b = [k.buf(f"p7sb{i}") for i in range(2)]
    rzf_b = k.buf("rzf")
    x2t_b = [k.buf(f"p7xn{i}") for i in range(2)]
    sq7_b = k.buf("sq7")
    mue_b = [k.buf(f"mue{i}") for i in range(2)]
    ree_b = [k.buf(f"ree{i}") for i in range(2)]
    tmp7_b = k.buf("tmp7")
    xnt7_b = [k.buf(f"xnt7{i}") for i in range(2)]
    g1t_b = [k.buf(f"g1t{i}") for i in range(2)]
    xt7_b = [k.buf(f"xt7{i}") for i in range(2)]

    ch_rows = lambda t: t.ap().rearrange("(c p) m -> p c m", p=128)
    k.GQ(lambda g: g.dma_start(vw_sb[:], ch_rows(vwT)), writes=[vw_b], tag="ld_vw")
    for nm, sb, bb, ti in (("qb", qb_sb, qb_b, qb_in), ("kb", kb_sb, kb_b, kb_in),
                           ("ob", ob_sb, ob_b, ob_in),
                           ("nw", nw_sb, nw_b, nw_in), ("nb", nb_sb, nb_b, nb_in)):
        k.SQ(lambda s, sb=sb, ti=ti: s.dma_start(sb[:], ti.ap()), writes=[bb], tag="ld7_" + nm)
    k.GQ(lambda g: g.dma_start(vbr_sb[:], vbrow_in.ap()), writes=[vbr_b], tag="ld7_vbr")

    MC = [8, 4, 2, 1]

    for blk in range(TOK // BLK):
        b, t0 = divmod(blk * BLK, T)
        i = blk % 2
        k.SQ(lambda s, b=b, t0=t0, i=i: s.dma_start(
                x2t[i][:], xn1_d.ap()[b].rearrange("(c p) t -> p c t", p=128)[:, :, t0 : t0 + BLK]),
             reads=[xn1_d_b], writes=[x2t_b[i]], tag="p7_ldxn")
        for c in range(nch):
            k.V(lambda v, i=i, c=c: v.tensor_mul(sq7[:], x2t[i][:, c, :], x2t[i][:, c, :]),
                reads=[x2t_b[i]], writes=[sq7_b], tag="p7_sq")
            k.T(lambda t, c=c, i=i: t.matmul(ps[1][:, 0:BLK], onesb[:], x2t[i][:, c, :],
                                             start=(c == 0), stop=(c == nch - 1)),
                reads=[x2t_b[i], onesb_b], writes=[psb[1]], tag="p7_mmS")
            k.T(lambda t, c=c: t.matmul(ps[2][:, 0:BLK], onesb[:], sq7[:],
                                        start=(c == 0), stop=(c == nch - 1)),
                reads=[sq7_b, onesb_b], writes=[psb[2]], tag="p7_mmS2")
        k.V(lambda v, i=i: v.tensor_scalar_mul(mue[i][:], ps[1][:, 0:BLK], 1.0 / D),
            reads=[psb[1]], writes=[mue_b[i]], tag="p7_mu")
        k.V(lambda v, i=i: v.tensor_mul(tmp7[:], mue[i][:], mue[i][:]),
            reads=[mue_b[i]], writes=[tmp7_b], tag="p7_musq")
        k.V(lambda v, i=i: v.scalar_tensor_tensor(ree[i][:], ps[2][:, 0:BLK], 1.0 / D,
                                                  tmp7[:], ALU.mult, ALU.subtract),
            reads=[psb[2], tmp7_b], writes=[ree_b[i]], tag="p7_var")
        k.S(lambda s, i=i: s.activation(ree[i][:], ree[i][:], AF.Ln, bias=epsc[:]),
            reads=[ree_b[i], epsc_b], writes=[ree_b[i]], tag="p7_ln")
        k.S(lambda s, i=i: s.activation(ree[i][:], ree[i][:], AF.Exp, scale=-0.5),
            reads=[ree_b[i]], writes=[ree_b[i]], tag="p7_exp")
        e = t0 // 512
        nchunks = nch if e == 0 else MC[e]
        for c in range(nchunks):
            k.V(lambda v, i=i, c=c: v.tensor_sub(xnt7[i][:], x2t[i][:, c, :], mue[i][:]),
                reads=[x2t_b[i], mue_b[i]], writes=[xnt7_b[i]], tag="p7_xs")
            k.V(lambda v, i=i: v.tensor_mul(xnt7[i][:], xnt7[i][:], ree[i][:]),
                reads=[ree_b[i]], writes=[xnt7_b[i]], tag="p7_xm")
            if c < 4:
                dst = xnl_a[:, c, blk * BLK : blk * BLK + BLK]
                dbuf = xnl_a_b
            else:
                dst = xnl_b4[:, c - 4, b * BLK : b * BLK + BLK]
                dbuf = xnl_b4_b
            k.S(lambda s, i=i, c=c, dst=dst: s.activation(dst, xnt7[i][:], AF.Identity,
                                                          bias=nb_sb[:, c : c + 1],
                                                          scale=nw_sb[:, c : c + 1]),
                reads=[xnt7_b[i]], writes=[dbuf], tag="p7_gb", sreads=[nw_b, nb_b])

    def xnl(c, cols, b):
        if c < 4:
            return xnl_a[:, c, cols[0] : cols[1]]
        off = (cols[0] - b * T) + b * BLK
        return xnl_b4[:, c - 4, off : off + (cols[1] - cols[0])]

    for b in range(B_PER_CORE):
        for e in range(4):
            tg0 = b * T + e * 512
            for tt in range(4):
                c0 = tg0 + tt * 128
                ti = e * 4 + tt
                pv = 3 * (tt % 2)
                for n2 in range(2):
                    for mc in range(MC[e]):
                        k.T(lambda t, ti=ti, c0=c0, mc=mc, n2=n2, b=b, pv=pv: t.matmul(
                                ps[pv][:, n2 * 512 : n2 * 512 + 512],
                                xnl(mc, (c0, c0 + 128), b),
                                vw_sb[:, mc, n2 * 512 : n2 * 512 + 512],
                                start=(mc == 0), stop=False),
                            reads=[xnl_a_b, xnl_b4_b, vw_b], writes=[psb[pv]], tag="p7v_mm")
                    k.T(lambda t, n2=n2, pv=pv: t.matmul(
                            ps[pv][:, n2 * 512 : n2 * 512 + 512],
                            onesb[0:1, :], vbr_sb[0:1, n2 * 512 : n2 * 512 + 512],
                            start=False, stop=True),
                        reads=[onesb_b, vbr_b], writes=[psb[pv]], tag="p7v_bias")
                k.V(lambda v, ti=ti, pv=pv: v.tensor_copy(vtm[:, ti, :], ps[pv][:, 0:1024]),
                    reads=[psb[pv]], writes=[vtm_b], tag="p7v_ev")
        for h in range(8):
            k.GQ(lambda g, h=h: g.dma_start(
                    qkw[:, :, 0:128],
                    qwT.ap().rearrange("(c p) m -> p c m", p=128)[:, :, h * 128 : h * 128 + 128]),
                 writes=[qkw_qb], tag="p7_ldqw")
            k.GQ(lambda g, h=h: g.dma_start(
                    qkw[:, :, 128:256],
                    kwT.ap().rearrange("(c p) m -> p c m", p=128)[:, :, h * 128 : h * 128 + 128]),
                 writes=[qkw_kb], tag="p7_ldkw")
            for e in range(4):
                cg0 = b * T + e * 512
                for off, dstw, dbuf, bias, bbuf, tg in (
                        (0, qh, qh_b, qb_sb, qb_b, "p7_qmm"),
                        (128, kh, kh_b, kb_sb, kb_b, "p7_kmm")):
                    wb = qkw_qb if off == 0 else qkw_kb
                    pqk = 3 * ((e + (off // 128)) % 2)
                    for mc in range(MC[e]):
                        k.T(lambda t, e=e, cg0=cg0, mc=mc, off=off, b=b, pqk=pqk: t.matmul(
                                ps[pqk][:, 0:512], qkw[:, mc, off : off + 128],
                                xnl(mc, (cg0, cg0 + 512), b),
                                start=(mc == 0), stop=(mc == MC[e] - 1)),
                            reads=[wb, xnl_a_b, xnl_b4_b], writes=[psb[pqk]], tag=tg)
                    k.V(lambda v, e=e, dstw=dstw, bias=bias, h=h, pqk=pqk: v.tensor_scalar(
                            dstw[:, e * 512 : e * 512 + 512], ps[pqk][:, 0:512],
                            bias[:, h : h + 1], None, ALU.add),
                        reads=[psb[pqk]], writes=[dbuf], tag=tg + "e", sreads=[bbuf], ptrw=True)
            for half in range(2):
                hc0 = half * 1024
                for m in range(8):
                    se, so = 2 * m, 2 * m + 1
                    pm = m % 2
                    for sig in (se, so):
                        pi = sig % 2
                        for n2 in range(2):
                            k.T(lambda t, sig=sig, pi=pi, n2=n2, hc0=hc0: t.matmul(
                                    ps[pi][:, n2 * 512 : n2 * 512 + 512],
                                    kh[:, sig * 128 : sig * 128 + 128],
                                    qh[:, hc0 + n2 * 512 : hc0 + n2 * 512 + 512],
                                    start=True, stop=True),
                                reads=[kh_b, qh_b], writes=[psb[pi]], tag="p7_sc")
                    k.S(lambda s, pm=pm: s.activation(p7sb[pm][:], psAf[:, :], AF.Exp,
                                                      scale=0.03125),
                        reads=[psb[0], psb[1]], writes=[p7sb_b[pm]], tag="p7_exp2")
                    for sig, base in ((se, 0), (so, 1024)):
                        for n2 in range(2):
                            k.T(lambda t, pm=pm, n2=n2, sig=sig, base=base: t.matmul(
                                    ps[2][:, n2 * 512 : n2 * 512 + 512], onesb[:],
                                    p7sb[pm][:, base + n2 * 512 : base + n2 * 512 + 512],
                                    start=(sig == 0), stop=(sig == 15)),
                                reads=[onesb_b, p7sb_b[pm]], writes=[psb[2]], tag="p7_z")
                            k.T(lambda t, pm=pm, n2=n2, sig=sig, h=h, base=base: t.matmul(
                                    ps[3][:, n2 * 512 : n2 * 512 + 512],
                                    vtm[:, sig, h * 128 : h * 128 + 128],
                                    p7sb[pm][:, base + n2 * 512 : base + n2 * 512 + 512],
                                    start=(sig == 0), stop=(sig == 15)),
                                reads=[vtm_b, p7sb_b[pm]], writes=[psb[3]], tag="p7_ctx")
                k.S(lambda s: s.activation(rzf[:], ps[2][:, :], AF.Ln),
                    reads=[psb[2]], writes=[rzf_b], tag="p7_lnz")
                k.S(lambda s: s.activation(rzf[:], rzf[:], AF.Exp, scale=-1.0),
                    reads=[rzf_b], writes=[rzf_b], tag="p7_rz")
                k.V(lambda v, h=h, hc0=hc0, b=b: v.tensor_mul(
                        aob[:, h, hc0 : hc0 + 1024], ps[3][:, :], rzf[:]),
                    reads=[psb[3], rzf_b], writes=[aob_b], tag="p7_cev")
        for e in range(4):
            t0 = e * 512
            for oc in range(nch):
                i = oc % 2
                if oc < MC[e]:
                    k.GQ(lambda g, e=e, oc=oc: g.dma_start(
                            oww[:, 0 : MC[e], :],
                            owT.ap().rearrange("(c p) m -> p c m", p=128)[:, 0 : MC[e],
                                                                         oc * 128 : oc * 128 + 128]),
                         writes=[oww_b], tag="p7_ldow")
                    po7 = 3 * (oc % 2)
                    for mc in range(MC[e]):
                        k.T(lambda t, e=e, mc=mc, t0=t0, b=b, po7=po7: t.matmul(
                                ps[po7][:, 0:512], oww[:, mc, :], aob[:, mc, t0 : t0 + 512],
                                start=(mc == 0), stop=(mc == MC[e] - 1)),
                            reads=[oww_b, aob_b], writes=[psb[po7]], tag="p7_omm")
                k.SQ(lambda s, b=b, oc=oc, t0=t0, i=i: s.dma_start(
                        xt7[i][:], xn1_d.ap()[b, oc * 128 : oc * 128 + 128, t0 : t0 + 512]),
                     reads=[xn1_d_b], writes=[xt7_b[i]], tag="p7_ldxn2")
                if oc < MC[e]:
                    po7 = 3 * (oc % 2)
                    k.V(lambda v, oc=oc, i=i, po7=po7: v.tensor_scalar(g1t[i][:], ps[po7][:, 0:512],
                                                              ob_sb[:, oc : oc + 1], None, ALU.add),
                        reads=[psb[po7]], writes=[g1t_b[i]], tag="p7_gb1", sreads=[ob_b], ptrw=True)
                    k.V(lambda v, i=i: v.tensor_add(g1t[i][:], g1t[i][:], xt7[i][:]),
                        reads=[xt7_b[i]], writes=[g1t_b[i]], tag="p7_gb2")
                    k.S(lambda s, i=i: s.activation(g1t[i][:], g1t[i][:], AF.Silu),
                        reads=[g1t_b[i]], writes=[g1t_b[i]], tag="p7_silu")
                else:
                    k.S(lambda s, i=i: s.activation(g1t[i][:], xt7[i][:], AF.Silu),
                        reads=[xt7_b[i]], writes=[g1t_b[i]], tag="p7_silu")
                k.SQ(lambda s, b=b, oc=oc, t0=t0, i=i: s.dma_start(
                        g1_d.ap()[b, oc * 128 : oc * 128 + 128, t0 : t0 + 512], g1t[i][:]),
                     reads=[g1t_b[i]], writes=[g1_d_b], tag="p7_stg1")
    k.release_arena()
    k.barrier()


def _build_p8(k, env):
    nc = k.nc
    ar = k.ar
    nch = NCH
    ps, psx, psb = env["ps"], env["psx"], env["psb"]
    psAf = env["psAf"]
    onesb, onesb_b = env["onesb"], env["onesb_b"]
    identb, identb_b = env["identb"], env["identb_b"]
    xn1_d, xn1_d_b = env["xn1_d"], env["xn1_d_b"]
    x1_d, x1_d_b = env["x1_d"], env["x1_d_b"]
    g1_d, g1_d_b = env["g1_d"], env["g1_d_b"]
    yout, yout_b = env["yout"], env["yout_b"]
    tril_in = env["tril_in"]

    wqkv_in = k.din("wqkv", (D, 3 * D))
    bqkvp_in = k.din("bqkvp", (128, 24))
    wout_in = k.din("wout", (D, D))
    bout_in = k.din("bout", (128, nch))

    ar.mark()
    wqkv_sb = ar.alloc([128, nch, 3 * D], BF16, "wqkv_sb")
    wout_sb = ar.alloc([128, nch, D], BF16, "wout_sb")
    bqkvp = ar.alloc([128, 24], F32, "bqkvp")
    bout_sb = ar.alloc([128, nch], F32, "bout_sb")
    tril8 = ar.alloc([128, 128], BF16, "tril8")
    qsb = [ar.alloc([64, T], BF16, f"qsb{j}") for j in range(2)]
    ksb = [ar.alloc([64, T], BF16, f"ksb{j}") for j in range(2)]
    vfb = [ar.alloc([64, T], BF16, f"vfb{j}") for j in range(2)]
    vtm8 = [ar.alloc([128, 16, 64], BF16, f"vtm8{j}") for j in range(2)]
    xtl = [ar.alloc([128, nch, 128], BF16, f"xtl{j}") for j in range(2)]
    sail = [ar.alloc([128, nch, 128], BF16, f"sail{j}") for j in range(2)]
    s1t = [ar.alloc([128, nch, 128], F32, f"s1t{j}") for j in range(2)]
    x1t8 = [ar.alloc([128, nch, 128], F32, f"x1t8{j}") for j in range(2)]
    g1t8 = [ar.alloc([128, nch, 128], F32, f"g1t8{j}") for j in range(2)]
    p8sb = [ar.alloc([128, 2048], BF16, f"p8sb{i}") for i in range(2)]
    rz8 = ar.alloc([64, 1024], F32, "rz8")

    wqkv_b = k.buf("wqkv_sb")
    wout_b = k.buf("wout_sb")
    bqkvp_b = k.buf("bqkvp")
    bout_b = k.buf("bout_sb")
    tril8_b = k.buf("tril8")
    qsb_b = [k.buf(f"qsb{j}") for j in range(2)]
    ksb_b = [k.buf(f"ksb{j}") for j in range(2)]
    vfb_b = [k.buf(f"vfb{j}") for j in range(2)]
    vtm8_b = [k.buf(f"vtm8{j}") for j in range(2)]
    xtl_b = [k.buf(f"xtl{j}") for j in range(2)]
    sail_b = [k.buf(f"sail{j}") for j in range(2)]
    s1t_b = [k.buf(f"s1t{j}") for j in range(2)]
    x1t8_b = [k.buf(f"x1t8{j}") for j in range(2)]
    g1t8_b = [k.buf(f"g1t8{j}") for j in range(2)]
    p8sb_b = [k.buf(f"p8sb{i}") for i in range(2)]
    rz8_b = k.buf("rz8")

    k.GQ(lambda g: g.dma_start(wqkv_sb[:], wqkv_in.ap().rearrange("(c p) m -> p c m", p=128)),
         writes=[wqkv_b], tag="ld8_wqkv")
    k.GQ(lambda g: g.dma_start(wout_sb[:], wout_in.ap().rearrange("(c p) m -> p c m", p=128)),
         writes=[wout_b], tag="ld8_wout")
    k.SQ(lambda s: s.dma_start(bqkvp[:], bqkvp_in.ap()), writes=[bqkvp_b], tag="ld8_bq")
    k.SQ(lambda s: s.dma_start(bout_sb[:], bout_in.ap()), writes=[bout_b], tag="ld8_bo")
    k.GQ(lambda g: g.dma_start(tril8[:], tril_in.ap()), writes=[tril8_b], tag="ld8_tril")

    SCALE = (2.0 * D) ** -0.5

    for b in range(B_PER_CORE):
        for pair in range(8):
            blks = (2 * pair, 2 * pair + 1)
            for j, blk in enumerate(blks):
                k.SQ(lambda s, b=b, blk=blk, j=j: s.dma_start(
                        xtl[j][:], xn1_d.ap()[b].rearrange("(c p) t -> p c t", p=128)
                        [:, :, blk * 128 : blk * 128 + 128]),
                     reads=[xn1_d_b], writes=[xtl_b[j]], tag="p8_ldxn")
            for mp in range(24):
                o = mp // 8
                jg = (mp % 8) * 2
                dsts = (qsb, qsb_b) if o == 0 else (ksb, ksb_b) if o == 1 else (vfb, vfb_b)
                pq8 = mp % 2
                for j in range(2):
                    slot = j
                    for kc in range(nch):
                        k.T(lambda t, mp=mp, j=j, kc=kc, slot=slot, pq8=pq8: t.matmul(
                                ps[pq8][:, slot * 128 : slot * 128 + 128],
                                wqkv_sb[:, kc, mp * 128 : mp * 128 + 128],
                                xtl[j][:, kc, :],
                                start=(kc == 0), stop=(kc == nch - 1)),
                            reads=[wqkv_b, xtl_b[j]], writes=[psb[pq8]], tag="p8_qkv")
                    for half64 in range(2):
                        jj = jg + half64
                        k.V(lambda v, mp=mp, j=j, slot=slot, half64=half64, jj=jj, pq8=pq8,
                                   dstw=dsts[0]: v.tensor_scalar(
                                dstw[j][0:64, jj : T : 16],
                                ps[pq8][half64 * 64 : half64 * 64 + 64,
                                      slot * 128 : slot * 128 + 128],
                                bqkvp[half64 * 64 : half64 * 64 + 64, mp : mp + 1],
                                None, ALU.add),
                            reads=[psb[pq8]], writes=[dsts[1][j]], tag="p8_ev",
                            sreads=[bqkvp_b], ptrw=True)
            for j in range(2):
                for sg in range(16):
                    k.T(lambda t, j=j, sg=sg: t.transpose(
                            psx[1][:, sg * 64 : sg * 64 + 64],
                            vfb[j][0:64, sg * 128 : sg * 128 + 128], identb[0:64, 0:64]),
                        reads=[vfb_b[j], identb_b], writes=[psb[1]], tag="p8_vtr")
                k.V(lambda v, j=j: v.tensor_copy(
                        vtm8[j][:].rearrange("p a b -> p (a b)"), psx[1][:, 0:1024]),
                    reads=[psb[1]], writes=[vtm8_b[j]], tag="p8_vev")
            for j in range(2):
                for half in range(2):
                    h0 = half * 1024
                    nsig = 8 * (half + 1)
                    for m in range(nsig // 2):
                        se, so = 2 * m, 2 * m + 1
                        rel_e = max(se * 128, h0) - h0
                        rel_o = max(so * 128, h0) - h0
                        pm = m % 2
                        for sig, rel in ((se, rel_e), (so, rel_o)):
                            pi = sig % 2
                            segs = ([(rel, 512 - rel), (512, 512)] if rel < 512
                                    else [(rel, 1024 - rel)])
                            for o2, w in segs:
                                k.T(lambda t, j=j, sig=sig, pi=pi, o2=o2, w=w, h0=h0: t.matmul(
                                        ps[pi][:, o2 : o2 + w],
                                        ksb[j][0:64, sig * 128 : sig * 128 + 128],
                                        qsb[j][0:64, h0 + o2 : h0 + o2 + w],
                                        start=True, stop=True),
                                    reads=[ksb_b[j], qsb_b[j]], writes=[psb[pi]], tag="p8_sc")
                        k.S(lambda s, pm=pm, rel_e=rel_e: s.activation(
                                p8sb[pm][:, rel_e : 2048], psAf[:, rel_e : 2048],
                                AF.Exp, scale=SCALE),
                            reads=[psb[0], psb[1]], writes=[p8sb_b[pm]], tag="p8_exp")
                        if rel_o > rel_e:
                            k.V(lambda v, pm=pm, rel_e=rel_e, rel_o=rel_o: v.memset(
                                    p8sb[pm][:, 1024 + rel_e : 1024 + rel_o], 0.0),
                                reads=[], writes=[p8sb_b[pm]], tag="p8_inv")
                        if se * 128 >= h0:
                            k.V(lambda v, pm=pm, rel_e=rel_e: v.tensor_mul(
                                    p8sb[pm][:, rel_e : rel_e + 128],
                                    p8sb[pm][:, rel_e : rel_e + 128], tril8[:]),
                                reads=[tril8_b], writes=[p8sb_b[pm]], tag="p8_mask")
                        if so * 128 >= h0:
                            k.V(lambda v, pm=pm, rel_o=rel_o: v.tensor_mul(
                                    p8sb[pm][:, 1024 + rel_o : 1024 + rel_o + 128],
                                    p8sb[pm][:, 1024 + rel_o : 1024 + rel_o + 128], tril8[:]),
                                reads=[tril8_b], writes=[p8sb_b[pm]], tag="p8_mask")
                        for sig, rel, base in ((se, rel_e, 0), (so, rel_o, 1024)):
                            segs = ([(rel, 512 - rel), (512, 512)] if rel < 512
                                    else [(rel, 1024 - rel)])
                            last = sig == nsig - 1
                            for o2, w in segs:
                                k.T(lambda t, pm=pm, o2=o2, w=w, sig=sig, last=last, base=base: t.matmul(
                                        ps[2][:, o2 : o2 + w], onesb[:],
                                        p8sb[pm][:, base + o2 : base + o2 + w],
                                        start=(sig == 0), stop=last),
                                    reads=[onesb_b, p8sb_b[pm]], writes=[psb[2]], tag="p8_z")
                            for o2, w in segs:
                                k.T(lambda t, j=j, sig=sig, pm=pm, o2=o2, w=w, last=last, base=base: t.matmul(
                                        ps[3][0:64, o2 : o2 + w], vtm8[j][:, sig, :],
                                        p8sb[pm][:, base + o2 : base + o2 + w],
                                        start=(sig == 0), stop=last),
                                    reads=[vtm8_b[j], p8sb_b[pm]], writes=[psb[3]], tag="p8_ctx")
                    k.S(lambda s: s.activation(rz8[:], ps[2][0:64, :], AF.Ln),
                        reads=[psb[2]], writes=[rz8_b], tag="p8_lnz")
                    k.S(lambda s: s.activation(rz8[:], rz8[:], AF.Exp, scale=-1.0),
                        reads=[rz8_b], writes=[rz8_b], tag="p8_rz")
                    for jg in range(16):
                        k.V(lambda v, j=j, jg=jg, half=half: v.tensor_mul(
                                sail[j][(jg % 2) * 64 : (jg % 2) * 64 + 64, jg // 2,
                                        half * 64 : half * 64 + 64],
                                ps[3][0:64, jg : 1024 : 16], rz8[0:64, jg : 1024 : 16]),
                            reads=[psb[3], rz8_b], writes=[sail_b[j]], tag="p8_dei")
            for oc in range(nch):
                pw8 = oc % 2
                for j in range(2):
                    slot = j
                    for kc in range(nch):
                        k.T(lambda t, oc=oc, j=j, kc=kc, slot=slot, pw8=pw8: t.matmul(
                                ps[pw8][:, slot * 128 : slot * 128 + 128],
                                wout_sb[:, kc, oc * 128 : oc * 128 + 128],
                                sail[j][:, kc, :],
                                start=(kc == 0), stop=(kc == nch - 1)),
                            reads=[wout_b, sail_b[j]], writes=[psb[pw8]], tag="p8_wo")
                    k.V(lambda v, oc=oc, j=j, slot=slot, pw8=pw8: v.tensor_scalar(
                            s1t[j][:, oc, :],
                            ps[pw8][:, slot * 128 : slot * 128 + 128],
                            bout_sb[:, oc : oc + 1], None, ALU.add),
                        reads=[psb[pw8]], writes=[s1t_b[j]], tag="p8_woev",
                        sreads=[bout_b], ptrw=True)
            for j, blk in enumerate(blks):
                cols = (blk * 128, blk * 128 + 128)
                k.SQ(lambda s, b=b, cols=cols, j=j: s.dma_start(
                        x1t8[j][:], x1_d.ap()[b].rearrange("(c p) t -> p c t", p=128)
                        [:, :, cols[0] : cols[1]]),
                     reads=[x1_d_b], writes=[x1t8_b[j]], tag="p8_ldx1")
                k.SQ(lambda s, b=b, cols=cols, j=j: s.dma_start(
                        g1t8[j][:], g1_d.ap()[b].rearrange("(c p) t -> p c t", p=128)
                        [:, :, cols[0] : cols[1]]),
                     reads=[g1_d_b], writes=[g1t8_b[j]], tag="p8_ldg1")
                fl = lambda ap: ap.rearrange("p a b -> p (a b)")
                k.V(lambda v, j=j, fl=fl: v.tensor_mul(fl(s1t[j][:]), fl(s1t[j][:]), fl(g1t8[j][:])),
                    reads=[g1t8_b[j]], writes=[s1t_b[j]], tag="p8_sg")
                k.V(lambda v, j=j, fl=fl: v.tensor_add(fl(s1t[j][:]), fl(s1t[j][:]), fl(x1t8[j][:])),
                    reads=[x1t8_b[j]], writes=[s1t_b[j]], tag="p8_x2")
                k.SQ(lambda s, b=b, cols=cols, j=j: s.dma_start(
                        yout.ap()[b].rearrange("(c p) t -> p c t", p=128)[:, :, cols[0] : cols[1]],
                        s1t[j][:]),
                     reads=[s1t_b[j]], writes=[yout_b], tag="p8_sty")
    k.release_arena()



def _to_bf16(a):
    return a.astype(ml_dtypes.bfloat16)


def _pack_chan(v):
    return np.ascontiguousarray(np.asarray(v).reshape(-1, 128).T)


def _host_inputs(inputs, core):
    x = inputs["L0_attn_x"]
    d = {}
    xs = x[2 * core : 2 * core + 2]
    d["xfm"] = np.ascontiguousarray(xs.transpose(0, 2, 1))
    d["ones_bf"] = np.ones((128, 128), dtype=ml_dtypes.bfloat16)
    d["ident_bf"] = np.eye(128, dtype=np.float32).astype(ml_dtypes.bfloat16)
    d["tril_bf"] = np.triu(np.ones((128, 128), dtype=np.float32)).astype(ml_dtypes.bfloat16)
    d["wdq"] = np.ascontiguousarray(inputs["L0_attn_W_dq"])
    d["wuqT"] = np.ascontiguousarray(inputs["L0_attn_W_uq"].T)
    d["wuq"] = np.ascontiguousarray(inputs["L0_attn_W_uq"])
    d["wdkvT"] = np.ascontiguousarray(inputs["L0_attn_W_dkv"].T)
    d["wdqT"] = np.ascontiguousarray(inputs["L0_attn_W_dq"].T)
    d["wuk"] = np.ascontiguousarray(inputs["L0_attn_W_uk"])
    d["wuv"] = np.ascontiguousarray(inputs["L0_attn_W_uv"])
    d["woT"] = np.ascontiguousarray(inputs["L0_attn_W_o"].T)
    d["sigsrawT"] = np.ascontiguousarray(inputs["L0_ffn_signatures_raw"].T)
    d["knots"] = np.ascontiguousarray(inputs["L0_ffn_knot_values"])
    d["temp"] = np.ascontiguousarray(inputs["L0_ffn_temperature"])
    d["cw1"] = np.ascontiguousarray(inputs["L0_ffn_compress_w1"])
    d["cb1"] = _pack_chan(inputs["L0_ffn_compress_b1"])
    d["cw2"] = np.ascontiguousarray(inputs["L0_ffn_compress_w2"])
    d["cb2"] = np.ascontiguousarray(inputs["L0_ffn_compress_b2"])
    t65 = np.concatenate([
        inputs["L0_ffn_spline_w1"].reshape(64, 32),
        inputs["L0_ffn_spline_b1"].reshape(64, 16),
        inputs["L0_ffn_spline_w2"].reshape(64, 16),
        inputs["L0_ffn_spline_b2"].reshape(64, 1)], axis=1)
    d["tab65"] = np.ascontiguousarray(np.concatenate([t65, np.zeros_like(t65)], axis=0))
    d["oscale"] = np.ascontiguousarray(inputs["L0_ffn_output_scale"])
    d["dirs"] = np.ascontiguousarray(np.concatenate(
        [inputs["L0_ffn_directions"], np.zeros_like(inputs["L0_ffn_directions"])], axis=0))
    d["gamma0"] = _pack_chan(inputs["L0_ffn_norm_gamma"])
    d["beta0"] = _pack_chan(inputs["L0_ffn_norm_beta"])
    d["iota8"] = np.tile(np.arange(8, dtype=np.float32), (128, 1))
    d["iota8m"] = np.tile(np.arange(8, dtype=np.float32) - 8.0, (128, 1))
    d["iota64"] = np.tile(np.arange(64, dtype=np.float32), (128, 1))
    d["iotad8"] = np.tile((np.arange(64) // 8).astype(np.float32), (128, 1))
    d["iota64m"] = np.tile(np.arange(64, dtype=np.float32) - 64.0, (128, 1))
    d["qwT"] = np.ascontiguousarray(inputs["L1_ffn_q_w"].T)
    d["kwT"] = np.ascontiguousarray(inputs["L1_ffn_k_w"].T)
    d["vwT"] = np.ascontiguousarray(inputs["L1_ffn_v_w"].T)
    d["owT"] = np.ascontiguousarray(inputs["L1_ffn_o_w"].T)
    d["qb"] = _pack_chan(inputs["L1_ffn_q_b"])
    d["kb"] = _pack_chan(inputs["L1_ffn_k_b"])
    d["ob"] = _pack_chan(inputs["L1_ffn_o_b"])
    vbr = np.zeros((128, 1024), dtype=np.float32)
    vbr[0] = inputs["L1_ffn_v_b"]
    d["vbrow"] = vbr
    d["nw"] = _pack_chan(inputs["L1_ffn_norm_w"])
    d["nb"] = _pack_chan(inputs["L1_ffn_norm_b"])
    perm = np.empty(3072, dtype=np.int64)
    idx = 0
    for o in range(3):
        for jj in range(16):
            base = 192 * jj + 64 * o
            perm[idx : idx + 64] = np.arange(base, base + 64)
            idx += 64
    d["wqkv"] = np.ascontiguousarray(inputs["L1_attn_W_qkv"][:, perm])
    d["bqkvp"] = _pack_chan(inputs["L1_attn_b_qkv"][perm])
    d["wout"] = np.ascontiguousarray(inputs["L1_attn_W_out"])
    d["bout"] = _pack_chan(inputs["L1_attn_b_out"])
    return d


def run_dev(inputs, dumps=(), upto=99, cores=(0,)):
    nc = bass.Bass("TRN2", target_bir_lowering=False)
    build(nc, dumps=dumps, upto=upto)
    in_maps = [_host_inputs(inputs, c) for c in cores]
    res = run_bass_kernel_spmd(nc, in_maps, core_ids=list(range(len(cores))))
    return res.results


def kernel(**inputs):
    nc = bass.Bass("TRN2", target_bir_lowering=False)
    build(nc)
    in_maps = [_host_inputs(inputs, c) for c in range(NCORES)]
    res = run_bass_kernel_spmd(nc, in_maps, core_ids=list(range(NCORES)))
    out = np.empty((16, T, D), dtype=np.float32)
    for c in range(NCORES):
        y = res.results[c]["yout"]
        out[2 * c : 2 * c + 2] = y.transpose(0, 2, 1)
    return out


# version 05
# speedup vs baseline: 1.1110x; 1.0003x over previous
import numpy as np
import ml_dtypes
import concourse.bass as bass
import concourse.mybir as mybir
from concourse.bass_utils import run_bass_kernel_spmd

F32 = mybir.dt.float32
BF16 = mybir.dt.bfloat16
U8 = mybir.dt.uint8
AF = mybir.ActivationFunctionType
ALU = mybir.AluOpType

NCORES = 8
B_PER_CORE = 2
T = 2048
D = 1024
NCH = D // 128
TOK = B_PER_CORE * T
BLK = 512
EPS = 1e-5

ENGINES = ("T", "V", "S", "SQ", "GQ")
SERIALIZE_TAGS = ("am_", "sp_", "cal_")
_ENG_BLOCK = {"T": "tensor", "V": "vector", "S": "scalar", "SQ": "sync", "GQ": "gpsimd"}
_DMA_ENGS = ("SQ", "GQ")


def _prod(xs):
    p = 1
    for x in xs:
        p *= x
    return p


class Buf:
    __slots__ = ("name", "writer", "readers", "ptrw")

    def __init__(self, name=""):
        self.name = name
        self.writer = None
        self.readers = []
        self.ptrw = False


class Plan:
    R_LANES = 12

    def __init__(self):
        self.ops = {e: [] for e in ENGINES}
        self.nop_emit = {}

    def op(self, eng, emit, reads=(), writes=(), tag="", sreads=(), ptrw=False):
        waits = {}

        def need(dep, allow_self=False):
            if dep is None:
                return
            e2, i2 = dep
            if e2 == eng and not allow_self:
                return
            key = (e2, i2 % self.R_LANES) if e2 in _DMA_ENGS else e2
            if waits.get(key, -1) < i2:
                waits[key] = i2

        for b in sreads:
            need(b.writer, allow_self=True)
        reads = list(reads) + list(sreads)
        for b in reads:
            need(b.writer, allow_self=b.ptrw)
        for b in writes:
            need(b.writer)
            for r in b.readers:
                need(r)
        idx = len(self.ops[eng])
        if SERIALIZE_TAGS and eng == "V" and idx > 0 and any(tag.startswith(p) for p in SERIALIZE_TAGS):
            if waits.get(eng, -1) < idx - 1:
                waits[eng] = idx - 1
        if waits.get(eng, -1) >= idx:
            del waits[eng]
        return self._append(eng, emit, waits, tag, reads, writes, ptrw)

    def _op(self, eng, emit, waits, tag=""):
        self.ops[eng].append({"emit": emit, "waits": waits, "tag": tag})
        return (eng, len(self.ops[eng]) - 1)

    def _append(self, eng, emit, waits, tag, reads, writes, ptrw):
        idx = len(self.ops[eng])
        self.ops[eng].append({"emit": emit, "waits": waits, "tag": tag})
        me = (eng, idx)
        for b in reads:
            b.readers.append(me)
        for b in writes:
            b.writer = me
            b.readers = []
            b.ptrw = ptrw
        return me

    def last_ops(self):
        out = []
        for e in ENGINES:
            n = len(self.ops[e])
            if not n:
                continue
            if e in _DMA_ENGS:
                for i in range(max(0, n - self.R_LANES), n):
                    out.append((e, i))
            else:
                out.append((e, n - 1))
        return out

    def finalize(self, nc):
        R = self.R_LANES
        targets = {e: set() for e in ENGINES}

        def tgt_of(key, i2):
            e2 = key if isinstance(key, str) else key[0]
            return e2, i2

        for e in ENGINES:
            for o in self.ops[e]:
                for key, i2 in o["waits"].items():
                    e2, i2 = tgt_of(key, i2)
                    targets[e2].add(i2)
        for e, i in self.last_ops():
            targets[e].add(i)
        for e in _DMA_ENGS:
            targets[e].update(range(len(self.ops[e])))
        inc_points = {e: sorted(targets[e]) for e in ENGINES}
        val_of = {}
        for e in ENGINES:
            vals = {}
            if e in _DMA_ENGS:
                for i in range(len(self.ops[e])):
                    vals[i] = (i // R + 1) * 16
            else:
                rank = 0
                pts = set(inc_points[e])
                for i in range(len(self.ops[e])):
                    if i in pts:
                        rank += 1
                    vals[i] = rank
            val_of[e] = vals

        sems = {}
        for e in ENGINES:
            if e in _DMA_ENGS:
                sems[e] = [nc.alloc_semaphore(f"sem_{e}_{j}") for j in range(R)]
            else:
                sems[e] = nc.alloc_semaphore(f"sem_{e}")

        def sem_and_val(e2, i2):
            if e2 in _DMA_ENGS:
                return sems[e2][i2 % R], val_of[e2][i2]
            return sems[e2], val_of[e2][i2]

        plan = self

        def body_for(e):
            def body(engine):
                waited = {}
                pts = set(inc_points[e])
                for i, o in enumerate(plan.ops[e]):
                    for key, i2 in sorted(o["waits"].items(), key=str):
                        e2 = key if isinstance(key, str) else key[0]
                        sem, v = sem_and_val(e2, i2)
                        wkey = (e2, i2 % R) if e2 in _DMA_ENGS else e2
                        if v > waited.get(wkey, 0):
                            engine.wait_ge(sem, v)
                            waited[wkey] = v
                    inst = o["emit"](engine)
                    if i in pts:
                        if inst is None:
                            raise RuntimeError(f"op {e}:{i} tag={o['tag']} returned None")
                        if e in _DMA_ENGS:
                            inst.then_inc(sems[e][i % R], 16)
                        else:
                            inst.then_inc(sems[e], 1)
                if e == "SQ":
                    for e2, i2 in plan.last_ops():
                        if e2 == "SQ" and i2 == len(plan.ops["SQ"]) - 1:
                            continue
                        sem, v = sem_and_val(e2, i2)
                        wkey = (e2, i2 % R) if e2 in _DMA_ENGS else e2
                        if v > waited.get(wkey, 0):
                            engine.wait_ge(sem, v)
                            waited[wkey] = v
            return body

        with nc.Block() as block:
            for e in ENGINES:
                if not self.ops[e] and e != "SQ":
                    continue
                getattr(block, _ENG_BLOCK[e])(body_for(e))


class Arena:
    def __init__(self, nc, nbytes=200 * 1024):
        self.pool = nc.alloc_sbuf_tensor("pool", [128, nbytes], U8)
        self.cap = nbytes
        self.off = 0
        self.stack = []
        self.hi = 0

    def alloc(self, shape, dt, name=""):
        isz = {F32: 4, BF16: 2, U8: 1}[dt]
        n = _prod(shape[1:]) * isz
        off = (self.off + 31) & ~31
        assert off + n <= self.cap, f"SBUF arena overflow: {name} needs {n}B at {off}"
        self.off = off + n
        self.hi = max(self.hi, self.off)
        ap = self.pool.ap()[: shape[0], off : off + n].bitcast(dt)
        if len(shape) > 2:
            dims = " ".join(f"d{i}" for i in range(1, len(shape)))
            kw = {f"d{i}": shape[i] for i in range(1, len(shape) - 1)}
            ap = ap.rearrange(f"p ({dims}) -> p {dims}", **kw)
        return ap

    def mark(self):
        self.stack.append(self.off)

    def release(self):
        self.off = self.stack.pop()


class K:
    def __init__(self, nc, dumps=()):
        self.nc = nc
        self.p = Plan()
        self.ar = Arena(nc)
        self.dumps = set(dumps)
        self.dump_names = []
        self.inp = {}
        self.hazard = []

    def buf(self, name=""):
        b = Buf(name)
        b.readers = list(self.hazard)
        return b

    def release_arena(self):
        self.ar.release()
        self.hazard = self.p.last_ops()

    def barrier(self):
        p = self.p
        join_waits = {}
        for e, i in p.last_ops():
            key = (e, i % p.R_LANES) if e in _DMA_ENGS else e
            join_waits[key] = max(join_waits.get(key, -1), i)
        ji = p._op("SQ", lambda s: s.nop(), join_waits, tag="bar_join")
        for e in ENGINES:
            if e == "SQ" or not p.ops[e]:
                continue
            p._op(e, lambda eng: eng.nop(), {"SQ": ji[1]}, tag="bar_leaf")
        self.hazard = p.last_ops()

    def din(self, name, shape, dt=F32):
        t = self.nc.dram_tensor(name, list(shape), dt, kind="ExternalInput")
        self.inp[name] = t
        return t

    def dscr(self, name, shape, dt=F32):
        kind = "ExternalOutput" if name in self.dumps else "Internal"
        if kind == "ExternalOutput":
            self.dump_names.append(name)
        return self.nc.dram_tensor(name, list(shape), dt, kind=kind)

    def dump(self, name, ap, reads, shape, dt=F32):
        if name not in self.dumps:
            return
        t = self.nc.dram_tensor("dump_" + name, list(shape), dt, kind="ExternalOutput")
        self.dump_names.append("dump_" + name)
        self.p.op("SQ", lambda s, t=t, ap=ap: s.dma_start(t.ap(), ap), reads=list(reads), tag="dump_" + name)

    def T(self, emit, reads=(), writes=(), tag=""):
        return self.p.op("T", emit, reads, writes, tag)

    def V(self, emit, reads=(), writes=(), tag="", sreads=(), ptrw=False):
        return self.p.op("V", emit, reads, writes, tag, sreads, ptrw)

    def S(self, emit, reads=(), writes=(), tag="", sreads=(), ptrw=False):
        return self.p.op("S", emit, reads, writes, tag, sreads, ptrw)

    def SQ(self, emit, reads=(), writes=(), tag=""):
        return self.p.op("SQ", emit, reads, writes, tag)

    def GQ(self, emit, reads=(), writes=(), tag=""):
        return self.p.op("GQ", emit, reads, writes, tag)



def build(nc, dumps=(), upto=99):
    k = K(nc, dumps)
    ar = k.ar
    nch = NCH

    xfm = k.din("xfm", (B_PER_CORE, D, T))
    xfm_b = Buf("xfm")
    ones_in = k.din("ones_bf", (128, 128), BF16)
    ident_in = k.din("ident_bf", (128, 128), BF16)

    wdq = k.din("wdq", (128, D))
    wuqT = k.din("wuqT", (128, D))
    wuq = k.din("wuq", (D, 128))
    wdkvT = k.din("wdkvT", (D, 128))
    wdqT = k.din("wdqT", (D, 128))
    wuk = k.din("wuk", (D, 128))
    wuv = k.din("wuv", (D, 128))
    woT = k.din("woT", (D, D))

    yout = nc.dram_tensor("yout", [B_PER_CORE, D, T], F32, kind="ExternalOutput")
    yout_b = Buf("yout")

    onesb = ar.alloc([128, 128], BF16, "onesb")
    identb = ar.alloc([128, 128], BF16, "identb")
    epsc = ar.alloc([128, 1], F32, "epsc")
    epsc_b = Buf("epsc")
    junk = ar.alloc([128, 8], F32, "junk")
    ar.mark()
    ckv = ar.alloc([128, TOK], BF16, "ckv")
    q1 = ar.alloc([128, TOK], BF16, "q1")
    ckv_tm = ar.alloc([128, 32, 128], BF16, "ckv_tm")
    bq = ar.alloc([128, 16, 128], BF16, "bq")
    v2 = ar.alloc([128, D], BF16, "v2")
    ohs_fm = ar.alloc([64, TOK], BF16, "ohs_fm")
    ohs_fm_b = Buf("ohs_fm")
    ar.mark()
    xn0 = ar.alloc([128, nch, TOK], BF16, "xn0")
    xn0_b = Buf("xn0")
    onesb_b = Buf("onesb")
    identb_b = Buf("identb")
    ckv_b = Buf("ckv")
    q1_b = Buf("q1")
    ckv_tm_b = Buf("ckv_tm")
    bq_b = Buf("bq")
    v2_b = Buf("v2")

    psA = nc.alloc_psum_tensor("psA", [128, 2048], F32)
    psB = nc.alloc_psum_tensor("psB", [128, 1024], F32)
    psC = nc.alloc_psum_tensor("psC", [128, 1024], F32)
    ps = [psA.ap()[:, 0:1024], psA.ap()[:, 1024:2048], psB.ap(), psC.ap()]
    psx = [psA.ap().bitcast(BF16)[:, 0:2048], psA.ap().bitcast(BF16)[:, 2048:4096],
           psB.ap().bitcast(BF16), psC.ap().bitcast(BF16)]
    psAf = psA.ap()
    psb = [Buf(f"ps{i}") for i in range(4)]

    k.GQ(lambda g: g.dma_start(onesb[:], ones_in.ap()), writes=[onesb_b], tag="ld_ones")
    k.GQ(lambda g: g.dma_start(identb[:], ident_in.ap()), writes=[identb_b], tag="ld_ident")

    k.V(lambda v: v.memset(epsc[:], EPS), writes=[epsc_b], tag="epsc")
    k.p.nop_emit["V"] = lambda v: v.memset(junk[:], 0.0)
    k.p.nop_emit["S"] = lambda s: s.memzero(junk[:])

    def wload(name, t, shape, src_ap=None, tag=""):
        sb = ar.alloc(shape, BF16, name)
        b = k.buf(name)
        src = src_ap if src_ap is not None else t.ap()
        k.GQ(lambda g, sb=sb, src=src: g.dma_start(sb[:], src), writes=[b], tag=tag or ("ld_" + name))
        return sb, b

    def chunk_rows(t, m):
        return t.ap().rearrange("(c p) m -> p c m", p=128)

    wdkvT_sb, wdkvT_sbb = wload("wdkvT_sb", wdkvT, [128, nch, 128], chunk_rows(wdkvT, 128))
    wdqT_sb, wdqT_sbb = wload("wdqT_sb", wdqT, [128, nch, 128], chunk_rows(wdqT, 128))

    ar.mark()
    wdq_sb, wdq_sbb = wload("wdq_sb", wdq, [128, D])
    wuqT_sb, wuqT_sbb = wload("wuqT_sb", wuqT, [128, D])
    wuq_sb, wuq_sbb = wload("wuq_sb", wuq, [128, nch, 128], chunk_rows(wuq, 128))
    wuk_sb, wuk_sbb = wload("wuk_sb", wuk, [128, nch, 128], chunk_rows(wuk, 128))
    wuv_sb, wuv_sbb = wload("wuv_sb", wuv, [128, nch, 128], chunk_rows(wuv, 128))
    woT_sb, woT_sbb = wload("woT_sb", woT, [128, nch, D], chunk_rows(woT, D))

    d1t = ar.alloc([128, nch, D], BF16, "d1t")
    d1t_b = k.buf("d1t")
    for jc in range(nch):
        for n2 in range(2):
            k.T(lambda t, jc=jc, n2=n2: t.matmul(
                    ps[0][:, n2 * 512 : n2 * 512 + 512],
                    wuqT_sb[:, jc * 128 : jc * 128 + 128],
                    wdq_sb[:, n2 * 512 : n2 * 512 + 512],
                    start=True, stop=True),
                reads=[wuqT_sbb, wdq_sbb], writes=[psb[0]], tag="d1t_mm")
        k.V(lambda v, jc=jc: v.tensor_copy(d1t[:, jc, :], ps[0][:, 0:1024]),
            reads=[psb[0]], writes=[d1t_b], tag="d1t_ev")

    a_sb = ar.alloc([128, nch, 128], BF16, "a_sb")
    a_b = k.buf("a_sb")
    for dc in range(nch):
        for jc in range(nch):
            k.T(lambda t, dc=dc, jc=jc: t.matmul(
                    ps[1][:, 0:128],
                    d1t[:, jc, dc * 128 : dc * 128 + 128],
                    wuk_sb[:, jc, :],
                    start=(jc == 0), stop=(jc == nch - 1)),
                reads=[d1t_b, wuk_sbb], writes=[psb[1]], tag="a_mm")
        k.V(lambda v, dc=dc: v.tensor_copy(a_sb[:, dc, :], ps[1][:, 0:128]),
            reads=[psb[1]], writes=[a_b], tag="a_ev")

    for h in range(16):
        po = (h % 2) * 64
        c = h // 2
        k.T(lambda t, h=h, po=po, c=c: t.matmul(
                ps[2][:, 0:128],
                wuq_sb[po : po + 64, c, :],
                a_sb[po : po + 64, c, :],
                start=True, stop=True),
            reads=[wuq_sbb, a_b], writes=[psb[2]], tag="bq_mm")
        k.V(lambda v, h=h: v.tensor_copy(bq[:, h, :], ps[2][:, 0:128]),
            reads=[psb[2]], writes=[bq_b], tag="bq_ev")

    for n2 in range(2):
        for uc in range(nch):
            k.T(lambda t, n2=n2, uc=uc: t.matmul(
                    ps[3][:, 0:512],
                    wuv_sb[:, uc, :],
                    woT_sb[:, uc, n2 * 512 : n2 * 512 + 512],
                    start=(uc == 0), stop=(uc == nch - 1)),
                reads=[wuv_sbb, woT_sbb], writes=[psb[3]], tag="v2_mm")
        k.V(lambda v, n2=n2: v.tensor_copy(v2[:, n2 * 512 : n2 * 512 + 512], ps[3][:, 0:512]),
            reads=[psb[3]], writes=[v2_b], tag="v2_ev")

    k.dump("d1t", d1t[:], [d1t_b], (128, nch, D), BF16)
    k.dump("a", a_sb[:], [a_b], (128, nch, 128), BF16)
    k.dump("bq", bq[:], [bq_b], (128, 16, 128), BF16)
    k.dump("v2", v2[:], [v2_b], (128, D), BF16)
    k.release_arena()
    k.barrier()

    mur2_d = k.dscr("mur2_d", (2, TOK // BLK, 128, BLK), F32)
    mur2_d_b = Buf("mur2_d")
    nblk = TOK // BLK
    ar.mark()
    xt = [ar.alloc([128, nch, BLK], F32, f"xt{i}") for i in range(2)]
    xbt = [ar.alloc([128, nch, BLK], BF16, f"xbt{i}") for i in range(2)]
    sqt = ar.alloc([128, nch, BLK], BF16, "sqt")
    tmp512 = ar.alloc([128, BLK], F32, "tmp512")
    mut = [ar.alloc([128, BLK], F32, f"mut{i}") for i in range(2)]
    rrt = [ar.alloc([128, BLK], F32, f"rrt{i}") for i in range(2)]
    xt_b = [k.buf(f"xt{i}") for i in range(2)]
    xbt_b = [k.buf(f"xbt{i}") for i in range(2)]
    sqt_b = k.buf("sqt")
    tmp512_b = k.buf("tmp512")
    mut_b = [k.buf(f"mut{i}") for i in range(2)]
    rrt_b = [k.buf(f"rrt{i}") for i in range(2)]

    for blk in range(nblk):
        b, t0 = divmod(blk * BLK, T)
        g = blk * BLK
        i = blk % 2
        src = xfm.ap()[b].rearrange("(c p) t -> p c t", p=128)[:, :, t0 : t0 + BLK]
        k.SQ(lambda s, src=src, i=i: s.dma_start(xt[i][:], src),
             reads=[xfm_b], writes=[xt_b[i]], tag="p1_ldx")
        k.GQ(lambda gq, src=src, i=i: gq.dma_start(xbt[i][:], src),
             reads=[xfm_b], writes=[xbt_b[i]], tag="p1_ldxb")
        k.V(lambda v, i=i: v.tensor_mul(sqt[:].rearrange("p c t -> p (c t)"),
                                        xbt[i][:].rearrange("p c t -> p (c t)"),
                                        xbt[i][:].rearrange("p c t -> p (c t)")),
            reads=[xbt_b[i]], writes=[sqt_b], tag="p1_sq")
        pst = (blk % 2) * 2
        for c in range(nch):
            k.T(lambda t, c=c, i=i, pst=pst: t.matmul(ps[pst][:, 0:BLK], onesb[:], xbt[i][:, c, :],
                                             start=(c == 0), stop=(c == nch - 1)),
                reads=[xbt_b[i], onesb_b], writes=[psb[pst]], tag="p1_mmS")
        for c in range(nch):
            k.T(lambda t, c=c, pst=pst: t.matmul(ps[pst + 1][:, 0:BLK], onesb[:], sqt[:, c, :],
                                        start=(c == 0), stop=(c == nch - 1)),
                reads=[sqt_b, onesb_b], writes=[psb[pst + 1]], tag="p1_mmS2")
        mu_ap = mut[i][:]
        r_ap = rrt[i][:]
        k.V(lambda v, mu_ap=mu_ap, pst=pst: v.tensor_scalar_mul(mu_ap, ps[pst][:, 0:BLK], 1.0 / D),
            reads=[psb[pst]], writes=[mut_b[i]], tag="p1_mu")
        k.V(lambda v, mu_ap=mu_ap: v.tensor_mul(tmp512[:], mu_ap, mu_ap),
            reads=[mut_b[i]], writes=[tmp512_b], tag="p1_musq")
        k.V(lambda v, r_ap=r_ap, pst=pst: v.scalar_tensor_tensor(r_ap, ps[pst + 1][:, 0:BLK], 1.0 / D,
                                                        tmp512[:], ALU.mult, ALU.subtract),
            reads=[psb[pst + 1], tmp512_b], writes=[rrt_b[i]], tag="p1_var")
        k.S(lambda s, r_ap=r_ap: s.activation(r_ap, r_ap, AF.Ln, bias=epsc[:]),
            reads=[rrt_b[i], epsc_b], writes=[rrt_b[i]], tag="p1_ln")
        k.S(lambda s, r_ap=r_ap: s.activation(r_ap, r_ap, AF.Exp, scale=-0.5),
            reads=[rrt_b[i]], writes=[rrt_b[i]], tag="p1_exp")
        k.SQ(lambda s, blk=blk, i=i: s.dma_start(mur2_d.ap()[0, blk], mut[i][:, :]),
             reads=[mut_b[i]], writes=[mur2_d_b], tag="p1_stmu")
        k.SQ(lambda s, blk=blk, i=i: s.dma_start(mur2_d.ap()[1, blk], rrt[i][:, :]),
             reads=[rrt_b[i]], writes=[mur2_d_b], tag="p1_str")
        for c in range(nch):
            k.V(lambda v, c=c, i=i, mu_ap=mu_ap: v.tensor_sub(xt[i][:, c, :], xt[i][:, c, :], mu_ap),
                reads=[mut_b[i]], writes=[xt_b[i]], tag="p1_sub")
            k.V(lambda v, c=c, i=i, g=g, r_ap=r_ap: v.tensor_mul(
                    xn0[:, c, g : g + BLK], xt[i][:, c, :], r_ap),
                reads=[xt_b[i], rrt_b[i]], writes=[xn0_b], tag="p1_xn")

    k.release_arena()
    k.barrier()

    k.dump("xn0", xn0[:], [xn0_b], (128, nch, TOK), BF16)

    if upto <= 1:
        k.p.finalize(nc)
        return k

    for nt in range(TOK // 512):
        pa = (nt % 2) * 2
        pb2 = (nt % 2) * 2 + 1
        for c in range(nch):
            k.T(lambda t, nt=nt, c=c, pa=pa: t.matmul(
                    ps[pa][:, 0:512], wdkvT_sb[:, c, :], xn0[:, c, nt * 512 : nt * 512 + 512],
                    start=(c == 0), stop=(c == nch - 1)),
                reads=[wdkvT_sbb, xn0_b], writes=[psb[pa]], tag="ckv_mm")
        k.V(lambda v, nt=nt, pa=pa: v.tensor_copy(ckv[:, nt * 512 : nt * 512 + 512], ps[pa][:, 0:512]),
            reads=[psb[pa]], writes=[ckv_b], tag="ckv_ev")
        for c in range(nch):
            k.T(lambda t, nt=nt, c=c, pb2=pb2: t.matmul(
                    ps[pb2][:, 0:512], wdqT_sb[:, c, :], xn0[:, c, nt * 512 : nt * 512 + 512],
                    start=(c == 0), stop=(c == nch - 1)),
                reads=[wdqT_sbb, xn0_b], writes=[psb[pb2]], tag="q1_mm")
        k.V(lambda v, nt=nt, pb2=pb2: v.tensor_copy(q1[:, nt * 512 : nt * 512 + 512], ps[pb2][:, 0:512]),
            reads=[psb[pb2]], writes=[q1_b], tag="q1_ev")

    for q4 in range(8):
        pq = 2 + (q4 % 2)
        for j in range(4):
            st = q4 * 4 + j
            k.T(lambda t, st=st, j=j, pq=pq: t.transpose(
                    psx[pq][:, j * 128 : j * 128 + 128],
                    ckv[:, st * 128 : st * 128 + 128], identb[:]),
                reads=[ckv_b, identb_b], writes=[psb[pq]], tag="ckvtm_tr")
        k.V(lambda v, q4=q4, pq=pq: v.tensor_copy(
                ckv_tm[:].rearrange("p a b -> p (a b)")[:, q4 * 512 : q4 * 512 + 512],
                psx[pq][:, 0:512]),
            reads=[psb[pq]], writes=[ckv_tm_b], tag="ckvtm_ev")

    k.dump("ckv", ckv[:], [ckv_b], (128, TOK), BF16)
    k.dump("q1", q1[:], [q1_b], (128, TOK), BF16)
    k.dump("ckv_tm", ckv_tm[:], [ckv_tm_b], (128, 32, 128), BF16)

    if upto <= 2:
        k.p.finalize(nc)
        return k

    sigsrawT = k.din("sigsrawT", (D, 64))
    knots_in = k.din("knots", (8,))
    temp_in = k.din("temp", (1,))
    cw1_in = k.din("cw1", (D, 256))
    cb1_in = k.din("cb1", (128, 2))
    cw2_in = k.din("cw2", (256, 2))
    cb2_in = k.din("cb2", (2,))
    tab65_in = k.din("tab65", (128, 65))
    osc_in = k.din("oscale", (1,))
    iota8_in = k.din("iota8", (128, 8))
    i8m_in = k.din("iota8m", (128, 8))
    iota64_in = k.din("iota64", (128, 64))
    iotad8_in = k.din("iotad8", (128, 64))
    i64m_in = k.din("iota64m", (128, 64))

    ar.mark()
    srw = ar.alloc([128, nch, 64], F32, "srw")
    m72 = ar.alloc([128, nch, 72], BF16, "m72")
    qtmp = ar.alloc([128, nch, 64], F32, "qtmp")
    a1 = ar.alloc([128, nch, 32], F32, "a1")
    a2 = ar.alloc([128, nch, 16], F32, "a2")
    a3 = ar.alloc([128, nch, 8], F32, "a3")
    scs = ar.alloc([128, 32, 72], F32, "scs")
    gacc = ar.alloc([128, 32 * 72], F32, "gacc")
    camp = ar.alloc([128, 32 * 72], F32, "camp")
    gath = gacc[:, 0 : 32 * 65].rearrange("p (a b) -> p a b", a=32)
    h1sb = ar.alloc([128, 2, TOK], BF16, "h1sb")
    oh01_tm = ar.alloc([128, 32, 64], BF16, "oh01_tm")
    oh01_fm = ar.alloc([64, TOK], BF16, "oh01_fm")
    ohm_tm = ar.alloc([128, 64], BF16, "ohm_tm")
    cw1_sb = ar.alloc([128, nch, 256], BF16, "cw1_sb")
    cb1_sb = ar.alloc([128, 2], F32, "cb1_sb")
    cw2_sb = ar.alloc([128, 2, 2], BF16, "cw2_sb")
    cb2_sb = ar.alloc([128, 2], F32, "cb2_sb")
    tab65_sb = ar.alloc([128, 65], BF16, "tab65_sb")
    kb = ar.alloc([128, 8], F32, "kb")
    dk = ar.alloc([128, 7], F32, "dk")
    rt = ar.alloc([128, 1], F32, "rt")
    oscb = ar.alloc([128, 1], F32, "oscb")
    iota8c = ar.alloc([128, 8], F32, "iota8c")
    i8mc = ar.alloc([128, 8], F32, "i8mc")
    iota64c = ar.alloc([128, 64], F32, "iota64c")
    iotad8c = ar.alloc([128, 64], F32, "iotad8c")
    i64mc = ar.alloc([128, 64], F32, "i64mc")
    u_sb = ar.alloc([128, 512], F32, "u_sb")
    e_sb = ar.alloc([128, 512], F32, "e_sb")
    compsb = ar.alloc([128, 64], F32, "compsb")
    sm1 = ar.alloc([128, 64], F32, "sm1")
    sm2 = ar.alloc([128, 64], F32, "sm2")
    sm3 = ar.alloc([128, 16], F32, "sm3")
    clv = ar.alloc([128, 1], F32, "clv")
    mgv = ar.alloc([128, 1], F32, "mgv")

    srw_b = k.buf("srw"); m72_b = k.buf("m72"); qtmp_b = k.buf("qtmp")
    a1_b = k.buf("a1"); a2_b = k.buf("a2"); a3_b = k.buf("a3")
    scs_b = k.buf("scs"); gacc_b = k.buf("gacc"); camp_b = k.buf("camp")
    gath_b = gacc_b; cacc_b = gacc_b
    h1_b = k.buf("h1sb")
    oh01_tm_b = k.buf("oh01_tm"); oh01_fm_b = k.buf("oh01_fm"); ohm_tm_b = k.buf("ohm_tm")
    cw1_b = k.buf("cw1_sb"); cb1_b = k.buf("cb1_sb"); cw2_b = k.buf("cw2_sb")
    cb2_b = k.buf("cb2_sb"); tab65_b = k.buf("tab65_sb")
    kb_b = k.buf("kb"); dk_b = k.buf("dk"); rt_b = k.buf("rt"); oscb_b = k.buf("oscb")
    iota8c_b = k.buf("iota8c"); i8mc_b = k.buf("i8mc"); iota64c_b = k.buf("iota64c")
    iotad8c_b = k.buf("iotad8c"); i64mc_b = k.buf("i64mc")
    u_b = k.buf("u_sb"); e_b = k.buf("e_sb"); comp_b = k.buf("compsb")
    sm1_b = k.buf("sm1"); sm2_b = k.buf("sm2"); sm3_b = k.buf("sm3")
    clv_b = k.buf("clv"); mgv_b = k.buf("mgv")

    k.SQ(lambda s: s.dma_start(srw[:], sigsrawT.ap().rearrange("(c p) m -> p c m", p=128)),
         writes=[srw_b], tag="ld_srw")
    k.SQ(lambda s: s.dma_start(kb[:], knots_in.ap().unsqueeze(0).partition_broadcast(128)),
         writes=[kb_b], tag="ld_knots")
    k.SQ(lambda s: s.dma_start(rt[:], temp_in.ap().unsqueeze(0).partition_broadcast(128)),
         writes=[rt_b], tag="ld_temp")
    k.SQ(lambda s: s.dma_start(oscb[:], osc_in.ap().unsqueeze(0).partition_broadcast(128)),
         writes=[oscb_b], tag="ld_osc")
    k.GQ(lambda g: g.dma_start(cw1_sb[:], cw1_in.ap().rearrange("(c p) m -> p c m", p=128)),
         writes=[cw1_b], tag="ld_cw1")
    k.SQ(lambda s: s.dma_start(cb1_sb[:], cb1_in.ap()), writes=[cb1_b], tag="ld_cb1")
    k.GQ(lambda g: g.dma_start(cw2_sb[:], cw2_in.ap().rearrange("(c p) m -> p c m", p=128)),
         writes=[cw2_b], tag="ld_cw2")
    k.SQ(lambda s: s.dma_start(cb2_sb[:], cb2_in.ap().unsqueeze(0).partition_broadcast(128)),
         writes=[cb2_b], tag="ld_cb2")
    k.GQ(lambda g: g.dma_start(tab65_sb[:], tab65_in.ap()), writes=[tab65_b], tag="ld_tab")
    for nm, sb, bb, t_in in (("i8", iota8c, iota8c_b, iota8_in), ("i8m", i8mc, i8mc_b, i8m_in),
                             ("i64", iota64c, iota64c_b, iota64_in),
                             ("id8", iotad8c, iotad8c_b, iotad8_in), ("i64m", i64mc, i64mc_b, i64m_in)):
        k.SQ(lambda s, sb=sb, t_in=t_in: s.dma_start(sb[:], t_in.ap()), writes=[bb], tag="ld_" + nm)

    k.dump("dbg_i8m", i8mc[:], [i8mc_b], (128, 8), F32)
    k.dump("dbg_i64", iota64c[:], [iota64c_b], (128, 64), F32)
    k.dump("dbg_id8", iotad8c[:], [iotad8c_b], (128, 64), F32)
    fl = lambda ap: ap.rearrange("p c m -> p (c m)")
    k.V(lambda v: v.tensor_single_scalar(qtmp[:], srw[:], 0.3, ALU.is_gt),
        reads=[srw_b], writes=[qtmp_b], tag="q_gt")
    k.V(lambda v: v.tensor_single_scalar(srw[:], srw[:], -0.3, ALU.is_lt),
        reads=[srw_b], writes=[srw_b], tag="q_lt")
    k.V(lambda v: v.tensor_sub(m72[:, :, 8:72], qtmp[:], srw[:]),
        reads=[qtmp_b, srw_b], writes=[m72_b], tag="q_sub")
    k.V(lambda v: v.tensor_add(a1[:], m72[:, :, 8:72:2], m72[:, :, 9:72:2]),
        reads=[m72_b], writes=[a1_b], tag="cs_t1")
    k.V(lambda v: v.tensor_add(a2[:], a1[:, :, 0:32:2], a1[:, :, 1:32:2]),
        reads=[a1_b], writes=[a2_b], tag="cs_t2")
    k.V(lambda v: v.tensor_add(a3[:], a2[:, :, 0:16:2], a2[:, :, 1:16:2]),
        reads=[a2_b], writes=[a3_b], tag="cs_t3")
    k.V(lambda v: v.tensor_single_scalar(a1[:, :, 0:8], a3[:], 0.0, ALU.is_gt),
        reads=[a3_b], writes=[a1_b], tag="cs_gt")
    k.V(lambda v: v.tensor_single_scalar(a2[:, :, 0:8], a3[:], 0.0, ALU.is_lt),
        reads=[a3_b], writes=[a2_b], tag="cs_lt")
    k.V(lambda v: v.tensor_sub(m72[:, :, 0:8], a1[:, :, 0:8], a2[:, :, 0:8]),
        reads=[a1_b, a2_b], writes=[m72_b], tag="cs_sub")

    k.dump("m72", m72[:], [m72_b], (128, nch, 72), BF16)
    for tt in range(32):
        pt = tt % 4
        for c in range(nch):
            k.T(lambda t, tt=tt, c=c, pt=pt: t.matmul(
                    ps[pt][:, 0:72], xn0[:, c, tt * 128 : tt * 128 + 128], m72[:, c, :],
                    start=(c == 0), stop=(c == nch - 1)),
                reads=[xn0_b, m72_b], writes=[psb[pt]], tag="fs_mm")
        k.V(lambda v, tt=tt, pt=pt: v.tensor_copy(scs[:, tt, :], ps[pt][:, 0:72]),
            reads=[psb[pt]], writes=[scs_b], tag="fs_ev")

    k.dump("scs_raw", scs[:], [scs_b], (128, 32, 72), F32)
    k.V(lambda v: v.tensor_single_scalar(rt[:], rt[:], 1e-6, ALU.add),
        reads=[rt_b], writes=[rt_b], tag="cal_t1")
    k.V(lambda v: v.reciprocal(rt[:], rt[:]), reads=[rt_b], writes=[rt_b], tag="cal_t2")
    k.V(lambda v: v.tensor_sub(dk[:], kb[:, 1:8], kb[:, 0:7]), reads=[kb_b], writes=[dk_b], tag="cal_dk")
    scf = scs[:].rearrange("p a b -> p (a b)")
    k.S(lambda s: s.activation(scf, scf, AF.Sigmoid, scale=rt[:]),
        reads=[scs_b, rt_b], writes=[scs_b], tag="cal_sig")
    cacc = gacc
    for j in range(7):
        k.V(lambda v, j=j: v.tensor_scalar(camp[:], scf, 7.0, float(-j), ALU.mult, ALU.add),
            reads=[scs_b], writes=[camp_b], tag="cal_a")
        k.V(lambda v: v.tensor_scalar(camp[:], camp[:], 0.0, 1.0, ALU.max, ALU.min),
            reads=[camp_b], writes=[camp_b], tag="cal_b")
        if j == 0:
            k.V(lambda v: v.tensor_scalar(cacc[:], camp[:], dk[:, 0:1], kb[:, 0:1], ALU.mult, ALU.add),
                reads=[camp_b], writes=[cacc_b], tag="cal_c", sreads=[dk_b, kb_b], ptrw=True)
        else:
            k.V(lambda v, j=j: v.scalar_tensor_tensor(cacc[:], camp[:], dk[:, j : j + 1], cacc[:],
                                                      ALU.mult, ALU.add),
                reads=[camp_b, cacc_b], writes=[cacc_b], tag="cal_c", sreads=[dk_b], ptrw=True)
    gates = cacc.rearrange("p (a b) -> p a b", a=32)
    k.dump("gates", cacc[:], [cacc_b], (128, 32 * 72), F32)
    k.dump("sigm", scs[:], [scs_b], (128, 32, 72), F32)

    for tt in range(32):
        k.V(lambda v, tt=tt: v.reduce_max(clv[:], gates[:, tt, 0:8], mybir.AxisListType.X),
            reads=[cacc_b], writes=[clv_b], tag="am_max1")
        k.V(lambda v, tt=tt: v.tensor_scalar(sm1[:, 0:8], gates[:, tt, 0:8], clv[:], None, ALU.is_equal),
            reads=[cacc_b], writes=[sm1_b], tag="am_eq1", sreads=[clv_b], ptrw=True)
        if tt == 0:
            k.dump("dbg_max1", clv[:], [clv_b], (128, 1), F32)
            k.dump("dbg_eq1", sm1[:], [sm1_b], (128, 64), F32)
        k.V(lambda v: v.tensor_mul(sm2[:, 0:8], sm1[:, 0:8], i8mc[:]),
            reads=[sm1_b, i8mc_b], writes=[sm2_b], tag="am_sel1")
        k.V(lambda v: v.tensor_scalar_add(sm2[:, 0:8], sm2[:, 0:8], 8.0),
            reads=[sm2_b], writes=[sm2_b], tag="am_sel1b")
        if tt == 0:
            k.dump("dbg_sel1", sm2[:], [sm2_b], (128, 64), F32)
        k.V(lambda v: v.tensor_reduce(clv[:], sm2[:, 0:8], mybir.AxisListType.X, ALU.min),
            reads=[sm2_b], writes=[clv_b], tag="am_min1")
        if tt == 0:
            k.dump("dbg_cid0", clv[:], [clv_b], (128, 1), F32)
        k.V(lambda v: v.tensor_scalar(sm1[:], iotad8c[:], clv[:], None, ALU.is_equal),
            reads=[iotad8c_b], writes=[sm1_b], tag="am_eq2", sreads=[clv_b], ptrw=True)
        k.V(lambda v: v.tensor_scalar(sm1[:], sm1[:], -1.0, 1e30, ALU.add, ALU.mult),
            reads=[sm1_b], writes=[sm1_b], tag="am_pen")
        k.V(lambda v, tt=tt: v.tensor_add(sm2[:], gates[:, tt, 8:72], sm1[:]),
            reads=[cacc_b, sm1_b], writes=[sm2_b], tag="am_g2")
        k.V(lambda v: v.reduce_max(clv[:], sm2[:], mybir.AxisListType.X),
            reads=[sm2_b], writes=[clv_b], tag="am_max2")
        k.V(lambda v: v.tensor_scalar(sm1[:], sm2[:], clv[:], None, ALU.is_equal),
            reads=[sm2_b], writes=[sm1_b], tag="am_eq3", sreads=[clv_b], ptrw=True)
        k.V(lambda v: v.tensor_mul(sm2[:], sm1[:], i64mc[:]),
            reads=[sm1_b, i64mc_b], writes=[sm2_b], tag="am_sel2")
        k.V(lambda v: v.tensor_scalar_add(sm2[:], sm2[:], 64.0),
            reads=[sm2_b], writes=[sm2_b], tag="am_sel2b")
        k.V(lambda v: v.tensor_reduce(clv[:], sm2[:], mybir.AxisListType.X, ALU.min),
            reads=[sm2_b], writes=[clv_b], tag="am_min2")
        if tt == 0:
            k.dump("dbg_tidx0", clv[:], [clv_b], (128, 1), F32)
            k.dump("dbg_g2", sm2[:], [sm2_b], (128, 64), F32)
        k.V(lambda v, tt=tt: v.tensor_scalar(oh01_tm[:, tt, :], iota64c[:], clv[:], None, ALU.is_equal),
            reads=[iota64c_b], writes=[oh01_tm_b], tag="am_oh", sreads=[clv_b], ptrw=True)

    k.dump("oh01_tm", oh01_tm[:], [oh01_tm_b], (128, 32, 64), BF16)
    for q4 in range(8):
        for j in range(4):
            tt = q4 * 4 + j
            k.T(lambda t, tt=tt, j=j: t.transpose(
                    psx[0][0:64, j * 128 : j * 128 + 128], oh01_tm[:, tt, :], identb[:]),
                reads=[oh01_tm_b, identb_b], writes=[psb[0]], tag="oh_tr")
        k.V(lambda v, q4=q4: v.tensor_copy(oh01_fm[:, q4 * 512 : q4 * 512 + 512], psx[0][0:64, 0:512]),
            reads=[psb[0]], writes=[oh01_fm_b], tag="oh_ev")

    for tt in range(32):
        pt = tt % 4
        k.T(lambda t, tt=tt, pt=pt: t.matmul(
                ps[pt][:, 0:65], oh01_fm[:, tt * 128 : tt * 128 + 128], tab65_sb[0:64, :],
                start=True, stop=True),
            reads=[oh01_fm_b, tab65_b], writes=[psb[pt]], tag="ga_mm")
        k.V(lambda v, tt=tt, pt=pt: v.tensor_copy(gath[:, tt, :], ps[pt][:, 0:65]),
            reads=[psb[pt]], writes=[gath_b], tag="ga_ev")

    k.V(lambda v: v.tensor_scalar_mul(cw2_sb[:].rearrange("p a b -> p (a b)"),
                                      cw2_sb[:].rearrange("p a b -> p (a b)"), 0.5),
        reads=[cw2_b], writes=[cw2_b], tag="cw2_half")
    for h2c in range(2):
        for nt in range(TOK // 512):
            pt = nt % 2
            for c in range(nch):
                k.T(lambda t, h2c=h2c, nt=nt, c=c, pt=pt: t.matmul(
                        ps[pt][:, 0:512], cw1_sb[:, c, h2c * 128 : h2c * 128 + 128],
                        xn0[:, c, nt * 512 : nt * 512 + 512],
                        start=(c == 0), stop=(c == nch - 1)),
                    reads=[cw1_b, xn0_b], writes=[psb[pt]], tag="cp_mm")
            k.V(lambda v, h2c=h2c, pt=pt: v.tensor_scalar(u_sb[:], ps[pt][:, 0:512],
                                                          cb1_sb[:, h2c : h2c + 1], None, ALU.add),
                reads=[psb[pt]], writes=[u_b], tag="cp_u", sreads=[cb1_b], ptrw=True)
            k.S(lambda s: s.activation(e_sb[:], u_sb[:], AF.Erf, scale=0.7071067811865476),
                reads=[u_b], writes=[e_b], tag="cp_erf")
            k.V(lambda v, h2c=h2c, nt=nt: v.scalar_tensor_tensor(
                    h1sb[:, h2c, nt * 512 : nt * 512 + 512], e_sb[:], 1.0, u_sb[:],
                    ALU.add, ALU.mult),
                reads=[e_b, u_b], writes=[h1_b], tag="cp_h1")

    for tt in range(32):
        for h2c in range(2):
            k.T(lambda t, tt=tt, h2c=h2c: t.matmul(
                    ps[2][:, 2 * tt : 2 * tt + 2], h1sb[:, h2c, tt * 128 : tt * 128 + 128],
                    cw2_sb[:, h2c, :], start=(h2c == 0), stop=(h2c == 1)),
                reads=[h1_b, cw2_b], writes=[psb[2]], tag="co_mm")
    for i2 in range(2):
        k.V(lambda v, i2=i2: v.tensor_scalar(compsb[:, i2::2], ps[2][:, i2:64:2],
                                             cb2_sb[:, i2 : i2 + 1], None, ALU.add),
            reads=[psb[2]], writes=[comp_b], tag="co_b", sreads=[cb2_b], ptrw=True)
    k.S(lambda s: s.activation(compsb[:], compsb[:], AF.Tanh),
        reads=[comp_b], writes=[comp_b], tag="co_tanh")

    for tt in range(32):
        k.V(lambda v, tt=tt: v.tensor_scalar(sm3[:], gath[:, tt, 0:16],
                                             compsb[:, 2 * tt : 2 * tt + 1], None, ALU.mult),
            reads=[gath_b], writes=[sm3_b], tag="sp_m1", sreads=[comp_b], ptrw=True)
        k.V(lambda v, tt=tt: v.scalar_tensor_tensor(sm3[:], gath[:, tt, 16:32],
                                                    compsb[:, 2 * tt + 1 : 2 * tt + 2], sm3[:],
                                                    ALU.mult, ALU.add),
            reads=[gath_b, sm3_b], writes=[sm3_b], tag="sp_m2", sreads=[comp_b], ptrw=True)
        k.V(lambda v, tt=tt: v.tensor_add(sm3[:], sm3[:], gath[:, tt, 32:48]),
            reads=[gath_b, sm3_b], writes=[sm3_b], tag="sp_b1")
        k.V(lambda v: v.tensor_scalar_max(sm3[:], sm3[:], 0.0),
            reads=[sm3_b], writes=[sm3_b], tag="sp_relu")
        k.V(lambda v, tt=tt: v.tensor_mul(sm3[:], sm3[:], gath[:, tt, 48:64]),
            reads=[gath_b, sm3_b], writes=[sm3_b], tag="sp_w2")
        k.V(lambda v: v.reduce_sum(mgv[:], sm3[:], mybir.AxisListType.X),
            reads=[sm3_b], writes=[mgv_b], tag="sp_sum")
        k.V(lambda v, tt=tt: v.tensor_scalar(mgv[:], mgv[:], gath[:, tt, 64:65], oscb[:],
                                             ALU.add, ALU.mult),
            reads=[mgv_b], writes=[mgv_b], tag="sp_mag", sreads=[gath_b, oscb_b], ptrw=True)
        k.V(lambda v, tt=tt: v.tensor_scalar(ohm_tm[:], oh01_tm[:, tt, :], mgv[:], None, ALU.mult),
            reads=[oh01_tm_b], writes=[ohm_tm_b], tag="sp_ohm", sreads=[mgv_b], ptrw=True)
        k.T(lambda t, tt=tt: t.transpose(psx[3][0:64, (tt % 4) * 128 : (tt % 4) * 128 + 128],
                                         ohm_tm[:], identb[:]),
            reads=[ohm_tm_b, identb_b], writes=[psb[3]], tag="sp_tr")
        if tt % 4 == 3:
            q4 = tt // 4
            k.V(lambda v, q4=q4: v.tensor_copy(ohs_fm[:, q4 * 512 : q4 * 512 + 512],
                                               psx[3][0:64, 0:512]),
                reads=[psb[3]], writes=[ohs_fm_b], tag="sp_ev")

    k.dump("ohs_fm", ohs_fm[:], [ohs_fm_b], (64, TOK), BF16)
    k.dump("oh01_fm", oh01_fm[:], [oh01_fm_b], (64, TOK), BF16)
    k.dump("gath", gath[:], [gath_b], (128, 32, 65), F32)
    k.dump("compsb", compsb[:], [comp_b], (128, 64), F32)
    k.release_arena()
    k.release_arena()
    k.barrier()

    if upto <= 3:
        k.p.finalize(nc)
        return k

    import os as _os
    _skip_p4 = bool(int(_os.environ.get("SKIP_P4", "0")))
    s0_d = k.dscr("s0_d", (B_PER_CORE, D, T), F32)
    s0_d_b = Buf("s0_d")

    ar.mark()
    ql_sb = ar.alloc([128, T], BF16, "ql_sb")
    p_sb = [ar.alloc([128, 2048], BF16, f"p_sb{i}") for i in range(2)]
    ctx_sb = ar.alloc([128, 1024], BF16, "ctx_sb")
    s0_sb = ar.alloc([128, T], F32, "s0_sb")
    rz64 = ar.alloc([64, 1024], F32, "rz64")
    trilb = ar.alloc([128, 128], BF16, "trilb")
    ql_b = k.buf("ql_sb")
    p_b = [k.buf(f"p_sb{i}") for i in range(2)]
    ctx_sb_b = k.buf("ctx_sb")
    s0_sb_b = k.buf("s0_sb")
    rz64_b = k.buf("rz64")
    trilb_b = k.buf("trilb")
    tril_in = k.din("tril_bf", (128, 128), BF16)
    k.GQ(lambda g: g.dma_start(trilb[:], tril_in.ap()), writes=[trilb_b], tag="ld_tril")

    for b in range(B_PER_CORE if not _skip_p4 else 0):
        for h in range(16):
            for half in range(2):
                for n2 in range(2):
                    k.T(lambda t, b=b, h=h, half=half, n2=n2: t.matmul(
                            ps[half][:, n2 * 512 : n2 * 512 + 512],
                            bq[:, h, :],
                            q1[:, b * T + half * 1024 + n2 * 512 : b * T + half * 1024 + n2 * 512 + 512],
                            start=True, stop=True),
                        reads=[bq_b, q1_b], writes=[psb[half]], tag="ql_mm")
                k.V(lambda v, half=half: v.tensor_copy(
                        ql_sb[:, half * 1024 : half * 1024 + 1024], ps[half][:, :]),
                    reads=[psb[half]], writes=[ql_b], tag="ql_ev")
            for half in range(2):
                h0 = half * 1024
                nsig = 8 * (half + 1)
                for m in range(nsig // 2):
                    se, so = 2 * m, 2 * m + 1
                    rel_e = max(se * 128, h0) - h0
                    rel_o = max(so * 128, h0) - h0
                    pm = m % 2
                    for sig, rel in ((se, rel_e), (so, rel_o)):
                        pi = sig % 2
                        segs = ([(rel, 512 - rel), (512, 512)] if rel < 512
                                else [(rel, 1024 - rel)])
                        for o, w in segs:
                            k.T(lambda t, b=b, sig=sig, pi=pi, o=o, w=w, h0=h0: t.matmul(
                                    ps[pi][:, o : o + w],
                                    ckv[:, b * T + sig * 128 : b * T + sig * 128 + 128],
                                    ql_sb[:, h0 + o : h0 + o + w],
                                    start=True, stop=True),
                                reads=[ckv_b, ql_b], writes=[psb[pi]], tag="sc_mm")
                    k.S(lambda s, pm=pm, rel_e=rel_e: s.activation(
                            p_sb[pm][:, rel_e : 2048], psAf[:, rel_e : 2048],
                            AF.Exp, scale=1.0 / 8.0),
                        reads=[psb[0], psb[1]], writes=[p_b[pm]], tag="sc_exp")
                    if rel_o > rel_e:
                        k.V(lambda v, pm=pm, rel_e=rel_e, rel_o=rel_o: v.memset(
                                p_sb[pm][:, 1024 + rel_e : 1024 + rel_o], 0.0),
                            reads=[], writes=[p_b[pm]], tag="sc_inv")
                    if se * 128 >= h0:
                        k.V(lambda v, pm=pm, rel_e=rel_e: v.tensor_mul(
                                p_sb[pm][:, rel_e : rel_e + 128],
                                p_sb[pm][:, rel_e : rel_e + 128], trilb[:]),
                            reads=[trilb_b], writes=[p_b[pm]], tag="sc_mask")
                    if so * 128 >= h0:
                        k.V(lambda v, pm=pm, rel_o=rel_o: v.tensor_mul(
                                p_sb[pm][:, 1024 + rel_o : 1024 + rel_o + 128],
                                p_sb[pm][:, 1024 + rel_o : 1024 + rel_o + 128], trilb[:]),
                            reads=[trilb_b], writes=[p_b[pm]], tag="sc_mask")
                    for sig, rel, base in ((se, rel_e, 0), (so, rel_o, 1024)):
                        segs = ([(rel, 512 - rel), (512, 512)] if rel < 512
                                else [(rel, 1024 - rel)])
                        last = sig == nsig - 1
                        for o, w in segs:
                            k.T(lambda t, pm=pm, o=o, w=w, sig=sig, last=last, base=base: t.matmul(
                                    ps[2][:, o : o + w], onesb[:],
                                    p_sb[pm][:, base + o : base + o + w],
                                    start=(sig == 0), stop=last),
                                reads=[onesb_b, p_b[pm]], writes=[psb[2]], tag="z_mm")
                        for o, w in segs:
                            k.T(lambda t, b=b, sig=sig, pm=pm, o=o, w=w, last=last, base=base: t.matmul(
                                    ps[3][:, o : o + w], ckv_tm[:, b * 16 + sig, :],
                                    p_sb[pm][:, base + o : base + o + w],
                                    start=(sig == 0), stop=last),
                                reads=[ckv_tm_b, p_b[pm]], writes=[psb[3]], tag="ctx_mm")
                k.S(lambda s: s.activation(rz64[:, :], ps[2][0:64, :], AF.Ln),
                    reads=[psb[2]], writes=[rz64_b], tag="lnz")
                k.S(lambda s: s.activation(rz64[:, :], rz64[:, :], AF.Exp, scale=-1.0),
                    reads=[rz64_b], writes=[rz64_b], tag="rz")
                k.V(lambda v: v.tensor_copy(ctx_sb[:, :], ps[3][:, :]),
                    reads=[psb[3]], writes=[ctx_sb_b], tag="ctx_ev")
                for n2 in range(2):
                    k.T(lambda t, h=h, n2=n2: t.matmul(
                            ps[2][0:64, n2 * 512 : n2 * 512 + 512],
                            v2[:, h * 64 : h * 64 + 64],
                            ctx_sb[:, n2 * 512 : n2 * 512 + 512],
                            start=True, stop=True),
                        reads=[v2_b, ctx_sb_b], writes=[psb[2]], tag="y_mm")
                po = (h % 2) * 64
                k.V(lambda v, half=half, po=po: v.tensor_mul(
                        s0_sb[po : po + 64, half * 1024 : half * 1024 + 1024],
                        ps[2][0:64, :], rz64[:, :]),
                    reads=[psb[2], rz64_b], writes=[s0_sb_b], tag="y_ev")
            if h % 2 == 1:
                k.SQ(lambda s, b=b, h=h: s.dma_start(
                        s0_d.ap()[b, (h // 2) * 128 : (h // 2) * 128 + 128, :], s0_sb[:, :]),
                    reads=[s0_sb_b], writes=[s0_d_b], tag="st_s0")
    k.release_arena()
    k.barrier()

    if upto <= 4:
        k.p.finalize(nc)
        return k

    dirs_in = k.din("dirs", (128, D))
    gam0_in = k.din("gamma0", (128, NCH))
    bet0_in = k.din("beta0", (128, NCH))
    y_d = k.dscr("y_d", (B_PER_CORE, D, T), BF16)
    x1_d = k.dscr("x1_d", (B_PER_CORE, D, T), F32)
    xn1_d = k.dscr("xn1_d", (B_PER_CORE, D, T), BF16)
    y_d_b = Buf("y_d")
    x1_d_b = Buf("x1_d")
    xn1_d_b = Buf("xn1_d")

    ar.mark()
    dirs_sb = ar.alloc([128, D], BF16, "dirs_sb")
    gam0_sb = ar.alloc([128, nch], F32, "gam0_sb")
    bet0_sb = ar.alloc([128, nch], F32, "bet0_sb")
    mu_bc = [ar.alloc([128, BLK], F32, f"mu_bc{i}") for i in range(2)]
    r_bc = [ar.alloc([128, BLK], F32, f"r_bc{i}") for i in range(2)]
    xti = [ar.alloc([128, BLK], F32, f"xti{i}") for i in range(2)]
    ybt = [ar.alloc([128, BLK], BF16, f"ybt{i}") for i in range(2)]
    sqb = ar.alloc([128, BLK], BF16, "sqb")
    s0t = [ar.alloc([128, BLK], F32, f"s0t{i}") for i in range(2)]
    muy = ar.alloc([128, 4, BLK], F32, "muy")
    ry = ar.alloc([128, 4, BLK], F32, "ry")
    yint = [ar.alloc([128, BLK], BF16, f"yint{i}") for i in range(2)]
    g0t = [ar.alloc([128, BLK], F32, f"g0t{i}") for i in range(2)]
    x1t = [ar.alloc([128, BLK], F32, f"x1t{i}") for i in range(2)]
    x1b = ar.alloc([128, nch, T], BF16, "x1b")
    mu1 = ar.alloc([128, 4, BLK], F32, "mu1")
    r1 = ar.alloc([128, 4, BLK], F32, "r1")
    tmpb = ar.alloc([128, BLK], F32, "tmpb")
    xn1t = [ar.alloc([128, BLK], BF16, f"xn1t{i}") for i in range(2)]

    dirs_b = k.buf("dirs_sb"); gam0_b = k.buf("gam0_sb"); bet0_b = k.buf("bet0_sb")
    mu_bc_b = [k.buf(f"mu_bc{i}") for i in range(2)]
    r_bc_b = [k.buf(f"r_bc{i}") for i in range(2)]
    xti_b = [k.buf(f"xti{i}") for i in range(2)]
    ybt_b = [k.buf(f"ybt{i}") for i in range(2)]
    sqb_b = k.buf("sqb")
    s0t_b = [k.buf(f"s0t{i}") for i in range(2)]
    muy_b = k.buf("muy"); ry_b = k.buf("ry")
    yint_b = [k.buf(f"yint{i}") for i in range(2)]
    g0t_b = [k.buf(f"g0t{i}") for i in range(2)]
    x1t_b = [k.buf(f"x1t{i}") for i in range(2)]
    x1b_b = k.buf("x1b")
    mu1_b = k.buf("mu1"); r1_b = k.buf("r1"); tmpb_b = k.buf("tmpb")
    xn1t_b = [k.buf(f"xn1t{i}") for i in range(2)]

    k.GQ(lambda g: g.dma_start(dirs_sb[:], dirs_in.ap()), writes=[dirs_b], tag="ld_dirs")
    k.SQ(lambda s: s.dma_start(gam0_sb[:], gam0_in.ap()), writes=[gam0_b], tag="ld_g0")
    k.SQ(lambda s: s.dma_start(bet0_sb[:], bet0_in.ap()), writes=[bet0_b], tag="ld_b0")

    nblk_b = T // BLK
    for b in range(B_PER_CORE):
        for blk in range(nblk_b):
            t0 = blk * BLK
            g = b * T + t0
            i = blk % 2
            gb = b * nblk_b + blk
            k.GQ(lambda gq, gb=gb, i=i: gq.dma_start(mu_bc[i][:], mur2_d.ap()[0, gb]),
                 reads=[mur2_d_b], writes=[mu_bc_b[i]], tag="p5_mubc")
            k.GQ(lambda gq, gb=gb, i=i: gq.dma_start(r_bc[i][:], mur2_d.ap()[1, gb]),
                 reads=[mur2_d_b], writes=[r_bc_b[i]], tag="p5_rbc")
            for c in range(nch):
                ic = (blk * nch + c) % 2
                pd = 3 * (c % 2)
                k.T(lambda t, c=c, g=g, pd=pd: t.matmul(
                        ps[pd][:, 0:BLK], dirs_sb[0:64, c * 128 : c * 128 + 128],
                        ohs_fm[:, g : g + BLK], start=True, stop=True),
                    reads=[dirs_b, ohs_fm_b], writes=[psb[pd]], tag="p5_dir")
                k.SQ(lambda s, b=b, c=c, t0=t0, ic=ic: s.dma_start(
                        xti[ic][:], xfm.ap()[b, c * 128 : c * 128 + 128, t0 : t0 + BLK]),
                     reads=[xfm_b], writes=[xti_b[ic]], tag="p5_ldx")
                k.V(lambda v, ic=ic, i=i: v.tensor_sub(xti[ic][:], xti[ic][:], mu_bc[i][:]),
                    reads=[mu_bc_b[i]], writes=[xti_b[ic]], tag="p5_xnsub")
                k.V(lambda v, ic=ic, i=i: v.tensor_mul(xti[ic][:], xti[ic][:], r_bc[i][:]),
                    reads=[r_bc_b[i]], writes=[xti_b[ic]], tag="p5_xnmul")
                import os as _os2
                if int(_os2.environ.get("NO_PSUM_Y", "0")):
                    k.V(lambda v, ic=ic: v.tensor_copy(ybt[ic][:], xti[ic][:]),
                        reads=[xti_b[ic]], writes=[ybt_b[ic]], tag="p5_y")
                else:
                    k.V(lambda v, ic=ic: v.tensor_add(ybt[ic][:], xti[ic][:], ps[0][:, 0:BLK]),
                        reads=[xti_b[ic], psb[0]], writes=[ybt_b[ic]], tag="p5_y")
                k.SQ(lambda s, b=b, c=c, t0=t0, ic=ic: s.dma_start(
                        y_d.ap()[b, c * 128 : c * 128 + 128, t0 : t0 + BLK], ybt[ic][:]),
                     reads=[ybt_b[ic]], writes=[y_d_b], tag="p5_sty")
                k.V(lambda v, ic=ic: v.tensor_mul(sqb[:], ybt[ic][:], ybt[ic][:]),
                    reads=[ybt_b[ic]], writes=[sqb_b], tag="p5_sq")
                k.T(lambda t, c=c, ic=ic: t.matmul(ps[1][:, 0:BLK], onesb[:], ybt[ic][:],
                                                   start=(c == 0), stop=(c == nch - 1)),
                    reads=[ybt_b[ic], onesb_b], writes=[psb[1]], tag="p5_mmS")
                k.T(lambda t, c=c: t.matmul(ps[2][:, 0:BLK], onesb[:], sqb[:],
                                            start=(c == 0), stop=(c == nch - 1)),
                    reads=[sqb_b, onesb_b], writes=[psb[2]], tag="p5_mmS2")

            k.V(lambda v, blk=blk: v.tensor_scalar_mul(muy[:, blk, :], ps[1][:, 0:BLK], 1.0 / D),
                reads=[psb[1]], writes=[muy_b], tag="p5_muy")
            k.V(lambda v, blk=blk: v.tensor_mul(tmpb[:], muy[:, blk, :], muy[:, blk, :]),
                reads=[muy_b], writes=[tmpb_b], tag="p5_musq")
            k.V(lambda v, blk=blk: v.scalar_tensor_tensor(ry[:, blk, :], ps[2][:, 0:BLK], 1.0 / D,
                                                          tmpb[:], ALU.mult, ALU.subtract),
                reads=[psb[2], tmpb_b], writes=[ry_b], tag="p5_vary")
            k.S(lambda s, blk=blk: s.activation(ry[:, blk, :], ry[:, blk, :], AF.Ln, bias=epsc[:]),
                reads=[ry_b, epsc_b], writes=[ry_b], tag="p5_lny")
            k.S(lambda s, blk=blk: s.activation(ry[:, blk, :], ry[:, blk, :], AF.Exp, scale=-0.5),
                reads=[ry_b], writes=[ry_b], tag="p5_expy")
        if b == 1:
            k.dump("dbg_ry1", ry[:], [ry_b], (128, 4, BLK), F32)
            k.dump("dbg_muy1", muy[:], [muy_b], (128, 4, BLK), F32)
        for blk in range(nblk_b):
            t0 = blk * BLK
            for c in range(nch):
                ic = (blk * nch + c) % 2
                k.SQ(lambda s, b=b, c=c, t0=t0, ic=ic: s.dma_start(
                        yint[ic][:], y_d.ap()[b, c * 128 : c * 128 + 128, t0 : t0 + BLK]),
                     reads=[y_d_b], writes=[yint_b[ic]], tag="p5_ldy")
                k.V(lambda v, blk=blk, ic=ic: v.tensor_sub(g0t[ic][:], yint[ic][:], muy[:, blk, :]),
                    reads=[yint_b[ic], muy_b], writes=[g0t_b[ic]], tag="p5_ynsub")
                k.V(lambda v, blk=blk, ic=ic: v.tensor_mul(g0t[ic][:], g0t[ic][:], ry[:, blk, :]),
                    reads=[ry_b], writes=[g0t_b[ic]], tag="p5_ynmul")
                k.S(lambda s, c=c, ic=ic: s.activation(g0t[ic][:], g0t[ic][:], AF.Silu,
                                                       bias=bet0_sb[:, c : c + 1],
                                                       scale=gam0_sb[:, c : c + 1]),
                    reads=[g0t_b[ic]], writes=[g0t_b[ic]], tag="p5_silu",
                    sreads=[gam0_b, bet0_b])
                k.SQ(lambda s, b=b, c=c, t0=t0, ic=ic: s.dma_start(
                        x1t[ic][:], xfm.ap()[b, c * 128 : c * 128 + 128, t0 : t0 + BLK]),
                     reads=[xfm_b], writes=[x1t_b[ic]], tag="p5_ldx2")
                k.SQ(lambda s, b=b, c=c, t0=t0, ic=ic: s.dma_start(
                        s0t[ic][:], s0_d.ap()[b, c * 128 : c * 128 + 128, t0 : t0 + BLK]),
                     reads=[s0_d_b], writes=[s0t_b[ic]], tag="p5_lds0")
                k.V(lambda v, ic=ic: v.tensor_mul(g0t[ic][:], g0t[ic][:], s0t[ic][:]),
                    reads=[s0t_b[ic]], writes=[g0t_b[ic]], tag="p5_inc")
                k.V(lambda v, ic=ic: v.tensor_add(x1t[ic][:], x1t[ic][:], g0t[ic][:]),
                    reads=[g0t_b[ic]], writes=[x1t_b[ic]], tag="p5_x1")
                k.SQ(lambda s, b=b, c=c, t0=t0, ic=ic: s.dma_start(
                        x1_d.ap()[b, c * 128 : c * 128 + 128, t0 : t0 + BLK], x1t[ic][:]),
                     reads=[x1t_b[ic]], writes=[x1_d_b], tag="p5_stx1")
                k.V(lambda v, c=c, t0=t0, ic=ic: v.tensor_copy(x1b[:, c, t0 : t0 + BLK], x1t[ic][:]),
                    reads=[x1t_b[ic]], writes=[x1b_b], tag="p5_x1b")
                k.V(lambda v, c=c, t0=t0: v.tensor_mul(sqb[:], x1b[:, c, t0 : t0 + BLK],
                                                       x1b[:, c, t0 : t0 + BLK]),
                    reads=[x1b_b], writes=[sqb_b], tag="p5_sq1")
                k.T(lambda t, c=c, t0=t0: t.matmul(ps[1][:, 0:BLK], onesb[:],
                                                   x1b[:, c, t0 : t0 + BLK],
                                                   start=(c == 0), stop=(c == nch - 1)),
                    reads=[x1b_b, onesb_b], writes=[psb[1]], tag="p5_mmT")
                k.T(lambda t, c=c: t.matmul(ps[2][:, 0:BLK], onesb[:], sqb[:],
                                            start=(c == 0), stop=(c == nch - 1)),
                    reads=[sqb_b, onesb_b], writes=[psb[2]], tag="p5_mmT2")
            k.V(lambda v, blk=blk: v.tensor_scalar_mul(mu1[:, blk, :], ps[1][:, 0:BLK], 1.0 / D),
                reads=[psb[1]], writes=[mu1_b], tag="p5_mu1")
            k.V(lambda v, blk=blk: v.tensor_mul(tmpb[:], mu1[:, blk, :], mu1[:, blk, :]),
                reads=[mu1_b], writes=[tmpb_b], tag="p5_musq1")
            k.V(lambda v, blk=blk: v.scalar_tensor_tensor(r1[:, blk, :], ps[2][:, 0:BLK], 1.0 / D,
                                                          tmpb[:], ALU.mult, ALU.subtract),
                reads=[psb[2], tmpb_b], writes=[r1_b], tag="p5_var1")
        for blk in range(nblk_b):
            t0 = blk * BLK
            k.S(lambda s, blk=blk: s.activation(r1[:, blk, :], r1[:, blk, :], AF.Ln, bias=epsc[:]),
                reads=[r1_b, epsc_b], writes=[r1_b], tag="p5_ln1")
            k.S(lambda s, blk=blk: s.activation(r1[:, blk, :], r1[:, blk, :], AF.Exp, scale=-0.5),
                reads=[r1_b], writes=[r1_b], tag="p5_exp1")
            for c in range(nch):
                ic = (blk * nch + c) % 2
                k.V(lambda v, blk=blk, c=c, t0=t0, ic=ic: v.tensor_sub(
                        xn1t[ic][:], x1b[:, c, t0 : t0 + BLK], mu1[:, blk, :]),
                    reads=[x1b_b, mu1_b], writes=[xn1t_b[ic]], tag="p5_xn1s")
                k.V(lambda v, blk=blk, ic=ic: v.tensor_mul(xn1t[ic][:], xn1t[ic][:], r1[:, blk, :]),
                    reads=[r1_b], writes=[xn1t_b[ic]], tag="p5_xn1m")
                k.SQ(lambda s, b=b, c=c, t0=t0, ic=ic: s.dma_start(
                        xn1_d.ap()[b, c * 128 : c * 128 + 128, t0 : t0 + BLK], xn1t[ic][:]),
                     reads=[xn1t_b[ic]], writes=[xn1_d_b], tag="p5_stxn1")
    k.release_arena()
    k.release_arena()
    k.barrier()

    if upto <= 5:
        k.p.finalize(nc)
        return k

    env = dict(locals())
    _build_p7(k, env)
    if upto <= 7:
        k.p.finalize(nc)
        return k
    _build_p8(k, env)
    k.p.finalize(nc)
    return k


def _build_p7(k, env):
    nc = k.nc
    ar = k.ar
    nch = NCH
    ps, psx, psb = env["ps"], env["psx"], env["psb"]
    psAf = env["psAf"]
    onesb, onesb_b = env["onesb"], env["onesb_b"]
    epsc, epsc_b = env["epsc"], env["epsc_b"]
    xn1_d, xn1_d_b = env["xn1_d"], env["xn1_d_b"]

    qwT = k.din("qwT", (D, D))
    kwT = k.din("kwT", (D, D))
    vwT = k.din("vwT", (D, D))
    owT = k.din("owT", (D, D))
    qb_in = k.din("qb", (128, nch))
    kb_in = k.din("kb", (128, nch))
    ob_in = k.din("ob", (128, nch))
    vbrow_in = k.din("vbrow", (128, D))
    nw_in = k.din("nw", (128, nch))
    nb_in = k.din("nb", (128, nch))
    g1_d = k.dscr("g1_d", (B_PER_CORE, D, T), F32)
    g1_d_b = Buf("g1_d")
    env["g1_d"] = g1_d
    env["g1_d_b"] = g1_d_b

    ar.mark()
    xnl_a = ar.alloc([128, 4, TOK], BF16, "xnl_a")
    xnl_b4 = ar.alloc([128, 4, 1024], BF16, "xnl_b4")
    vw_sb = ar.alloc([128, nch, D], BF16, "vw_sb")
    qkw = [ar.alloc([128, nch, 256], BF16, f"qkw{i}") for i in range(2)]
    oww = [ar.alloc([128, nch, 128], BF16, f"oww{i}") for i in range(2)]
    qb_sb = ar.alloc([128, nch], F32, "qb_sb")
    kb_sb = ar.alloc([128, nch], F32, "kb_sb")
    ob_sb = ar.alloc([128, nch], F32, "ob_sb")
    vbr_sb = ar.alloc([128, D], BF16, "vbr_sb")
    nw_sb = ar.alloc([128, nch], F32, "nw_sb")
    nb_sb = ar.alloc([128, nch], F32, "nb_sb")
    vtm = ar.alloc([128, 16, D], BF16, "vtm")
    aob = ar.alloc([128, nch, T], BF16, "aob")
    qh = ar.alloc([128, T], BF16, "qh")
    kh = ar.alloc([128, T], BF16, "kh")
    p7sb = [ar.alloc([128, 2048], BF16, f"p7sb{i}") for i in range(2)]
    rzf = ar.alloc([128, 1024], F32, "rzf")
    x2t = [ar.alloc([128, nch, BLK], BF16, f"p7xn{i}") for i in range(2)]
    sq7 = ar.alloc([128, BLK], BF16, "sq7")
    mue = [ar.alloc([128, BLK], F32, f"mue{i}") for i in range(2)]
    ree = [ar.alloc([128, BLK], F32, f"ree{i}") for i in range(2)]
    tmp7 = ar.alloc([128, BLK], F32, "tmp7")
    xnt7 = [ar.alloc([128, BLK], F32, f"xnt7{i}") for i in range(2)]
    g1t = [ar.alloc([128, BLK], F32, f"g1t{i}") for i in range(2)]
    xt7 = [ar.alloc([128, BLK], BF16, f"xt7{i}") for i in range(2)]

    xnl_a_b = k.buf("xnl_a"); xnl_b4_b = k.buf("xnl_b4")
    vw_b = k.buf("vw_sb")
    qkw_qb = [k.buf(f"qkw_q{i}") for i in range(2)]
    qkw_kb = [k.buf(f"qkw_k{i}") for i in range(2)]
    oww_b = [k.buf(f"oww{i}") for i in range(2)]
    qb_b = k.buf("qb_sb"); kb_b = k.buf("kb_sb"); ob_b = k.buf("ob_sb")
    vbr_b = k.buf("vbr_sb"); nw_b = k.buf("nw_sb"); nb_b = k.buf("nb_sb")
    vtm_b = k.buf("vtm"); aob_b = k.buf("aob")
    qh_b = k.buf("qh"); kh_b = k.buf("kh")
    p7sb_b = [k.buf(f"p7sb{i}") for i in range(2)]
    rzf_b = k.buf("rzf")
    x2t_b = [k.buf(f"p7xn{i}") for i in range(2)]
    sq7_b = k.buf("sq7")
    mue_b = [k.buf(f"mue{i}") for i in range(2)]
    ree_b = [k.buf(f"ree{i}") for i in range(2)]
    tmp7_b = k.buf("tmp7")
    xnt7_b = [k.buf(f"xnt7{i}") for i in range(2)]
    g1t_b = [k.buf(f"g1t{i}") for i in range(2)]
    xt7_b = [k.buf(f"xt7{i}") for i in range(2)]

    ch_rows = lambda t: t.ap().rearrange("(c p) m -> p c m", p=128)
    k.GQ(lambda g: g.dma_start(vw_sb[:], ch_rows(vwT)), writes=[vw_b], tag="ld_vw")
    for nm, sb, bb, ti in (("qb", qb_sb, qb_b, qb_in), ("kb", kb_sb, kb_b, kb_in),
                           ("ob", ob_sb, ob_b, ob_in),
                           ("nw", nw_sb, nw_b, nw_in), ("nb", nb_sb, nb_b, nb_in)):
        k.SQ(lambda s, sb=sb, ti=ti: s.dma_start(sb[:], ti.ap()), writes=[bb], tag="ld7_" + nm)
    k.GQ(lambda g: g.dma_start(vbr_sb[:], vbrow_in.ap()), writes=[vbr_b], tag="ld7_vbr")

    MC = [8, 4, 2, 1]

    for blk in range(TOK // BLK):
        b, t0 = divmod(blk * BLK, T)
        i = blk % 2
        k.SQ(lambda s, b=b, t0=t0, i=i: s.dma_start(
                x2t[i][:], xn1_d.ap()[b].rearrange("(c p) t -> p c t", p=128)[:, :, t0 : t0 + BLK]),
             reads=[xn1_d_b], writes=[x2t_b[i]], tag="p7_ldxn")
        for c in range(nch):
            k.V(lambda v, i=i, c=c: v.tensor_mul(sq7[:], x2t[i][:, c, :], x2t[i][:, c, :]),
                reads=[x2t_b[i]], writes=[sq7_b], tag="p7_sq")
            k.T(lambda t, c=c, i=i: t.matmul(ps[1][:, 0:BLK], onesb[:], x2t[i][:, c, :],
                                             start=(c == 0), stop=(c == nch - 1)),
                reads=[x2t_b[i], onesb_b], writes=[psb[1]], tag="p7_mmS")
            k.T(lambda t, c=c: t.matmul(ps[2][:, 0:BLK], onesb[:], sq7[:],
                                        start=(c == 0), stop=(c == nch - 1)),
                reads=[sq7_b, onesb_b], writes=[psb[2]], tag="p7_mmS2")
        k.V(lambda v, i=i: v.tensor_scalar_mul(mue[i][:], ps[1][:, 0:BLK], 1.0 / D),
            reads=[psb[1]], writes=[mue_b[i]], tag="p7_mu")
        k.V(lambda v, i=i: v.tensor_mul(tmp7[:], mue[i][:], mue[i][:]),
            reads=[mue_b[i]], writes=[tmp7_b], tag="p7_musq")
        k.V(lambda v, i=i: v.scalar_tensor_tensor(ree[i][:], ps[2][:, 0:BLK], 1.0 / D,
                                                  tmp7[:], ALU.mult, ALU.subtract),
            reads=[psb[2], tmp7_b], writes=[ree_b[i]], tag="p7_var")
        k.S(lambda s, i=i: s.activation(ree[i][:], ree[i][:], AF.Ln, bias=epsc[:]),
            reads=[ree_b[i], epsc_b], writes=[ree_b[i]], tag="p7_ln")
        k.S(lambda s, i=i: s.activation(ree[i][:], ree[i][:], AF.Exp, scale=-0.5),
            reads=[ree_b[i]], writes=[ree_b[i]], tag="p7_exp")
        e = t0 // 512
        nchunks = nch if e == 0 else MC[e]
        for c in range(nchunks):
            k.V(lambda v, i=i, c=c: v.tensor_sub(xnt7[i][:], x2t[i][:, c, :], mue[i][:]),
                reads=[x2t_b[i], mue_b[i]], writes=[xnt7_b[i]], tag="p7_xs")
            k.V(lambda v, i=i: v.tensor_mul(xnt7[i][:], xnt7[i][:], ree[i][:]),
                reads=[ree_b[i]], writes=[xnt7_b[i]], tag="p7_xm")
            if c < 4:
                dst = xnl_a[:, c, blk * BLK : blk * BLK + BLK]
                dbuf = xnl_a_b
            else:
                dst = xnl_b4[:, c - 4, b * BLK : b * BLK + BLK]
                dbuf = xnl_b4_b
            k.S(lambda s, i=i, c=c, dst=dst: s.activation(dst, xnt7[i][:], AF.Identity,
                                                          bias=nb_sb[:, c : c + 1],
                                                          scale=nw_sb[:, c : c + 1]),
                reads=[xnt7_b[i]], writes=[dbuf], tag="p7_gb", sreads=[nw_b, nb_b])

    def xnl(c, cols, b):
        if c < 4:
            return xnl_a[:, c, cols[0] : cols[1]]
        off = (cols[0] - b * T) + b * BLK
        return xnl_b4[:, c - 4, off : off + (cols[1] - cols[0])]

    for b in range(B_PER_CORE):
        for e in range(4):
            tg0 = b * T + e * 512
            for tt in range(4):
                c0 = tg0 + tt * 128
                ti = e * 4 + tt
                pv = 3 * (tt % 2)
                for n2 in range(2):
                    for mc in range(MC[e]):
                        k.T(lambda t, ti=ti, c0=c0, mc=mc, n2=n2, b=b, pv=pv: t.matmul(
                                ps[pv][:, n2 * 512 : n2 * 512 + 512],
                                xnl(mc, (c0, c0 + 128), b),
                                vw_sb[:, mc, n2 * 512 : n2 * 512 + 512],
                                start=(mc == 0), stop=False),
                            reads=[xnl_a_b, xnl_b4_b, vw_b], writes=[psb[pv]], tag="p7v_mm")
                    k.T(lambda t, n2=n2, pv=pv: t.matmul(
                            ps[pv][:, n2 * 512 : n2 * 512 + 512],
                            onesb[0:1, :], vbr_sb[0:1, n2 * 512 : n2 * 512 + 512],
                            start=False, stop=True),
                        reads=[onesb_b, vbr_b], writes=[psb[pv]], tag="p7v_bias")
                k.V(lambda v, ti=ti, pv=pv: v.tensor_copy(vtm[:, ti, :], ps[pv][:, 0:1024]),
                    reads=[psb[pv]], writes=[vtm_b], tag="p7v_ev")
        for h in range(8):
            hp = h % 2
            k.GQ(lambda g, h=h, hp=hp: g.dma_start(
                    qkw[hp][:, :, 0:128],
                    qwT.ap().rearrange("(c p) m -> p c m", p=128)[:, :, h * 128 : h * 128 + 128]),
                 writes=[qkw_qb[hp]], tag="p7_ldqw")
            k.GQ(lambda g, h=h, hp=hp: g.dma_start(
                    qkw[hp][:, :, 128:256],
                    kwT.ap().rearrange("(c p) m -> p c m", p=128)[:, :, h * 128 : h * 128 + 128]),
                 writes=[qkw_kb[hp]], tag="p7_ldkw")
            for e in range(4):
                cg0 = b * T + e * 512
                for off, dstw, dbuf, bias, bbuf, tg in (
                        (0, qh, qh_b, qb_sb, qb_b, "p7_qmm"),
                        (128, kh, kh_b, kb_sb, kb_b, "p7_kmm")):
                    wb = qkw_qb[hp] if off == 0 else qkw_kb[hp]
                    pqk = 3 * ((e + (off // 128)) % 2)
                    for mc in range(MC[e]):
                        k.T(lambda t, e=e, cg0=cg0, mc=mc, off=off, b=b, pqk=pqk, hp=hp: t.matmul(
                                ps[pqk][:, 0:512], qkw[hp][:, mc, off : off + 128],
                                xnl(mc, (cg0, cg0 + 512), b),
                                start=(mc == 0), stop=(mc == MC[e] - 1)),
                            reads=[wb, xnl_a_b, xnl_b4_b], writes=[psb[pqk]], tag=tg)
                    k.V(lambda v, e=e, dstw=dstw, bias=bias, h=h, pqk=pqk: v.tensor_scalar(
                            dstw[:, e * 512 : e * 512 + 512], ps[pqk][:, 0:512],
                            bias[:, h : h + 1], None, ALU.add),
                        reads=[psb[pqk]], writes=[dbuf], tag=tg + "e", sreads=[bbuf], ptrw=True)
            for half in range(2):
                hc0 = half * 1024
                for m in range(8):
                    se, so = 2 * m, 2 * m + 1
                    pm = m % 2
                    for sig in (se, so):
                        pi = sig % 2
                        for n2 in range(2):
                            k.T(lambda t, sig=sig, pi=pi, n2=n2, hc0=hc0: t.matmul(
                                    ps[pi][:, n2 * 512 : n2 * 512 + 512],
                                    kh[:, sig * 128 : sig * 128 + 128],
                                    qh[:, hc0 + n2 * 512 : hc0 + n2 * 512 + 512],
                                    start=True, stop=True),
                                reads=[kh_b, qh_b], writes=[psb[pi]], tag="p7_sc")
                    k.S(lambda s, pm=pm: s.activation(p7sb[pm][:], psAf[:, :], AF.Exp,
                                                      scale=0.03125),
                        reads=[psb[0], psb[1]], writes=[p7sb_b[pm]], tag="p7_exp2")
                    for sig, base in ((se, 0), (so, 1024)):
                        for n2 in range(2):
                            k.T(lambda t, pm=pm, n2=n2, sig=sig, base=base: t.matmul(
                                    ps[2][:, n2 * 512 : n2 * 512 + 512], onesb[:],
                                    p7sb[pm][:, base + n2 * 512 : base + n2 * 512 + 512],
                                    start=(sig == 0), stop=(sig == 15)),
                                reads=[onesb_b, p7sb_b[pm]], writes=[psb[2]], tag="p7_z")
                            k.T(lambda t, pm=pm, n2=n2, sig=sig, h=h, base=base: t.matmul(
                                    ps[3][:, n2 * 512 : n2 * 512 + 512],
                                    vtm[:, sig, h * 128 : h * 128 + 128],
                                    p7sb[pm][:, base + n2 * 512 : base + n2 * 512 + 512],
                                    start=(sig == 0), stop=(sig == 15)),
                                reads=[vtm_b, p7sb_b[pm]], writes=[psb[3]], tag="p7_ctx")
                k.S(lambda s: s.activation(rzf[:], ps[2][:, :], AF.Ln),
                    reads=[psb[2]], writes=[rzf_b], tag="p7_lnz")
                k.S(lambda s: s.activation(rzf[:], rzf[:], AF.Exp, scale=-1.0),
                    reads=[rzf_b], writes=[rzf_b], tag="p7_rz")
                k.V(lambda v, h=h, hc0=hc0, b=b: v.tensor_mul(
                        aob[:, h, hc0 : hc0 + 1024], ps[3][:, :], rzf[:]),
                    reads=[psb[3], rzf_b], writes=[aob_b], tag="p7_cev")
        for e in range(4):
            t0 = e * 512
            for oc in range(nch):
                i = oc % 2
                if oc < MC[e]:
                    op2 = oc % 2
                    k.GQ(lambda g, e=e, oc=oc, op2=op2: g.dma_start(
                            oww[op2][:, 0 : MC[e], :],
                            owT.ap().rearrange("(c p) m -> p c m", p=128)[:, 0 : MC[e],
                                                                         oc * 128 : oc * 128 + 128]),
                         writes=[oww_b[op2]], tag="p7_ldow")
                    po7 = 3 * (oc % 2)
                    for mc in range(MC[e]):
                        k.T(lambda t, e=e, mc=mc, t0=t0, b=b, po7=po7, op2=op2: t.matmul(
                                ps[po7][:, 0:512], oww[op2][:, mc, :], aob[:, mc, t0 : t0 + 512],
                                start=(mc == 0), stop=(mc == MC[e] - 1)),
                            reads=[oww_b[op2], aob_b], writes=[psb[po7]], tag="p7_omm")
                k.SQ(lambda s, b=b, oc=oc, t0=t0, i=i: s.dma_start(
                        xt7[i][:], xn1_d.ap()[b, oc * 128 : oc * 128 + 128, t0 : t0 + 512]),
                     reads=[xn1_d_b], writes=[xt7_b[i]], tag="p7_ldxn2")
                if oc < MC[e]:
                    po7 = 3 * (oc % 2)
                    k.V(lambda v, oc=oc, i=i, po7=po7: v.tensor_scalar(g1t[i][:], ps[po7][:, 0:512],
                                                              ob_sb[:, oc : oc + 1], None, ALU.add),
                        reads=[psb[po7]], writes=[g1t_b[i]], tag="p7_gb1", sreads=[ob_b], ptrw=True)
                    k.V(lambda v, i=i: v.tensor_add(g1t[i][:], g1t[i][:], xt7[i][:]),
                        reads=[xt7_b[i]], writes=[g1t_b[i]], tag="p7_gb2")
                    k.S(lambda s, i=i: s.activation(g1t[i][:], g1t[i][:], AF.Silu),
                        reads=[g1t_b[i]], writes=[g1t_b[i]], tag="p7_silu")
                else:
                    k.S(lambda s, i=i: s.activation(g1t[i][:], xt7[i][:], AF.Silu),
                        reads=[xt7_b[i]], writes=[g1t_b[i]], tag="p7_silu")
                k.SQ(lambda s, b=b, oc=oc, t0=t0, i=i: s.dma_start(
                        g1_d.ap()[b, oc * 128 : oc * 128 + 128, t0 : t0 + 512], g1t[i][:]),
                     reads=[g1t_b[i]], writes=[g1_d_b], tag="p7_stg1")
    k.release_arena()
    k.barrier()


def _build_p8(k, env):
    nc = k.nc
    ar = k.ar
    nch = NCH
    ps, psx, psb = env["ps"], env["psx"], env["psb"]
    psAf = env["psAf"]
    onesb, onesb_b = env["onesb"], env["onesb_b"]
    identb, identb_b = env["identb"], env["identb_b"]
    xn1_d, xn1_d_b = env["xn1_d"], env["xn1_d_b"]
    x1_d, x1_d_b = env["x1_d"], env["x1_d_b"]
    g1_d, g1_d_b = env["g1_d"], env["g1_d_b"]
    yout, yout_b = env["yout"], env["yout_b"]
    tril_in = env["tril_in"]

    wqkv_in = k.din("wqkv", (D, 3 * D))
    bqkvp_in = k.din("bqkvp", (128, 24))
    wout_in = k.din("wout", (D, D))
    bout_in = k.din("bout", (128, nch))

    ar.mark()
    wqkv_sb = ar.alloc([128, nch, 3 * D], BF16, "wqkv_sb")
    wout_sb = ar.alloc([128, nch, D], BF16, "wout_sb")
    bqkvp = ar.alloc([128, 24], F32, "bqkvp")
    bout_sb = ar.alloc([128, nch], F32, "bout_sb")
    tril8 = ar.alloc([128, 128], BF16, "tril8")
    qsb = [ar.alloc([64, T], BF16, f"qsb{j}") for j in range(2)]
    ksb = [ar.alloc([64, T], BF16, f"ksb{j}") for j in range(2)]
    vfb = [ar.alloc([64, T], BF16, f"vfb{j}") for j in range(2)]
    vtm8 = [ar.alloc([128, 16, 64], BF16, f"vtm8{j}") for j in range(2)]
    xtl = [ar.alloc([128, nch, 128], BF16, f"xtl{j}") for j in range(2)]
    sail = [ar.alloc([128, nch, 128], BF16, f"sail{j}") for j in range(2)]
    s1t = [ar.alloc([128, nch, 128], F32, f"s1t{j}") for j in range(2)]
    x1t8 = [ar.alloc([128, nch, 128], F32, f"x1t8{j}") for j in range(2)]
    g1t8 = [ar.alloc([128, nch, 128], F32, f"g1t8{j}") for j in range(2)]
    p8sb = [ar.alloc([128, 2048], BF16, f"p8sb{i}") for i in range(2)]
    rz8 = ar.alloc([64, 1024], F32, "rz8")

    wqkv_b = k.buf("wqkv_sb")
    wout_b = k.buf("wout_sb")
    bqkvp_b = k.buf("bqkvp")
    bout_b = k.buf("bout_sb")
    tril8_b = k.buf("tril8")
    qsb_b = [k.buf(f"qsb{j}") for j in range(2)]
    ksb_b = [k.buf(f"ksb{j}") for j in range(2)]
    vfb_b = [k.buf(f"vfb{j}") for j in range(2)]
    vtm8_b = [k.buf(f"vtm8{j}") for j in range(2)]
    xtl_b = [k.buf(f"xtl{j}") for j in range(2)]
    sail_b = [k.buf(f"sail{j}") for j in range(2)]
    s1t_b = [k.buf(f"s1t{j}") for j in range(2)]
    x1t8_b = [k.buf(f"x1t8{j}") for j in range(2)]
    g1t8_b = [k.buf(f"g1t8{j}") for j in range(2)]
    p8sb_b = [k.buf(f"p8sb{i}") for i in range(2)]
    rz8_b = k.buf("rz8")

    k.GQ(lambda g: g.dma_start(wqkv_sb[:], wqkv_in.ap().rearrange("(c p) m -> p c m", p=128)),
         writes=[wqkv_b], tag="ld8_wqkv")
    k.GQ(lambda g: g.dma_start(wout_sb[:], wout_in.ap().rearrange("(c p) m -> p c m", p=128)),
         writes=[wout_b], tag="ld8_wout")
    k.SQ(lambda s: s.dma_start(bqkvp[:], bqkvp_in.ap()), writes=[bqkvp_b], tag="ld8_bq")
    k.SQ(lambda s: s.dma_start(bout_sb[:], bout_in.ap()), writes=[bout_b], tag="ld8_bo")
    k.GQ(lambda g: g.dma_start(tril8[:], tril_in.ap()), writes=[tril8_b], tag="ld8_tril")

    SCALE = (2.0 * D) ** -0.5

    for b in range(B_PER_CORE):
        for pair in range(8):
            blks = (2 * pair, 2 * pair + 1)
            for j, blk in enumerate(blks):
                k.SQ(lambda s, b=b, blk=blk, j=j: s.dma_start(
                        xtl[j][:], xn1_d.ap()[b].rearrange("(c p) t -> p c t", p=128)
                        [:, :, blk * 128 : blk * 128 + 128]),
                     reads=[xn1_d_b], writes=[xtl_b[j]], tag="p8_ldxn")
            for mp in range(24):
                o = mp // 8
                jg = (mp % 8) * 2
                dsts = (qsb, qsb_b) if o == 0 else (ksb, ksb_b) if o == 1 else (vfb, vfb_b)
                pq8 = mp % 2
                for j in range(2):
                    slot = j
                    for kc in range(nch):
                        k.T(lambda t, mp=mp, j=j, kc=kc, slot=slot, pq8=pq8: t.matmul(
                                ps[pq8][:, slot * 128 : slot * 128 + 128],
                                wqkv_sb[:, kc, mp * 128 : mp * 128 + 128],
                                xtl[j][:, kc, :],
                                start=(kc == 0), stop=(kc == nch - 1)),
                            reads=[wqkv_b, xtl_b[j]], writes=[psb[pq8]], tag="p8_qkv")
                    for half64 in range(2):
                        jj = jg + half64
                        k.V(lambda v, mp=mp, j=j, slot=slot, half64=half64, jj=jj, pq8=pq8,
                                   dstw=dsts[0]: v.tensor_scalar(
                                dstw[j][0:64, jj : T : 16],
                                ps[pq8][half64 * 64 : half64 * 64 + 64,
                                      slot * 128 : slot * 128 + 128],
                                bqkvp[half64 * 64 : half64 * 64 + 64, mp : mp + 1],
                                None, ALU.add),
                            reads=[psb[pq8]], writes=[dsts[1][j]], tag="p8_ev",
                            sreads=[bqkvp_b], ptrw=True)
            for j in range(2):
                for sg in range(16):
                    k.T(lambda t, j=j, sg=sg: t.transpose(
                            psx[1][:, sg * 64 : sg * 64 + 64],
                            vfb[j][0:64, sg * 128 : sg * 128 + 128], identb[0:64, 0:64]),
                        reads=[vfb_b[j], identb_b], writes=[psb[1]], tag="p8_vtr")
                k.V(lambda v, j=j: v.tensor_copy(
                        vtm8[j][:].rearrange("p a b -> p (a b)"), psx[1][:, 0:1024]),
                    reads=[psb[1]], writes=[vtm8_b[j]], tag="p8_vev")
            for j in range(2):
                for half in range(2):
                    h0 = half * 1024
                    nsig = 8 * (half + 1)
                    for m in range(nsig // 2):
                        se, so = 2 * m, 2 * m + 1
                        rel_e = max(se * 128, h0) - h0
                        rel_o = max(so * 128, h0) - h0
                        pm = m % 2
                        for sig, rel in ((se, rel_e), (so, rel_o)):
                            pi = sig % 2
                            segs = ([(rel, 512 - rel), (512, 512)] if rel < 512
                                    else [(rel, 1024 - rel)])
                            for o2, w in segs:
                                k.T(lambda t, j=j, sig=sig, pi=pi, o2=o2, w=w, h0=h0: t.matmul(
                                        ps[pi][:, o2 : o2 + w],
                                        ksb[j][0:64, sig * 128 : sig * 128 + 128],
                                        qsb[j][0:64, h0 + o2 : h0 + o2 + w],
                                        start=True, stop=True),
                                    reads=[ksb_b[j], qsb_b[j]], writes=[psb[pi]], tag="p8_sc")
                        k.S(lambda s, pm=pm, rel_e=rel_e: s.activation(
                                p8sb[pm][:, rel_e : 2048], psAf[:, rel_e : 2048],
                                AF.Exp, scale=SCALE),
                            reads=[psb[0], psb[1]], writes=[p8sb_b[pm]], tag="p8_exp")
                        if rel_o > rel_e:
                            k.V(lambda v, pm=pm, rel_e=rel_e, rel_o=rel_o: v.memset(
                                    p8sb[pm][:, 1024 + rel_e : 1024 + rel_o], 0.0),
                                reads=[], writes=[p8sb_b[pm]], tag="p8_inv")
                        if se * 128 >= h0:
                            k.V(lambda v, pm=pm, rel_e=rel_e: v.tensor_mul(
                                    p8sb[pm][:, rel_e : rel_e + 128],
                                    p8sb[pm][:, rel_e : rel_e + 128], tril8[:]),
                                reads=[tril8_b], writes=[p8sb_b[pm]], tag="p8_mask")
                        if so * 128 >= h0:
                            k.V(lambda v, pm=pm, rel_o=rel_o: v.tensor_mul(
                                    p8sb[pm][:, 1024 + rel_o : 1024 + rel_o + 128],
                                    p8sb[pm][:, 1024 + rel_o : 1024 + rel_o + 128], tril8[:]),
                                reads=[tril8_b], writes=[p8sb_b[pm]], tag="p8_mask")
                        for sig, rel, base in ((se, rel_e, 0), (so, rel_o, 1024)):
                            segs = ([(rel, 512 - rel), (512, 512)] if rel < 512
                                    else [(rel, 1024 - rel)])
                            last = sig == nsig - 1
                            for o2, w in segs:
                                k.T(lambda t, pm=pm, o2=o2, w=w, sig=sig, last=last, base=base: t.matmul(
                                        ps[2][:, o2 : o2 + w], onesb[:],
                                        p8sb[pm][:, base + o2 : base + o2 + w],
                                        start=(sig == 0), stop=last),
                                    reads=[onesb_b, p8sb_b[pm]], writes=[psb[2]], tag="p8_z")
                            for o2, w in segs:
                                k.T(lambda t, j=j, sig=sig, pm=pm, o2=o2, w=w, last=last, base=base: t.matmul(
                                        ps[3][0:64, o2 : o2 + w], vtm8[j][:, sig, :],
                                        p8sb[pm][:, base + o2 : base + o2 + w],
                                        start=(sig == 0), stop=last),
                                    reads=[vtm8_b[j], p8sb_b[pm]], writes=[psb[3]], tag="p8_ctx")
                    k.S(lambda s: s.activation(rz8[:], ps[2][0:64, :], AF.Ln),
                        reads=[psb[2]], writes=[rz8_b], tag="p8_lnz")
                    k.S(lambda s: s.activation(rz8[:], rz8[:], AF.Exp, scale=-1.0),
                        reads=[rz8_b], writes=[rz8_b], tag="p8_rz")
                    for jg in range(16):
                        k.V(lambda v, j=j, jg=jg, half=half: v.tensor_mul(
                                sail[j][(jg % 2) * 64 : (jg % 2) * 64 + 64, jg // 2,
                                        half * 64 : half * 64 + 64],
                                ps[3][0:64, jg : 1024 : 16], rz8[0:64, jg : 1024 : 16]),
                            reads=[psb[3], rz8_b], writes=[sail_b[j]], tag="p8_dei")
            for oc in range(nch):
                pw8 = oc % 2
                for j in range(2):
                    slot = j
                    for kc in range(nch):
                        k.T(lambda t, oc=oc, j=j, kc=kc, slot=slot, pw8=pw8: t.matmul(
                                ps[pw8][:, slot * 128 : slot * 128 + 128],
                                wout_sb[:, kc, oc * 128 : oc * 128 + 128],
                                sail[j][:, kc, :],
                                start=(kc == 0), stop=(kc == nch - 1)),
                            reads=[wout_b, sail_b[j]], writes=[psb[pw8]], tag="p8_wo")
                    k.V(lambda v, oc=oc, j=j, slot=slot, pw8=pw8: v.tensor_scalar(
                            s1t[j][:, oc, :],
                            ps[pw8][:, slot * 128 : slot * 128 + 128],
                            bout_sb[:, oc : oc + 1], None, ALU.add),
                        reads=[psb[pw8]], writes=[s1t_b[j]], tag="p8_woev",
                        sreads=[bout_b], ptrw=True)
            for j, blk in enumerate(blks):
                cols = (blk * 128, blk * 128 + 128)
                k.SQ(lambda s, b=b, cols=cols, j=j: s.dma_start(
                        x1t8[j][:], x1_d.ap()[b].rearrange("(c p) t -> p c t", p=128)
                        [:, :, cols[0] : cols[1]]),
                     reads=[x1_d_b], writes=[x1t8_b[j]], tag="p8_ldx1")
                k.SQ(lambda s, b=b, cols=cols, j=j: s.dma_start(
                        g1t8[j][:], g1_d.ap()[b].rearrange("(c p) t -> p c t", p=128)
                        [:, :, cols[0] : cols[1]]),
                     reads=[g1_d_b], writes=[g1t8_b[j]], tag="p8_ldg1")
                fl = lambda ap: ap.rearrange("p a b -> p (a b)")
                k.V(lambda v, j=j, fl=fl: v.tensor_mul(fl(s1t[j][:]), fl(s1t[j][:]), fl(g1t8[j][:])),
                    reads=[g1t8_b[j]], writes=[s1t_b[j]], tag="p8_sg")
                k.V(lambda v, j=j, fl=fl: v.tensor_add(fl(s1t[j][:]), fl(s1t[j][:]), fl(x1t8[j][:])),
                    reads=[x1t8_b[j]], writes=[s1t_b[j]], tag="p8_x2")
                k.SQ(lambda s, b=b, cols=cols, j=j: s.dma_start(
                        yout.ap()[b].rearrange("(c p) t -> p c t", p=128)[:, :, cols[0] : cols[1]],
                        s1t[j][:]),
                     reads=[s1t_b[j]], writes=[yout_b], tag="p8_sty")
    k.release_arena()



def _to_bf16(a):
    return a.astype(ml_dtypes.bfloat16)


def _pack_chan(v):
    return np.ascontiguousarray(np.asarray(v).reshape(-1, 128).T)


def _host_inputs(inputs, core):
    x = inputs["L0_attn_x"]
    d = {}
    xs = x[2 * core : 2 * core + 2]
    d["xfm"] = np.ascontiguousarray(xs.transpose(0, 2, 1))
    d["ones_bf"] = np.ones((128, 128), dtype=ml_dtypes.bfloat16)
    d["ident_bf"] = np.eye(128, dtype=np.float32).astype(ml_dtypes.bfloat16)
    d["tril_bf"] = np.triu(np.ones((128, 128), dtype=np.float32)).astype(ml_dtypes.bfloat16)
    d["wdq"] = np.ascontiguousarray(inputs["L0_attn_W_dq"])
    d["wuqT"] = np.ascontiguousarray(inputs["L0_attn_W_uq"].T)
    d["wuq"] = np.ascontiguousarray(inputs["L0_attn_W_uq"])
    d["wdkvT"] = np.ascontiguousarray(inputs["L0_attn_W_dkv"].T)
    d["wdqT"] = np.ascontiguousarray(inputs["L0_attn_W_dq"].T)
    d["wuk"] = np.ascontiguousarray(inputs["L0_attn_W_uk"])
    d["wuv"] = np.ascontiguousarray(inputs["L0_attn_W_uv"])
    d["woT"] = np.ascontiguousarray(inputs["L0_attn_W_o"].T)
    d["sigsrawT"] = np.ascontiguousarray(inputs["L0_ffn_signatures_raw"].T)
    d["knots"] = np.ascontiguousarray(inputs["L0_ffn_knot_values"])
    d["temp"] = np.ascontiguousarray(inputs["L0_ffn_temperature"])
    d["cw1"] = np.ascontiguousarray(inputs["L0_ffn_compress_w1"])
    d["cb1"] = _pack_chan(inputs["L0_ffn_compress_b1"])
    d["cw2"] = np.ascontiguousarray(inputs["L0_ffn_compress_w2"])
    d["cb2"] = np.ascontiguousarray(inputs["L0_ffn_compress_b2"])
    t65 = np.concatenate([
        inputs["L0_ffn_spline_w1"].reshape(64, 32),
        inputs["L0_ffn_spline_b1"].reshape(64, 16),
        inputs["L0_ffn_spline_w2"].reshape(64, 16),
        inputs["L0_ffn_spline_b2"].reshape(64, 1)], axis=1)
    d["tab65"] = np.ascontiguousarray(np.concatenate([t65, np.zeros_like(t65)], axis=0))
    d["oscale"] = np.ascontiguousarray(inputs["L0_ffn_output_scale"])
    d["dirs"] = np.ascontiguousarray(np.concatenate(
        [inputs["L0_ffn_directions"], np.zeros_like(inputs["L0_ffn_directions"])], axis=0))
    d["gamma0"] = _pack_chan(inputs["L0_ffn_norm_gamma"])
    d["beta0"] = _pack_chan(inputs["L0_ffn_norm_beta"])
    d["iota8"] = np.tile(np.arange(8, dtype=np.float32), (128, 1))
    d["iota8m"] = np.tile(np.arange(8, dtype=np.float32) - 8.0, (128, 1))
    d["iota64"] = np.tile(np.arange(64, dtype=np.float32), (128, 1))
    d["iotad8"] = np.tile((np.arange(64) // 8).astype(np.float32), (128, 1))
    d["iota64m"] = np.tile(np.arange(64, dtype=np.float32) - 64.0, (128, 1))
    d["qwT"] = np.ascontiguousarray(inputs["L1_ffn_q_w"].T)
    d["kwT"] = np.ascontiguousarray(inputs["L1_ffn_k_w"].T)
    d["vwT"] = np.ascontiguousarray(inputs["L1_ffn_v_w"].T)
    d["owT"] = np.ascontiguousarray(inputs["L1_ffn_o_w"].T)
    d["qb"] = _pack_chan(inputs["L1_ffn_q_b"])
    d["kb"] = _pack_chan(inputs["L1_ffn_k_b"])
    d["ob"] = _pack_chan(inputs["L1_ffn_o_b"])
    vbr = np.zeros((128, 1024), dtype=np.float32)
    vbr[0] = inputs["L1_ffn_v_b"]
    d["vbrow"] = vbr
    d["nw"] = _pack_chan(inputs["L1_ffn_norm_w"])
    d["nb"] = _pack_chan(inputs["L1_ffn_norm_b"])
    perm = np.empty(3072, dtype=np.int64)
    idx = 0
    for o in range(3):
        for jj in range(16):
            base = 192 * jj + 64 * o
            perm[idx : idx + 64] = np.arange(base, base + 64)
            idx += 64
    d["wqkv"] = np.ascontiguousarray(inputs["L1_attn_W_qkv"][:, perm])
    d["bqkvp"] = _pack_chan(inputs["L1_attn_b_qkv"][perm])
    d["wout"] = np.ascontiguousarray(inputs["L1_attn_W_out"])
    d["bout"] = _pack_chan(inputs["L1_attn_b_out"])
    return d


def run_dev(inputs, dumps=(), upto=99, cores=(0,)):
    nc = bass.Bass("TRN2", target_bir_lowering=False)
    build(nc, dumps=dumps, upto=upto)
    in_maps = [_host_inputs(inputs, c) for c in cores]
    res = run_bass_kernel_spmd(nc, in_maps, core_ids=list(range(len(cores))))
    return res.results


def kernel(**inputs):
    nc = bass.Bass("TRN2", target_bir_lowering=False)
    build(nc)
    in_maps = [_host_inputs(inputs, c) for c in range(NCORES)]
    res = run_bass_kernel_spmd(nc, in_maps, core_ids=list(range(NCORES)))
    out = np.empty((16, T, D), dtype=np.float32)
    for c in range(NCORES):
        y = res.results[c]["yout"]
        out[2 * c : 2 * c + 2] = y.transpose(0, 2, 1)
    return out
